# version 50
# speedup vs baseline: 1.1313x; 1.0100x over previous
import contextlib
import numpy as np
import ml_dtypes

import concourse.bass as bass
import concourse.mybir as mybir
from concourse.bass_utils import run_bass_kernel_spmd
from concourse.bass_core_rust import IndirectOffsetOnAxis

F32 = mybir.dt.float32
BF16 = mybir.dt.bfloat16
I32 = mybir.dt.int32
AF = mybir.ActivationFunctionType
ALU = mybir.AluOpType
AX = mybir.AxisListType
BFNP = ml_dtypes.bfloat16

TOK = 2048
NT = TOK // 128
D = 1024
NCHUNK = D // 128
EPS = 1e-5
SEG = 256
NSEG = TOK // SEG

ENGINES = ("tensor", "vector", "scalar", "gpsimd", "sync")
ENG_SEM = {
    "tensor": ("PEs", 1),
    "vector": ("DVEs", 1),
    "scalar": ("ACTs", 1),
    "gpsimd": ("POOLs", 1),
}
DMA_SEM = {"sync": ("SQ", 16), "gpsimd": ("GQ", 16), "scalar": ("AQ", 16)}
ALL_SEMS = ["PEs", "DVEs", "ACTs", "POOLs", "SQ", "GQ", "AQ", "CC"]


class Buf:

    __slots__ = ("w", "r", "name", "qsem", "qcnt", "qkind")

    def __init__(self, name=""):
        self.w = None
        self.r = []
        self.name = name
        self.qsem = None
        self.qcnt = 0
        self.qkind = None


class Prog:

    def __init__(self, nc):
        self.nc = nc
        self.ops = {e: [] for e in ENGINES}
        self.all_sems = list(ALL_SEMS)
        self.cnt = {s: 0 for s in self.all_sems}
        self.wm = {e: {} for e in ENGINES}
        self.sem_handles = {}
        self._nqsem = 0

    def new_sem(self, prefix="q"):
        name = "%s%d" % (prefix, self._nqsem)
        self._nqsem += 1
        self.all_sems.append(name)
        self.cnt[name] = 0
        return name

    def _waits_from(self, eng, bufs_r, bufs_w, extra_waits):
        need = {}
        for b in bufs_r:
            if b.w is not None:
                s, v = b.w
                need[s] = max(need.get(s, 0), v)
        for b in bufs_w:
            toks = list(b.r)
            if b.w is not None:
                toks.append(b.w)
            for s, v in toks:
                need[s] = max(need.get(s, 0), v)
        for w in extra_waits or ():
            if w is None:
                continue
            s, v = w
            need[s] = max(need.get(s, 0), v)
        wm = self.wm[eng]
        waits = []
        for s, v in need.items():
            if wm.get(s, 0) < v:
                waits.append((s, v))
                wm[s] = v
        return waits

    def emit(self, eng, build, reads=(), writes=(), waits=None, sem=None, tag=""):
        if sem is None:
            sem, amt = ENG_SEM[eng]
        else:
            sem, amt = sem
        ws = self._waits_from(eng, reads, writes, waits)
        self.cnt[sem] += amt
        tok = (sem, self.cnt[sem])
        self.ops[eng].append((ws, build, sem, amt, tag))
        for b in reads:
            b.r.append(tok)
        for b in writes:
            b.w = tok
            b.r = []
        return tok

    def emit_group(self, eng, builds, reads=(), writes=(), waits=None, tag=""):
        sem, amt = ENG_SEM[eng]
        ws = self._waits_from(eng, reads, writes, waits)
        for b in builds[:-1]:
            self.ops[eng].append((ws, b, None, 0, tag))
            ws = []
        self.cnt[sem] += amt
        tok = (sem, self.cnt[sem])
        self.ops[eng].append((ws, builds[-1], sem, amt, tag))
        for b in reads:
            b.r.append(tok)
        for b in writes:
            b.w = tok
            b.r = []
        return tok

    def dma(self, queue, out, in_, reads=(), writes=(), waits=None, tag="", **kw):
        kind = "SW" if queue == "gpsimd" else "HW"
        assert len(writes) <= 1
        b = writes[0] if writes else reads[0]
        if b.qsem is None or b.qkind != kind:
            b.qsem = self.new_sem("q")
            b.qkind = kind
            b.qcnt = 0
        sem_name = b.qsem

        def build(h, out=out, in_=in_, kw=kw):
            return h.dma_start(out=out, in_=in_, **kw)

        need = {}
        for b2 in reads:
            if b2.w is not None:
                s, v = b2.w
                need[s] = max(need.get(s, 0), v)
        if writes:
            for s, v in writes[0].r:
                need[s] = max(need.get(s, 0), v)
        for w in waits or ():
            if w is None:
                continue
            s, v = w
            need[s] = max(need.get(s, 0), v)
        wm = self.wm[queue]
        ws = []
        for s, v in need.items():
            if wm.get(s, 0) < v:
                ws.append((s, v))
                wm[s] = v
        b.qcnt += 1
        self.cnt[sem_name] = max(self.cnt[sem_name], 16 * b.qcnt)
        tok = (sem_name, 16 * b.qcnt)
        self.ops[queue].append((ws, build, sem_name, 16, tag))
        for b2 in reads:
            b2.r.append(tok)
        if writes:
            b.w = tok
            b.r = []
        return tok

    def dma_custom(self, queue, build, reads=(), writes=(), waits=None, tag=""):
        kind = "SW" if queue == "gpsimd" else "HW"
        assert len(writes) == 1
        b = writes[0]
        if b.qsem is None or b.qkind != kind:
            b.qsem = self.new_sem("q")
            b.qkind = kind
            b.qcnt = 0
        need = {}
        for b2 in reads:
            if b2.w is not None:
                s, v = b2.w
                need[s] = max(need.get(s, 0), v)
        for s, v in b.r:
            need[s] = max(need.get(s, 0), v)
        for w in waits or ():
            if w is None:
                continue
            s, v = w
            need[s] = max(need.get(s, 0), v)
        wm = self.wm[queue]
        ws = []
        for s, v in need.items():
            if wm.get(s, 0) < v:
                ws.append((s, v))
                wm[s] = v
        b.qcnt += 1
        self.cnt[b.qsem] = 16 * b.qcnt
        tok = (b.qsem, 16 * b.qcnt)
        self.ops[queue].append((ws, build, b.qsem, 16, tag))
        for b2 in reads:
            b2.r.append(tok)
        b.w = tok
        b.r = []
        return tok

    def collective(self, kind, op, groups, ins, outs, reads=(), writes=(),
                   sem_name=None, tag="cc"):
        assert sem_name is not None
        if sem_name not in self.cnt:
            self.all_sems.append(sem_name)
            self.cnt[sem_name] = 0

        def build(h, kind=kind, op=op, groups=groups, ins=ins, outs=outs):
            return h.collective_compute(kind, op, replica_groups=groups,
                                        ins=ins, outs=outs)

        ws = self._waits_from("gpsimd", reads, writes, None)
        self.cnt[sem_name] = 1
        tok = (sem_name, 1)
        self.ops["gpsimd"].append((ws, build, sem_name, 1, tag))
        for b in reads:
            b.r.append(tok)
        for b in writes:
            b.w = tok
            b.r = []
        return tok

    def barrier(self):
        toks = [(s, self.cnt[s]) for s in self.all_sems if self.cnt[s] > 0]
        for e in ENGINES:
            wm = self.wm[e]
            ws = [(s, v) for s, v in toks if wm.get(s, 0) < v]
            for s, v in ws:
                wm[s] = v
            if ws:
                self.ops[e].append((ws, None, None, 0, "barrier"))

    def realize(self, es):
        for s in self.all_sems:
            self.sem_handles[s] = es.enter_context(self.nc.semaphore(s))
        block = es.enter_context(self.nc.Block())
        prog = self

        def runner(eng_name):
            def _run(h):
                for ws, build, sem, amt, tag in prog.ops[eng_name]:
                    for s, v in ws:
                        h.wait_ge(prog.sem_handles[s], v)
                    if build is not None:
                        inst = build(h)
                        if sem is not None and amt:
                            inst.then_inc(prog.sem_handles[sem], amt)
            return _run

        for eng_name in ENGINES:
            if prog.ops[eng_name]:
                getattr(block, eng_name)(runner(eng_name))




def ts(i, n):
    return slice(i * n, (i + 1) * n)


class Ctx:

    def __init__(self, nc, prog, es):
        self.nc = nc
        self.p = prog
        self.es = es
        self.es_root = es
        self.sb = {}
        self._cc = {}
        self._cct = es.enter_context(nc.sbuf_tensor("cconst", [128, 32], F32))
        self._ccb = Buf("cconst")
        self.sqtr_s = es.enter_context(nc.sbuf_tensor("sqtr", [128, 1024], BF16))
        self.sqtrb = Buf("sqtr")
        self.store_toks = []
        self.on_store = None

    def cconst(self, value):
        value = float(value)
        if value not in self._cc:
            col = len(self._cc)
            assert col < 32
            self._cc[value] = col
            self.p.emit("vector",
                        (lambda h, col=col, value=value: h.memset(
                            self._cct[:, col:col + 1], value)),
                        writes=[self._ccb], tag="cconst")
        col = self._cc[value]
        return self._cct[:, col:col + 1]

    def sbuf(self, name, shape, dt):
        t = self.es.enter_context(self.nc.sbuf_tensor(name, shape, dt))
        b = Buf(name)
        self.sb[name] = (t, b)
        return t, b

    def psum(self, name, shape, dt):
        t = self.es.enter_context(self.nc.psum_tensor(name, shape, dt))
        b = Buf(name)
        return t, b

    @contextlib.contextmanager
    def scope(self):
        es2 = contextlib.ExitStack()
        old = self.es
        self.es = es2
        try:
            with es2:
                yield
                self.p.barrier()
        finally:
            self.es = old


def ln_stats_acc(ctx, src_ap, src_buf, stat, statb, n_feat=D, pref="",
                 rows=128):
    p = ctx.p
    tr_ap = ctx.trash_s[0:rows, 0:n_feat]
    p.emit("scalar", lambda h: h.activation(tr_ap, src_ap, AF.Copy,
                                            accum_out=stat[:, 0:1]),
           reads=[src_buf], writes=[statb, ctx.trashb], tag=pref + "sum")
    p.emit("scalar", lambda h: h.activation(tr_ap, src_ap, AF.Square,
                                            accum_out=stat[:, 1:2]),
           reads=[src_buf], writes=[statb, ctx.trashb], tag=pref + "sumsq")


def ln_stats_acc_v(ctx, src_ap, src_buf, stat, statb, n_feat=D, pref="",
                   rows=128):
    p = ctx.p
    tr_ap = ctx.trash_s[0:rows, 0:n_feat]
    sq_ap = ctx.sqtr_s[0:rows, 0:n_feat]
    p.emit("vector", lambda h: h.tensor_scalar(
        tr_ap, src_ap, 1.0, 0.0, ALU.mult, ALU.add, accum_out=stat[:, 0:1]),
        reads=[src_buf], writes=[statb, ctx.trashb], tag=pref + "sumv")
    p.emit("vector", lambda h: h.scalar_tensor_tensor(
        sq_ap, src_ap, 1.0, src_ap, ALU.mult, ALU.mult,
        accum_out=stat[:, 1:2]),
        reads=[src_buf], writes=[statb, ctx.sqtrb], tag=pref + "sumsqv")


def grid_acc_v(ctx, src_ap, src_buf, t, tag):
    ng = getattr(ctx, "next_grid", None)
    if ng is None:
        return
    g_s, g_b = ng
    p = ctx.p
    p.emit("vector", (lambda h, src_ap=src_ap, t=t: h.tensor_scalar(
        ctx.trash_s[:, 0:D], src_ap, 1.0, 0.0, ALU.mult, ALU.add,
        accum_out=g_s[:, t:t + 1])),
        reads=[src_buf], writes=[g_b, ctx.trashb], tag=tag + " gsumv")
    p.emit("vector", (lambda h, src_ap=src_ap, t=t: h.scalar_tensor_tensor(
        ctx.sqtr_s[:, 0:D], src_ap, 1.0, src_ap, ALU.mult, ALU.mult,
        accum_out=g_s[:, 32 + t:33 + t])),
        reads=[src_buf], writes=[g_b, ctx.sqtrb], tag=tag + " gsqv")


def ln_stats_fin(ctx, stat, statb, n_feat=D, eps=EPS, eps_scale=1.0,
                 pref="", rows=128):
    p = ctx.p
    inv = 1.0 / n_feat
    p.emit("vector", lambda h: h.tensor_scalar(stat[:, 2:4], stat[:, 0:2], inv,
                                               None, ALU.mult),
           reads=[statb], writes=[statb], tag=pref + "mE2")
    p.emit("vector", lambda h: h.scalar_tensor_tensor(
        stat[:, 0:1], stat[:, 2:3], stat[:, 2:3], stat[:, 3:4],
        ALU.mult, ALU.subtract),
        reads=[statb], writes=[statb], tag=pref + "var")
    eps_ap = ctx.cconst(eps * eps_scale)[0:rows, :]
    neg1 = ctx.cconst(-1.0)[0:rows, :]
    p.emit("scalar", lambda h: h.activation(stat[:, 1:2], stat[:, 0:1], AF.Sqrt,
                                            bias=eps_ap, scale=neg1),
           reads=[statb, ctx._ccb], writes=[statb], tag=pref + "sd")
    p.emit("vector", lambda h: h.reciprocal(stat[:, 4:5], stat[:, 1:2]),
           reads=[statb], writes=[statb], tag=pref + "r")
    p.emit("vector", lambda h: h.tensor_scalar(stat[:, 5:6], stat[:, 2:3],
                                               stat[:, 4:5], -1.0, ALU.mult,
                                               ALU.mult),
           reads=[statb], writes=[statb], tag=pref + "b")


def ln_stats(ctx, src_ap, src_buf, stat, statb, n_feat=D, eps=EPS,
             eps_scale=1.0, pref="", rows=128):
    ln_stats_acc(ctx, src_ap, src_buf, stat, statb, n_feat=n_feat, pref=pref,
                 rows=rows)
    ln_stats_fin(ctx, stat, statb, n_feat=n_feat, eps=eps,
                 eps_scale=eps_scale, pref=pref, rows=rows)


def ln_prepass_tile(ctx, i, s, w, grid, gridb, grid2b, xs, xb, pref=""):
    p = ctx.p
    par = i % len(xs)
    p.dma("sync", xs[par], s, writes=[xb[par]], waits=w, tag=pref + "px")
    p.emit("vector", (lambda h, i=i, par=par: h.tensor_scalar(
        ctx.trash_s[:, 0:D], xs[par], 1.0, 0.0, ALU.mult, ALU.add,
        accum_out=grid[:, i:i + 1])),
        reads=[xb[par]], writes=[gridb, ctx.trashb], tag=pref + "psum")
    p.emit("scalar", (lambda h, i=i, par=par: h.activation(
        ctx.sqtr_s[:, 0:D], xs[par], AF.Square,
        accum_out=grid[:, 32 + i:33 + i])),
        reads=[xb[par]], writes=[grid2b, ctx.sqtrb], tag=pref + "psq")


def ln_prepass_finish(ctx, n, grid, gridb, grid2b, eps=EPS, eps_scale=1.0,
                      pref=""):
    p = ctx.p
    inv = 1.0 / D
    p.emit("vector", (lambda h, n=n: h.tensor_scalar(
        grid[:, 0:n], grid[:, 0:n], inv, None, ALU.mult)),
        reads=[gridb, grid2b], writes=[gridb], tag=pref + "pmE")
    p.emit("vector", (lambda h, n=n: h.tensor_scalar(
        grid[:, 32:32 + n], grid[:, 32:32 + n], inv, None, ALU.mult)),
        reads=[gridb], writes=[gridb], tag=pref + "pmE2")
    p.emit("vector", (lambda h, n=n: h.tensor_tensor(
        grid[:, 64:64 + n], grid[:, 0:n], grid[:, 0:n], op=ALU.mult)),
        reads=[gridb], writes=[gridb], tag=pref + "pmm")
    p.emit("vector", (lambda h, n=n: h.tensor_tensor(
        grid[:, 96:96 + n], grid[:, 64:64 + n], grid[:, 32:32 + n],
        op=ALU.subtract)),
        reads=[gridb], writes=[gridb], tag=pref + "pnv")
    eps_ap = ctx.cconst(eps * eps_scale)
    neg1 = ctx.cconst(-1.0)
    p.emit("scalar", (lambda h, n=n: h.activation(
        grid[:, 64:64 + n], grid[:, 96:96 + n], AF.Sqrt, bias=eps_ap,
        scale=neg1)),
        reads=[gridb, ctx._ccb], writes=[gridb], tag=pref + "psd")
    p.emit("vector", (lambda h, n=n: h.reciprocal(
        grid[:, 128:128 + n], grid[:, 64:64 + n])),
        reads=[gridb], writes=[gridb], tag=pref + "pr")
    p.emit("vector", (lambda h, n=n: h.scalar_tensor_tensor(
        grid[:, 160:160 + n], grid[:, 0:n], neg1, grid[:, 128:128 + n],
        ALU.mult, ALU.mult)),
        reads=[gridb, ctx._ccb], writes=[gridb], tag=pref + "pb")


def grid_acc(ctx, src_ap, src_buf, t, tag):
    ng = getattr(ctx, "next_grid", None)
    if ng is None:
        return
    g_s, g_b = ng
    p = ctx.p
    p.emit("scalar", (lambda h, src_ap=src_ap, t=t: h.activation(
        ctx.sqtr_s[:, 0:D], src_ap, AF.Copy, accum_out=g_s[:, t:t + 1])),
        reads=[src_buf], writes=[g_b, ctx.sqtrb], tag=tag + " gsum")
    p.emit("scalar", (lambda h, src_ap=src_ap, t=t: h.activation(
        ctx.sqtr_s[:, 0:D], src_ap, AF.Square,
        accum_out=g_s[:, 32 + t:33 + t])),
        reads=[src_buf], writes=[g_b, ctx.sqtrb], tag=tag + " gsq")


def ln_prepass(ctx, srcs, grid, gridb, xs, xb, eps=EPS, eps_scale=1.0, pref=""):
    grid2b = Buf(pref + "g2")
    for i, (s, w) in enumerate(srcs):
        ln_prepass_tile(ctx, i, s, w, grid, gridb, grid2b, xs, xb, pref=pref)
    ln_prepass_finish(ctx, len(srcs), grid, gridb, grid2b, eps, eps_scale, pref)




def build_l0attn(ctx, x_in, x_out, wq, wk, wv, wc, biases, summ_in, consts,
                 grid_ext=None):
    nc, p, es = ctx.nc, ctx.p, ctx.es
    ctx.store_toks = []

    wq_s, wq_b = ctx.sbuf("l0_wq", [128, NCHUNK * D], BF16)
    wk_s, wk_b = ctx.sbuf("l0_wk", [128, NCHUNK * D], BF16)
    wv_s, wv_b = ctx.sbuf("l0_wv", [128, NCHUNK * D], BF16)
    wc_s, wc_b = ctx.sbuf("l0_wc", [128, NCHUNK * D], BF16)
    bias_s, bias_b = ctx.sbuf("l0_bias", [4, D], F32)
    biasbf_s, biasbf_b = ctx.sbuf("l0_biasbf", [1, 4 * D], BF16)
    biasT_s, biasT_b = ctx.sbuf("l0_biasT", [128, 4 * NCHUNK], F32)
    summ_s, summ_b = ctx.sbuf("l0_summ", [16, D], F32)
    sstat_s, sstat_b = ctx.sbuf("l0_sstat", [16, 8], F32)
    sxn_s, sxn_b = ctx.sbuf("l0_sxn", [16, D], BF16)
    sxnT_s, sxnT_b = ctx.sbuf("l0_sxnT", [128, NCHUNK * 16], BF16)
    qgT_s, qgT_b = ctx.sbuf("l0_qgT", [128, NCHUNK * 16], BF16)
    kgT_s, kgT_b = ctx.sbuf("l0_kgT", [128, NCHUNK * 16], BF16)
    vg_s, vg_b = ctx.sbuf("l0_vg", [16, D + 1], BF16)
    pg_s, pg_b = ctx.sbuf("l0_pg", [16, 16], BF16)
    pgT_s, pgT_b = ctx.sbuf("l0_pgT", [16, 16], BF16)
    gstat_s, gstat_b = ctx.sbuf("l0_gstat", [16, 2], F32)
    goutT_s, goutT_b = ctx.sbuf("l0_goutT", [128, NCHUNK * 16], F32)

    x_s, x_b = zip(*[ctx.sbuf("l0_x%d" % i, [128, D], F32) for i in range(2)])
    stat_s, stat_b = zip(*[ctx.sbuf("l0_st%d" % i, [128, 16], F32) for i in range(2)])
    xn_s, xn_b = zip(*[ctx.sbuf("l0_xn%d" % i, [128, D], BF16) for i in range(2)])
    xnT_s, xnT_b = zip(*[ctx.sbuf("l0_xnT%d" % i, [128, NCHUNK * 2 * SEG], BF16) for i in range(2)])
    qT_s, qT_b = zip(*[ctx.sbuf("l0_qT%d" % i, [128, NCHUNK * 2 * SEG], BF16) for i in range(2)])
    kT_s, kT_b = zip(*[ctx.sbuf("l0_kT%d" % i, [128, NCHUNK * 2 * SEG], BF16) for i in range(2)])
    v_s, v_b = zip(*[ctx.sbuf("l0_v%d" % i, [128, 2 * (D + 1)], BF16) for i in range(2)])
    pT_s, pT_b = zip(*[ctx.sbuf("l0_pT%d" % i, [128, 2 * SEG], BF16) for i in range(2)])
    fused_s, fused_b = zip(*[ctx.sbuf("l0_fsd%d" % i, [128, 2 * D], BF16) for i in range(2)])
    fusedT_s, fusedT_b = zip(*[ctx.sbuf("l0_fsdT%d" % i, [128, NCHUNK * SEG], BF16) for i in range(2)])
    rsum_s, rsum_b = ctx.sbuf("l0_rsum", [128, 4], F32)
    xr_s, xr_b = zip(*[ctx.sbuf("l0_xr%d" % i, [128, D], F32) for i in range(2)])
    lny_s, lny_b = zip(*[ctx.sbuf("l0_lny%d" % i, [128, D], BF16) for i in range(2)])
    xo_s, xo_b = zip(*[ctx.sbuf("l0_xo%d" % i, [128, 2 * D], F32) for i in range(2)])

    ident = ctx.ident_s
    identb = ctx.ident_b
    ones1_s = ctx.ones1_s
    ones1_b = ctx.ones1_b

    P = [ctx.psum("l0_P%d" % i, [128, 512], F32) for i in range(6)]
    PT, PTb = ctx.psum("l0_PT", [128, 1024], BF16)
    PT2, PT2b = ctx.psum("l0_PT2", [128, 1024], BF16)

    for w_ap, (s, b) in ((wq, (wq_s, wq_b)), (wk, (wk_s, wk_b)),
                         (wv, (wv_s, wv_b)), (wc, (wc_s, wc_b))):
        for c in range(NCHUNK):
            p.dma("gpsimd", s[:, ts(c, D)], w_ap[ts(c, 128), :], writes=[b],
                  tag="l0 wload")
    p.dma("sync", bias_s[:, :], biases, writes=[bias_b], tag="l0 bias")
    p.dma("gpsimd", biasbf_s[:, :],
          biases.rearrange("r d -> (r d)").rearrange("(a n) -> a n", a=1),
          writes=[biasbf_b], tag="l0 biasbf")
    p.dma("sync", summ_s[:, :], summ_in, writes=[summ_b], tag="l0 summ")

    pS, pSb = P[4]
    for c in range(NCHUNK):
        p.emit("tensor", (lambda h, c=c: h.transpose(
            pS[:, 0:4], bias_s[0:4, ts(c, 128)], ctx.identf_s[0:4, 0:4])),
            reads=[bias_b, ctx.identf_b], writes=[pSb], tag="l0 biasT")
        p.emit("vector", (lambda h, c=c: h.tensor_copy(
            biasT_s[:, ts(c, 4)], pS[:, 0:4])),
            reads=[pSb], writes=[biasT_b], tag="l0 biasT evac")

    def biasT(row, c):
        return biasT_s[:, c * 4 + row:c * 4 + row + 1]

    ln_stats(ctx, summ_s[0:16, :], summ_b, sstat_s[0:16, :], sstat_b, pref="l0s ",
             rows=16)
    p.emit("vector", lambda h: h.tensor_scalar(
        sxn_s[0:16, :], summ_s[0:16, :], sstat_s[0:16, 4:5], sstat_s[0:16, 5:6],
        ALU.mult, ALU.add), reads=[summ_b, sstat_b], writes=[sxn_b], tag="l0s xn")
    for c in range(NCHUNK):
        p.emit("tensor", (lambda h, c=c: h.transpose(
            PT[:, 0:16], sxn_s[0:16, ts(c, 128)], ident[0:16, 0:16])),
            reads=[sxn_b, identb], writes=[PTb], tag="l0s xnT")
        p.emit("vector", (lambda h, c=c: h.tensor_copy(
            sxnT_s[:, ts(c, 16)], PT[:, 0:16])),
            reads=[PTb], writes=[sxnT_b], tag="l0s xnT evac")
    for (wsrc, wsb, dsts, dstb, brow) in ((wq_s, wq_b, qgT_s, qgT_b, 0),
                                          (wk_s, wk_b, kgT_s, kgT_b, 1)):
        for e in range(NCHUNK):
            pA, pAb = P[e % 2]
            p.emit_group("tensor", [
                (lambda h, e=e, c=c, wsrc=wsrc, pA=pA: h.matmul(
                    pA[:, 0:16], wsrc[:, c * D + e * 128:c * D + (e + 1) * 128],
                    sxnT_s[:, ts(c, 16)], start=(c == 0), stop=(c == NCHUNK - 1)))
                for c in range(NCHUNK)],
                reads=[wsb, sxnT_b], writes=[pAb], tag="l0s qg mm")
            p.emit("vector", (lambda h, e=e, dsts=dsts, brow=brow, pA=pA:
                              h.tensor_scalar(
                dsts[:, ts(e, 16)], pA[:, 0:16], biasT(brow, e), None, ALU.add)),
                reads=[pAb, biasT_b], writes=[dstb], tag="l0s qg evac")
    p.emit("vector", lambda h: h.memset(vg_s[0:16, D:D + 1], 1.0),
           writes=[vg_b], tag="l0s vg ones")
    for n in range(2):
        pA, pAb = P[n % 2]
        builds = [
            (lambda h, n=n, c=c, pA=pA: h.matmul(
                pA[0:16, 0:512], sxnT_s[:, ts(c, 16)],
                wv_s[:, c * D + n * 512:c * D + (n + 1) * 512],
                start=(c == 0), stop=False))
            for c in range(NCHUNK)]
        builds.append(lambda h, n=n, pA=pA: h.matmul(
            pA[0:16, 0:512], ones1_s[0:1, 0:16],
            biasbf_s[0:1, 2 * D + n * 512:2 * D + (n + 1) * 512],
            start=False, stop=True))
        p.emit_group("tensor", builds, reads=[sxnT_b, wv_b, ones1_b, biasbf_b],
                     writes=[pAb], tag="l0s vg mm")
        p.emit("vector", (lambda h, n=n, pA=pA: h.tensor_copy(
            vg_s[0:16, ts(n, 512)], pA[0:16, 0:512])),
            reads=[pAb], writes=[vg_b], tag="l0s vg evac")
    pC, pCb = P[2]
    p.emit_group("tensor", [
        (lambda h, c=c: h.matmul(pC[0:16, 0:16], qgT_s[:, ts(c, 16)],
                                 kgT_s[:, ts(c, 16)],
                                 start=(c == 0), stop=(c == NCHUNK - 1)))
        for c in range(NCHUNK)],
        reads=[qgT_b, kgT_b], writes=[pCb], tag="l0s sg mm")
    eighth = ctx.cconst(0.125)
    p.emit("scalar", lambda h: h.activation(
        pg_s[0:16, :], pC[0:16, 0:16], AF.Exp, scale=eighth[0:16, :],
        accum_out=gstat_s[0:16, 0:1]),
        reads=[pCb, ctx._ccb], writes=[pg_b, gstat_b], tag="l0s exp")
    p.emit("vector", lambda h: h.reciprocal(gstat_s[0:16, 1:2], gstat_s[0:16, 0:1]),
           reads=[gstat_b], writes=[gstat_b], tag="l0s recip")
    p.emit("vector", lambda h: h.tensor_scalar(
        pg_s[0:16, :], pg_s[0:16, :], gstat_s[0:16, 1:2], None, ALU.mult),
        reads=[gstat_b, pg_b], writes=[pg_b], tag="l0s pnorm")
    p.emit("tensor", lambda h: h.transpose(PT[0:16, 0:16], pg_s[0:16, 0:16],
                                           ident[0:16, 0:16]),
           reads=[pg_b, identb], writes=[PTb], tag="l0s pgT")
    p.emit("vector", lambda h: h.tensor_copy(pgT_s[0:16, :], PT[0:16, 0:16]),
           reads=[PTb], writes=[pgT_b], tag="l0s pgT evac")
    for c in range(NCHUNK):
        pA, pAb = P[c % 2]
        p.emit("tensor", (lambda h, c=c, pA=pA: h.matmul(
            pA[:, 0:16], vg_s[0:16, ts(c, 128)], pgT_s[0:16, 0:16],
            start=True, stop=True)),
            reads=[vg_b, pgT_b], writes=[pAb], tag="l0s gout mm")
        p.emit("vector", (lambda h, c=c, pA=pA: h.tensor_copy(
            goutT_s[:, ts(c, 16)], pA[:, 0:16])),
            reads=[pAb], writes=[goutT_b], tag="l0s gout evac")

    if grid_ext is None:
        if getattr(ctx, "grid_scratch", None) is not None:
            grid_s, grid_b = ctx.grid_scratch
            ctx.grid_scratch = None
        else:
            grid_s, grid_b = ctx.sbuf("l0_grid", [128, 192], F32)
        ln_prepass(ctx, [(x_in[ts(t, 128), :], None) for t in range(NT)],
                   grid_s, grid_b, [x_s[0][:, :], x_s[1][:, :]],
                   [x_b[0], x_b[1]], pref="l0 ")
        ctx.l0_ready_tok = ("DVEs", p.cnt["DVEs"])
    else:
        grid_s, grid_b = grid_ext

    pending = []

    def l0_tail(t, par, th):
        def run(t=t, par=par, th=th):
            st = stat_s[par][:, 8 * th:8 * th + 8]
            ln_stats_fin(ctx, st, stat_b[par], pref="l0y ")
            p.emit("vector", (lambda h, th=th, par=par, st=st: h.tensor_scalar(
                lny_s[par][:, :], xo_s[par][:, ts(th, D)],
                st[:, 4:5], st[:, 5:6], ALU.mult, ALU.add)),
                reads=[xo_b[par], stat_b[par]], writes=[lny_b[par]],
                tag="l0 lny")
            p.dma("sync", xr_s[par][:, :], x_in[ts(t, 128), :],
                  writes=[xr_b[par]], tag="l0 xre")
            p.emit("vector", (lambda h, th=th, par=par: h.scalar_tensor_tensor(
                xo_s[par][:, ts(th, D)], lny_s[par][:, :], 1.0,
                xr_s[par][:, :], ALU.mult, ALU.add)),
                reads=[lny_b[par], xr_b[par]], writes=[xo_b[par]],
                tag="l0 resid")
            ctx.store_toks.append(p.dma(
                "scalar", x_out[ts(t, 128), :], xo_s[par][:, ts(th, D)],
                reads=[xo_b[par]], tag="l0 xstore"))
            if ctx.on_store:
                ctx.on_store(len(ctx.store_toks) - 1, ctx.store_toks[-1])

        def grun(t=t, par=par, th=th):
            grid_acc(ctx, xo_s[par][:, ts(th, D)], xo_b[par], t, "l0")
        return run, grun

    for sp in range(NSEG // 2):
        ppar = sp % 2
        xnTp_s, xnTp_b = xnT_s[ppar], xnT_b[ppar]
        qTp_s, qTp_b = qT_s[ppar], qT_b[ppar]
        kTp_s, kTp_b = kT_s[ppar], kT_b[ppar]
        for q4 in range(4):
            t = 4 * sp + q4
            par = t % 2
            pt, ptb = PT, PTb
            p.dma("sync", x_s[par][:, :], x_in[ts(t, 128), :],
                  writes=[x_b[par]], tag="l0 xload")
            p.emit("vector", (lambda h, par=par, t=t: h.tensor_scalar(
                xn_s[par][:, :], x_s[par][:, :],
                grid_s[:, 128 + t:129 + t], grid_s[:, 160 + t:161 + t],
                ALU.mult, ALU.add)),
                reads=[x_b[par], grid_b], writes=[xn_b[par]], tag="l0 xn")
            p.emit_group("tensor", [
                (lambda h, c=c, par=par, pt=pt: h.transpose(
                    pt[:, ts(c, 128)],
                    xn_s[par][:, ts(c, 128)],
                    ident[:, :]))
                for c in range(NCHUNK)],
                reads=[xn_b[par], identb], writes=[ptb], tag="l0 xnT")
            p.emit("vector", (lambda h, q4=q4, par=par, pt=pt, xnTp_s=xnTp_s:
                              h.tensor_copy(
                xnTp_s.rearrange("p (c q) -> p c q", c=NCHUNK)[:, :, ts(q4, 128)],
                pt.rearrange("p (c q) -> p c q", c=NCHUNK)[:, :, :])),
                reads=[ptb], writes=[xnTp_b], tag="l0 xnT evac")
        for (wsrc, wsb, dsts, dstb, brow) in (
                (wq_s, wq_b, qTp_s, qTp_b, 0),
                (wk_s, wk_b, kTp_s, kTp_b, 1)):
            for e in range(NCHUNK):
                pp, ppb = P[e % 2]
                p.emit_group("tensor", [
                    (lambda h, e=e, c=c, wsrc=wsrc, pp=pp, xnTp_s=xnTp_s:
                     h.matmul(
                        pp[:, 0:512],
                        wsrc[:, c * D + e * 128:c * D + (e + 1) * 128],
                        xnTp_s[:, c * 2 * SEG:c * 2 * SEG + 512],
                        start=(c == 0), stop=(c == NCHUNK - 1)))
                    for c in range(NCHUNK)],
                    reads=[wsb, xnTp_b], writes=[ppb], tag="l0 qk mm")
                p.emit("vector", (lambda h, e=e, dsts=dsts, brow=brow, pp=pp:
                                  h.tensor_scalar(dsts[:, e * 512:(e + 1) * 512],
                                                  pp[:, 0:512], biasT(brow, e),
                                                  None, ALU.add)),
                       reads=[ppb, biasT_b], writes=[dstb], tag="l0 qk evac")
        for fn, _ in pending:
            fn()
        for _, g in pending:
            g()
        pending = []
        for si in range(2):
            s = 2 * sp + si
            gs = s
            par = s % 2
            pA, pAb = P[3 * par]
            pB, pBb = P[3 * par + 1]
            pC2, pC2b = P[3 * par + 2]
            pt, ptb = PT2, PT2b
            if sp == 0:
                for th in range(2):
                    p.emit("vector", (lambda h, th=th, par=par: h.memset(
                        v_s[par][:, th * (D + 1) + D:th * (D + 1) + D + 1], 1.0)),
                        writes=[v_b[par]], tag="l0 v ones")
            for th in range(2):
                for n in range(2):
                    pp, ppb = (pA, pAb) if n == 0 else (pB, pBb)
                    builds = [
                        (lambda h, th=th, n=n, c=c, pp=pp, si=si,
                         xnTp_s=xnTp_s: h.matmul(
                            pp[:, 0:512],
                            xnTp_s[:, c * 2 * SEG + si * SEG + th * 128:c * 2 * SEG + si * SEG + (th + 1) * 128],
                            wv_s[:, c * D + n * 512:c * D + (n + 1) * 512],
                            start=(c == 0), stop=False))
                        for c in range(NCHUNK)]
                    builds.append(lambda h, n=n, pp=pp: h.matmul(
                        pp[:, 0:512], ones1_s[0:1, :],
                        biasbf_s[0:1, 2 * D + n * 512:2 * D + (n + 1) * 512],
                        start=False, stop=True))
                    p.emit_group("tensor", builds,
                                 reads=[xnTp_b, wv_b, ones1_b, biasbf_b],
                                 writes=[ppb], tag="l0 v mm")
                    p.emit("vector", (lambda h, th=th, n=n, pp=pp, par=par:
                                      h.tensor_copy(
                        v_s[par][:, th * (D + 1) + n * 512:th * (D + 1) + (n + 1) * 512],
                        pp[:, 0:512])),
                        reads=[ppb], writes=[v_b[par]], tag="l0 v evac")
            for kt in range(2):
                pp, ppb = (pA, pAb) if kt == 0 else (pB, pBb)
                p.emit_group("tensor", [
                    (lambda h, kt=kt, c=c, pp=pp, si=si, kTp_s=kTp_s,
                     qTp_s=qTp_s: h.matmul(
                        pp[:, 0:256],
                        kTp_s[:, c * 2 * SEG + si * SEG + kt * 128:c * 2 * SEG + si * SEG + (kt + 1) * 128],
                        qTp_s[:, c * 2 * SEG + si * SEG:c * 2 * SEG + (si + 1) * SEG],
                        start=(c == 0), stop=(c == NCHUNK - 1)))
                    for c in range(NCHUNK)],
                    reads=[kTp_b, qTp_b], writes=[ppb], tag="l0 sc mm")
                p.emit("scalar", (lambda h, kt=kt, pp=pp, par=par: h.activation(
                    pT_s[par][:, ts(kt, 256)], pp[:, 0:256], AF.Exp,
                    scale=eighth)),
                    reads=[ppb, ctx._ccb], writes=[pT_b[par]], tag="l0 exp")
            for qh in range(2):
                sb_ = [
                    (lambda h, qh=qh, kt=kt, par=par: h.matmul(
                        pC2[:, 4 + qh:5 + qh],
                        pT_s[par][:, kt * 256 + qh * 128:kt * 256 + (qh + 1) * 128],
                        v_s[par][:, kt * (D + 1) + D:kt * (D + 1) + D + 1],
                        start=(kt == 0), stop=(kt == 1)))
                    for kt in range(2)]
                p.emit_group("tensor", sb_, reads=[pT_b[par], v_b[par]],
                             writes=[pC2b], tag="l0 sums mm")
                p.emit("vector", (lambda h, qh=qh, par=par: h.reciprocal(
                    rsum_s[:, 2 * par + qh:2 * par + qh + 1],
                    pC2[:, 4 + qh:5 + qh])),
                    reads=[pC2b], writes=[rsum_b], tag="l0 rsum")
                for n in range(2):
                    pp, ppb = (pA, pAb) if n == 0 else (pB, pBb)
                    builds = [
                        (lambda h, qh=qh, n=n, kt=kt, pp=pp, par=par: h.matmul(
                            pp[:, 0:512],
                            pT_s[par][:, kt * 256 + qh * 128:kt * 256 + (qh + 1) * 128],
                            v_s[par][:, kt * (D + 1) + n * 512:kt * (D + 1) + (n + 1) * 512],
                            start=(kt == 0), stop=(kt == 1)))
                        for kt in range(2)]
                    p.emit_group("tensor", builds, reads=[pT_b[par], v_b[par]],
                                 writes=[ppb], tag="l0 av mm")
                    p.emit("vector", (lambda h, qh=qh, n=n, pp=pp, par=par:
                                      h.tensor_scalar(
                        fused_s[par][:, qh * D + n * 512:qh * D + (n + 1) * 512],
                        pp[:, 0:512], rsum_s[:, 2 * par + qh:2 * par + qh + 1],
                        None, ALU.mult)),
                        reads=[ppb, rsum_b], writes=[fused_b[par]],
                        tag="l0 av evac")
            for qh in range(2):
                p.emit_group("tensor", [
                    (lambda h, qh=qh, c=c, par=par, pt=pt: h.transpose(
                        pt[:, ts(c, 128)],
                        fused_s[par][:, qh * D + c * 128:qh * D + (c + 1) * 128],
                        ident[:, :]))
                    for c in range(NCHUNK)],
                    reads=[fused_b[par], identb], writes=[ptb], tag="l0 fT")
                p.emit("vector", (lambda h, qh=qh, gs=gs, par=par,
                                  pt=pt: h.tensor_tensor(
                    fusedT_s[par].rearrange("p (c q) -> p c q", c=NCHUNK)[:, :, ts(qh, 128)],
                    pt.rearrange("p (c q) -> p c q", c=NCHUNK),
                    goutT_s.rearrange("p (c g) -> p c g", c=NCHUNK)[:, :, gs:gs + 1].broadcast_to((128, NCHUNK, 128)),
                    op=ALU.add)),
                    reads=[ptb, goutT_b], writes=[fusedT_b[par]],
                    tag="l0 fT evac")
            for th in range(2):
                t = 2 * s + th
                for n in range(2):
                    pp, ppb = (pA, pAb) if n == 0 else (pB, pBb)
                    builds = [
                        (lambda h, th=th, n=n, c=c, pp=pp, par=par: h.matmul(
                            pp[:, 0:512],
                            fusedT_s[par].rearrange("p (c q) -> p c q", c=NCHUNK)[:, c, ts(th, 128)],
                            wc_s[:, c * D + n * 512:c * D + (n + 1) * 512],
                            start=(c == 0), stop=False))
                        for c in range(NCHUNK)]
                    builds.append(lambda h, n=n, pp=pp: h.matmul(
                        pp[:, 0:512], ones1_s[0:1, :],
                        biasbf_s[0:1, 3 * D + n * 512:3 * D + (n + 1) * 512],
                        start=False, stop=True))
                    p.emit_group("tensor", builds,
                                 reads=[fusedT_b[par], wc_b, ones1_b, biasbf_b],
                                 writes=[ppb], tag="l0 wc mm")
                    p.emit("vector", (lambda h, th=th, n=n, pp=pp, par=par:
                                      h.tensor_copy(
                        xo_s[par][:, th * D + n * 512:th * D + (n + 1) * 512],
                        pp[:, 0:512])),
                        reads=[ppb], writes=[xo_b[par]], tag="l0 y evac")
                ln_stats_acc(ctx, xo_s[par][:, ts(th, D)], xo_b[par],
                             stat_s[par][:, 8 * th:8 * th + 8], stat_b[par],
                             pref="l0y ")
                pending.append(l0_tail(t, par, th))
    for fn, _ in pending:
        fn()
    for _, g in pending:
        g()
    pending = []




HID = 3072
NHC = HID // 128


def build_l0ffn(ctx, x_in, x_out, wgT, wuT, wdT, grid_ext=None):
    nc, p = ctx.nc, ctx.p
    ctx.store_toks = []

    wg_s, _wgb0 = ctx.sbuf("f0_wg", [128, NCHUNK * HID], BF16)
    wu_s, _wub0 = ctx.sbuf("f0_wu", [128, NCHUNK * HID], BF16)
    wd_s, _wdb0 = ctx.sbuf("f0_wd", [128, NHC * D], BF16)
    wg_b = [Buf("f0_wg%d" % n) for n in range(HID // 512)]
    wu_b = [Buf("f0_wu%d" % n) for n in range(HID // 512)]
    wd_b = [Buf("f0_wd%d" % n) for n in range(2)]
    x_s, x_b = zip(*[ctx.sbuf("f0_x%d" % i, [128, D], F32) for i in range(2)])
    stat_s, stat_b = zip(*[ctx.sbuf("f0_st%d" % i, [128, 8], F32) for i in range(2)])
    xn_s, xn_b = zip(*[ctx.sbuf("f0_xn%d" % i, [128, D], BF16) for i in range(2)])
    xnT_s, xnT_b = zip(*[ctx.sbuf("f0_xnT%d" % i, [128, D], BF16) for i in range(2)])
    g_s, g_b = zip(*[ctx.sbuf("f0_g%d" % i, [128, 512], BF16) for i in range(2)])
    h_s, _hb0 = zip(*[ctx.sbuf("f0_h%d" % i, [128, HID], BF16) for i in range(2)])
    hb = [[Buf("f0_h%d_%d" % (i, n)) for n in range(HID // 512)]
          for i in range(2)]
    hT_s, _hTb0 = zip(*[ctx.sbuf("f0_hT%d" % i, [128, HID], BF16)
                        for i in range(2)])
    hTg = [[Buf("f0_hTg%d_%d" % (i, g)) for g in range(3)] for i in range(2)]
    lny_s, lny_b = zip(*([ctx.sbuf("f0_lny", [128, D], BF16)] * 2))
    xr_s, xr_b = zip(*([ctx.sbuf("f0_xr", [128, D], F32)] * 2))
    xo_s, xo_b = zip(*[ctx.sbuf("f0_xo%d" % i, [128, D], F32) for i in range(2)])

    P = [ctx.psum("f0_P%d" % i, [128, 512], F32) for i in range(6)]
    PT, PTb = ctx.psum("f0_PT", [128, 1024], BF16)
    PT2, PT2b = ctx.psum("f0_PT2", [128, 1024], BF16)

    ident = ctx.ident_s
    identb = ctx.ident_b
    inv127 = ctx.cconst(1.0 / 127.0)

    for n in range(HID // 512):
        for c in range(NCHUNK):
            p.dma("gpsimd", wg_s[:, c * HID + n * 512:c * HID + (n + 1) * 512],
                  wgT[ts(c, 128), n * 512:(n + 1) * 512], writes=[wg_b[n]],
                  tag="f0 wg")
            p.dma("gpsimd", wu_s[:, c * HID + n * 512:c * HID + (n + 1) * 512],
                  wuT[ts(c, 128), n * 512:(n + 1) * 512], writes=[wu_b[n]],
                  tag="f0 wu")
    for n in range(2):
        for c in range(NHC):
            p.dma("gpsimd", wd_s[:, c * D + n * 512:c * D + (n + 1) * 512],
                  wdT[ts(c, 128), n * 512:(n + 1) * 512], writes=[wd_b[n]],
                  tag="f0 wd")

    if grid_ext is None:
        grid_s, grid_b = ctx.sbuf("f0_grid", [128, 192], F32)
        ln_prepass(ctx, [(x_in[ts(t, 128), :], None) for t in range(NT)],
                   grid_s, grid_b, [x_s[0][:, :], x_s[1][:, :]],
                   [x_b[0], x_b[1]], pref="f0 ")
    else:
        grid_s, grid_b = grid_ext
        ln_prepass_finish(ctx, NT, grid_s, grid_b, grid_b, pref="f0 ")

    def f0_fetch(t):
        par = t % 2
        p.dma("sync", x_s[par][:, :], x_in[ts(t, 128), :], writes=[x_b[par]],
              tag="f0 x")

    def f0_ln(t):
        par = t % 2
        p.emit("vector", (lambda h, par=par, t=t: h.tensor_scalar(
            xn_s[par][:, :], x_s[par][:, :], grid_s[:, 128 + t:129 + t],
            grid_s[:, 160 + t:161 + t], ALU.mult, ALU.add)),
            reads=[x_b[par], grid_b], writes=[xn_b[par]], tag="f0 xn")

    def f0_xnT(t):
        par = t % 2
        p.emit_group("tensor", [
            (lambda h, c=c, par=par: h.transpose(
                PT[:, ts(c, 128)], xn_s[par][:, ts(c, 128)], ident[:, :]))
            for c in range(NCHUNK)],
            reads=[xn_b[par], identb], writes=[PTb], tag="f0 xnT")
        p.emit("vector", (lambda h, par=par: h.tensor_copy(
            xnT_s[par][:, :], PT[:, :])),
            reads=[PTb], writes=[xnT_b[par]], tag="f0 xnT evac")

    def emit_hT(par, g):
        p.emit_group("tensor", [
            (lambda h, g=g, c=c, par=par: h.transpose(
                PT2[:, ts(c, 128)],
                h_s[par][:, g * 1024 + c * 128:g * 1024 + (c + 1) * 128],
                ident[:, :]))
            for c in range(NCHUNK)],
            reads=[hb[par][2 * g], hb[par][2 * g + 1], identb], writes=[PT2b],
            tag="f0 hT")
        p.emit("vector", (lambda h, g=g, par=par: h.tensor_copy(
            hT_s[par][:, ts(g, 1024)], PT2[:, :])),
            reads=[PT2b], writes=[hTg[par][g]], tag="f0 hT evac")

    def emit_down_sub(par, n, g, pp, ppb):
        p.emit_group("tensor", [
            (lambda h, n=n, c=c, par=par, pp=pp: h.matmul(
                pp[:, :], hT_s[par][:, ts(c, 128)],
                wd_s[:, c * D + n * 512:c * D + (n + 1) * 512],
                start=(c == 0), stop=(c == NHC - 1)))
            for c in range(8 * g, 8 * g + 8)],
            reads=[hTg[par][g], wd_b[n]], writes=[ppb], tag="f0 d mm")

    def emit_downs(t):
        par = t % 2
        pB, pBb = P[3 * par + 1]
        pC, pCb = P[3 * par + 2]
        emit_down_sub(par, 0, 0, pC, pCb)
        emit_down_sub(par, 0, 1, pC, pCb)
        emit_down_sub(par, 1, 0, pB, pBb)
        emit_down_sub(par, 1, 1, pB, pBb)
        emit_down_sub(par, 0, 2, pC, pCb)
        p.emit("vector", (lambda h, par=par, pC=pC: h.tensor_copy(
            xo_s[par][:, 0:512], pC[:, :])),
            reads=[pCb], writes=[xo_b[par]], tag="f0 y evac")
        emit_down_sub(par, 1, 2, pB, pBb)
        p.emit("vector", (lambda h, par=par, pB=pB: h.tensor_copy(
            xo_s[par][:, 512:1024], pB[:, :])),
            reads=[pBb], writes=[xo_b[par]], tag="f0 y evac")

    f0_tail = f0_tail_mk(ctx, x_in, x_out, xo_s, xo_b, stat_s, stat_b,
                         lny_s, lny_b, xr_s, xr_b)
    f0_fetch(0)
    f0_ln(0)
    f0_xnT(0)
    f0_fetch(1)
    for t in range(NT):
        par = t % 2
        pA, pAb = P[3 * par]
        pB, pBb = P[3 * par + 1]
        for n in range(HID // 512):
            p.emit_group("tensor", [
                (lambda h, n=n, c=c, par=par, pA=pA: h.matmul(
                    pA[:, :], xnT_s[par][:, ts(c, 128)],
                    wg_s[:, c * HID + n * 512:c * HID + (n + 1) * 512],
                    start=(c == 0), stop=(c == NCHUNK - 1)))
                for c in range(NCHUNK)],
                reads=[xnT_b[par], wg_b[n]], writes=[pAb], tag="f0 g mm")
            p.emit("scalar", (lambda h, n=n, par=par, pA=pA: h.activation(
                g_s[par][:, 0:512], pA[:, :], AF.Silu, scale=inv127)),
                reads=[pAb, ctx._ccb], writes=[g_b[par]], tag="f0 silu")
            p.emit_group("tensor", [
                (lambda h, n=n, c=c, par=par, pB=pB: h.matmul(
                    pB[:, :], xnT_s[par][:, ts(c, 128)],
                    wu_s[:, c * HID + n * 512:c * HID + (n + 1) * 512],
                    start=(c == 0), stop=(c == NCHUNK - 1)))
                for c in range(NCHUNK)],
                reads=[xnT_b[par], wu_b[n]], writes=[pBb], tag="f0 u mm")
            p.emit("vector", (lambda h, n=n, par=par, pB=pB: h.tensor_tensor(
                h_s[par][:, ts(n, 512)], g_s[par][:, 0:512],
                pB[:, :], op=ALU.mult)),
                reads=[g_b[par], pBb], writes=[hb[par][n]], tag="f0 h")
            if n == 2:
                emit_hT(par, 0)
            if n == 4:
                emit_hT(par, 1)
                if t + 1 < NT:
                    f0_ln(t + 1)
        if t + 1 < NT:
            f0_xnT(t + 1)
        if t + 2 < NT:
            f0_fetch(t + 2)
        if t > 0:
            emit_downs(t - 1)
        emit_hT(par, 2)
        if t > 0:
            ln_stats_acc_v(ctx, xo_s[1 - par][:, :], xo_b[1 - par],
                           stat_s[1 - par], stat_b[1 - par], pref="f0y ")
            f0_tail(t - 1)
    emit_downs(NT - 1)
    par = (NT - 1) % 2
    ln_stats_acc_v(ctx, xo_s[par][:, :], xo_b[par], stat_s[par], stat_b[par],
                   pref="f0y ")
    f0_tail(NT - 1)


def f0_tail_mk(ctx, x_in, x_out, xo_s, xo_b, stat_s, stat_b, lny_s, lny_b,
               xr_s, xr_b):
    p = ctx.p

    def f0_tail(t):
        par = t % 2
        ln_stats_fin(ctx, stat_s[par], stat_b[par], eps_scale=127.0 ** 4,
                     pref="f0y ")
        p.emit("vector", (lambda h, par=par: h.tensor_scalar(
            lny_s[par][:, :], xo_s[par][:, :], stat_s[par][:, 4:5],
            stat_s[par][:, 5:6], ALU.mult, ALU.add)),
            reads=[xo_b[par], stat_b[par]], writes=[lny_b[par]], tag="f0 lny")
        p.dma("sync", xr_s[par][:, :], x_in[ts(t, 128), :],
              writes=[xr_b[par]], tag="f0 xre")
        p.emit("vector", (lambda h, par=par: h.scalar_tensor_tensor(
            xo_s[par][:, :], lny_s[par][:, :], 1.0, xr_s[par][:, :],
            ALU.mult, ALU.add)),
            reads=[lny_b[par], xr_b[par]], writes=[xo_b[par]], tag="f0 resid")
        ctx.store_toks.append(p.dma(
            "scalar", x_out[ts(t, 128), :], xo_s[par][:, :], reads=[xo_b[par]],
            tag="f0 xstore"))
        grid_acc_v(ctx, xo_s[par][:, :], xo_b[par], t, "f0")
        if ctx.on_store:
            ctx.on_store(len(ctx.store_toks) - 1, ctx.store_toks[-1])

    return f0_tail


def build_program_l0ffn():
    nc = bass.Bass("TRN2", target_bir_lowering=False, debug=False)
    x_in = nc.declare_dram_parameter("x_in", [TOK, D], F32, isOutput=False)
    wgT = nc.declare_dram_parameter("wgT", [D, HID], I32, isOutput=False)
    wuT = nc.declare_dram_parameter("wuT", [D, HID], I32, isOutput=False)
    wdT = nc.declare_dram_parameter("wdT", [HID, D], I32, isOutput=False)
    ident = nc.declare_dram_parameter("ident", [128, 128], BF16, isOutput=False)
    x_out = nc.declare_dram_parameter("x_out", [TOK, D], F32, isOutput=True)

    es = contextlib.ExitStack()
    with es:
        prog = Prog(nc)
        ctx = Ctx(nc, prog, es)
        ctx.ident_s, ctx.ident_b = ctx.sbuf("ident_s", [128, 128], BF16)
        tr, trb = ctx.sbuf("trash", [128, D], BF16)
        ctx.trash_s = tr
        ctx.trashb = trb
        prog.dma("sync", ctx.ident_s[:, :], ident.ap(), writes=[ctx.ident_b])
        build_l0ffn(ctx, x_in.ap(), x_out.ap(), wgT.ap(), wuT.ap(), wdT.ap())
        prog.barrier()
        prog.realize(es)
    return nc



NKT = 18
KLO = 2304
NH = 16
NLH = 8
HD = 64


def _win(ktk):
    lo = 0
    hi = 384
    if ktk == 0:
        lo = 256
    elif ktk == 1:
        lo = 128
    if ktk == NKT - 2:
        hi = 256
    elif ktk == NKT - 1:
        hi = 128
    return lo, hi


def build_l1attn(ctx, x_in, xhalo, xother, x_out, wq, wk, wv, wgq, wgk, wgv,
                 r_out_w, glgrw, w_out, locmask, halomask, lmask, dbg_aps=None,
                 cc=None, grid_ext=None):
    nc, p = ctx.nc, ctx.p
    ctx.store_toks = []
    ident = ctx.ident_s
    identb = ctx.ident_b
    eighth = ctx.cconst(0.125)

    pooledT_s, pooledT_b = ctx.sbuf("a1_pooledT", [128, NCHUNK * 64], BF16)
    gates_s, gates_b = ctx.sbuf("a1_gates", [128, NT * 32], F32)
    local_s, local_b = ctx.sbuf("a1_local", [128, NT * D], BF16)
    rem_s, rem_b = ctx.sbuf("a1_rem", [128, NT * 512], BF16)
    lmask_s, lmask_b = ctx.sbuf("a1_lmk", [64, TOK], BF16)
    locm_s, locm_b = ctx.sbuf("a1_locm", [128, 2 * 384], BF16)

    PS = [ctx.psum("a1_P%d" % i, [128, 512], F32) for i in range(6)]
    PT, PTb = ctx.psum("a1_PT", [128, 1024], BF16)
    PT2, PT2b = ctx.psum("a1_PT2", [128, 1024], BF16)

    p.dma("gpsimd", lmask_s[:, :], lmask, writes=[lmask_b], tag="a1 lmask")
    p.dma("gpsimd", locm_s[:, 0:384], locmask, writes=[locm_b], tag="a1 locm")
    p.dma("gpsimd", locm_s[:, 384:768], halomask, writes=[locm_b], tag="a1 hmask")

    with ctx.scope():
        xnT_s, xnT_b = ctx.sbuf("a1_xnT", [128, NCHUNK * KLO], BF16)

        if cc is not None and not cc.get("tail_pre"):
            p.dma("gpsimd", cc["tail_in"], x_in[TOK - 256:TOK, :],
                  waits=[cc["x_ready"]], writes=[cc["tail_in_b"]], tag="cc tailw")
            p.collective("AllGather", ALU.bypass, cc["groups"],
                         [cc["tail_in"]], [cc["tail_out"]],
                         reads=[cc["tail_in_b"]], writes=[cc["tail_out_b"]],
                         sem_name=cc["sem_tail"], tag="cc tail")

        with ctx.scope():
            x_s, x_b = zip(*[ctx.sbuf("a1_x%d" % i, [128, D], F32) for i in range(2)])
            stat_s, stat_b = zip(*[ctx.sbuf("a1_st%d" % i, [128, 8], F32) for i in range(2)])
            xn_s, xn_b = zip(*[ctx.sbuf("a1_xn%d" % i, [128, D], BF16) for i in range(2)])
            poolw_s, poolw_b = ctx.sbuf("a1_plw", [128, 2], BF16)
            pstage_s, pstage_b = ctx.sbuf("a1_pstage", [128, NCHUNK * (32 if cc is not None else 64)], BF16)
            p.dma("gpsimd", poolw_s[:, :], ctx.consts["poolw"], writes=[poolw_b],
                  tag="a1 poolw")

            if grid_ext is not None:
                grid_s, grid_b = grid_ext
                p.dma("gpsimd", cc["ghalo_in"][:, 0:2], grid_s[:, 14:16],
                      reads=[grid_b], writes=[cc["ghalo_in_b"]], tag="cc ghw1")
                p.dma("gpsimd", cc["ghalo_in"][:, 2:4], grid_s[:, 46:48],
                      reads=[grid_b], writes=[cc["ghalo_in_b"]], tag="cc ghw2")
                p.collective("AllGather", ALU.bypass, cc["groups"],
                             [cc["ghalo_in"]], [cc["ghalo_out"]],
                             reads=[cc["ghalo_in_b"]],
                             writes=[cc["ghalo_out_b"]],
                             sem_name=cc["sem_ghalo"], tag="cc ghalo")
                p.dma("sync", grid_s[:, 16:18], cc["ghalo_out"][0:128, 0:2],
                      waits=[(cc["sem_ghalo"], 1)], writes=[grid_b],
                      tag="a1 ghr1")
                p.dma("sync", grid_s[:, 48:50], cc["ghalo_out"][0:128, 2:4],
                      waits=[(cc["sem_ghalo"], 1)], writes=[grid_b],
                      tag="a1 ghr2")
                ln_prepass_finish(ctx, NT + 2, grid_s, grid_b, grid_b,
                                  pref="a1 ")
            else:
                grid_s, grid_b = ctx.sbuf("a1_grid", [128, 192], F32)
                if cc is not None:
                    psrcs = [(x_in[ts(t, 128), :], None) for t in range(NT)]
                    psrcs += [(cc["tail_out"][th * 128:(th + 1) * 128, :],
                               [cc["tail_tok"]]) for th in range(2)]
                    ln_prepass(ctx, psrcs, grid_s, grid_b,
                               [x_s[0][:, :], x_s[1][:, :]],
                               [x_b[0], x_b[1]], pref="a1 ")

            def ln_tile(par, src_dram_slice, tag, extra_waits=None, gcol=None):
                p.dma("sync", x_s[par][:, :], src_dram_slice, writes=[x_b[par]],
                      waits=extra_waits, tag=tag + " x")
                if gcol is None:
                    ln_stats(ctx, x_s[par][:, :], x_b[par], stat_s[par],
                             stat_b[par], pref=tag + " ")
                    sc1 = stat_s[par][:, 4:5]
                    sc2 = stat_s[par][:, 5:6]
                    rb = [stat_b[par]]
                else:
                    sc1 = grid_s[:, 128 + gcol:129 + gcol]
                    sc2 = grid_s[:, 160 + gcol:161 + gcol]
                    rb = [grid_b]
                p.emit("vector", (lambda h, par=par, sc1=sc1, sc2=sc2:
                                  h.tensor_scalar(
                    xn_s[par][:, :], x_s[par][:, :], sc1, sc2,
                    ALU.mult, ALU.add)),
                    reads=[x_b[par]] + rb, writes=[xn_b[par]],
                    tag=tag + " xn")

            def xnT_tile(par, colbase, tag):
                pt, ptb = (PT, PTb) if par == 0 else (PT2, PT2b)
                p.emit_group("tensor", [
                    (lambda h, par=par, c=c, pt=pt: h.transpose(
                        pt[:, ts(c, 128)], xn_s[par][:, ts(c, 128)], ident[:, :]))
                    for c in range(NCHUNK)],
                    reads=[xn_b[par], identb], writes=[ptb], tag=tag + " xnT")
                p.emit("vector", (lambda h, colbase=colbase, pt=pt: h.tensor_copy(
                    xnT_s.rearrange("p (c q) -> p c q", c=NCHUNK)[:, :, colbase:colbase + 128],
                    pt.rearrange("p (c q) -> p c q", c=NCHUNK)[:, :, :])),
                    reads=[ptb], writes=[xnT_b], tag=tag + " xnT ev")

            def pool_tile(par, poolcol, tag):
                pp, ppb = PS[4 + par]
                for fs in range(NCHUNK):
                    p.emit("tensor", (lambda h, par=par, fs=fs, pp=pp: h.matmul(
                        pp[:, fs * 2:fs * 2 + 2], xn_s[par][:, ts(fs, 128)],
                        poolw_s[:, :], start=True, stop=True)),
                        reads=[xn_b[par], poolw_b], writes=[ppb], tag=tag + " pool")
                p.emit("vector", (lambda h, poolcol=poolcol, pp=pp: h.tensor_copy(
                    pstage_s.rearrange("p (c n) -> p c n", c=NCHUNK)[:, :, poolcol:poolcol + 2],
                    pp.rearrange("p (c n) -> p c n", c=256)[:, 0:NCHUNK, :])),
                    reads=[ppb], writes=[pstage_b], tag=tag + " poolev")

            if cc is None:
                for t in range(NT):
                    ln_tile(t % 2, x_in[ts(t, 128), :], "a1")
                    xnT_tile(t % 2, 256 + t * 128, "a1")
                    pool_tile(t % 2, 2 * t, "a1")
                for th in range(2):
                    ln_tile(th, xhalo[ts(th, 128), :], "a1h")
                    xnT_tile(th, th * 128, "a1h")
                for t in range(NT):
                    ln_tile(t % 2, xother[ts(t, 128), :], "a1o")
                    pool_tile(t % 2, 32 + 2 * t, "a1o")
                for c in range(NCHUNK):
                    p.emit("vector", (lambda h, c=c: h.tensor_copy(
                        pooledT_s.rearrange("p (c n) -> p c n", c=NCHUNK)[:, c, :],
                        pstage_s.rearrange("p (c n) -> p c n", c=NCHUNK)[:, c, :])),
                        reads=[pstage_b], writes=[pooledT_b], tag="a1 poolcp")
            else:
                for t in range(NT):
                    ln_tile(t % 2, x_in[ts(t, 128), :], "a1", gcol=t)
                    xnT_tile(t % 2, 256 + t * 128, "a1")
                    pool_tile(t % 2, 2 * t, "a1")
                for th in range(2):
                    ln_tile(th, cc["tail_out"][th * 128:(th + 1) * 128, :], "a1h",
                            extra_waits=[cc["tail_tok"]], gcol=NT + th)
                    xnT_tile(th, th * 128, "a1h")
                p.dma("gpsimd", cc["pool_in"], pstage_s[:, :],
                      reads=[pstage_b], writes=[cc["pool_in_b"]], tag="cc poolw")
                p.collective("AllGather", ALU.bypass, cc["groups"],
                             [cc["pool_in"]], [cc["pool_out"]],
                             reads=[cc["pool_in_b"]], writes=[cc["pool_out_b"]],
                             sem_name=cc["sem_pool"], tag="cc pool")
                for rnk in range(2):
                    p.dma("sync",
                          pooledT_s.rearrange("p (c n) -> p c n", c=NCHUNK)[:, :, 32 * rnk:32 * rnk + 32],
                          cc["pool_out"][rnk * 128:(rnk + 1) * 128, :].rearrange("p (c n) -> p c n", c=NCHUNK),
                          waits=[(cc["sem_pool"], 1)], writes=[pooledT_b],
                          tag="cc poolr")


        with ctx.scope():
            gqTe_s, gqTe_b = ctx.sbuf("a1_gqTe", [128, TOK], BF16)
            gkTe_s, gkTe_b = ctx.sbuf("a1_gkTe", [128, 64], BF16)
            gvp_s, gvp_b = ctx.sbuf("a1_gvp", [64, 130], BF16)
            wsg3_s, wsg3_b = zip(*[ctx.sbuf("a1_wsg%d" % i, [128, 3 * D], BF16)
                                   for i in range(2)])
            lpT_s, lpT_b = ctx.sbuf("a1_lpT", [64, TOK], BF16)
            rsum_s, rsum_b = ctx.sbuf("a1_rsum", [128, 2], F32)

            def load_wsg(e, par):
                for c in range(NCHUNK):
                    p.dma("gpsimd", wsg3_s[par][:, ts(c, 128)],
                          wgq[ts(c, 128), ts(e, 128)], writes=[wsg3_b[par]],
                          tag="a1 wsgq")
                    p.dma("gpsimd",
                          wsg3_s[par][:, D + c * 128:D + (c + 1) * 128],
                          wgk[ts(c, 128), ts(e, 128)], writes=[wsg3_b[par]],
                          tag="a1 wsgk")
                    p.dma("gpsimd",
                          wsg3_s[par][:, 2 * D + c * 128:2 * D + (c + 1) * 128],
                          wgv[ts(c, 128), ts(e, 128)], writes=[wsg3_b[par]],
                          tag="a1 wsgv")

            load_wsg(0, 0)
            for e in range(4):
                wsg_s, wsg_b = wsg3_s[e % 2], wsg3_b[e % 2]
                if e + 1 < 4:
                    load_wsg(e + 1, (e + 1) % 2)
                for g in range(TOK // 512):
                    pa, pab = PS[g % 2]
                    p.emit_group("tensor", [
                        (lambda h, g=g, c=c, pa=pa, wsg_s=wsg_s: h.matmul(
                            pa[:, :], wsg_s[:, ts(c, 128)],
                            xnT_s[:, c * KLO + 256 + g * 512:c * KLO + 256 + (g + 1) * 512],
                            start=(c == 0), stop=(c == NCHUNK - 1)))
                        for c in range(NCHUNK)],
                        reads=[wsg_b, xnT_b], writes=[pab], tag="a1 gq mm")
                    p.emit("vector", (lambda h, g=g, pa=pa: h.tensor_copy(
                        gqTe_s[:, ts(g, 512)], pa[:, :])),
                        reads=[pab], writes=[gqTe_b], tag="a1 gq ev")
                pb_, pbb_ = PS[2]
                p.emit_group("tensor", [
                    (lambda h, c=c, pb_=pb_, wsg_s=wsg_s: h.matmul(
                        pb_[:, 0:64], wsg_s[:, D + c * 128:D + (c + 1) * 128],
                        pooledT_s[:, ts(c, 64)], start=(c == 0),
                        stop=(c == NCHUNK - 1)))
                    for c in range(NCHUNK)],
                    reads=[wsg_b, pooledT_b], writes=[pbb_], tag="a1 gk mm")
                p.emit("vector", (lambda h, pb_=pb_: h.tensor_copy(
                    gkTe_s[:, 0:64], pb_[:, 0:64])),
                    reads=[pbb_], writes=[gkTe_b], tag="a1 gk ev")
                p.emit("vector", lambda h: h.memset(
                    gvp_s[0:64, 64:65], 1.0), writes=[gvp_b], tag="a1 gv ones")
                p.emit("vector", lambda h: h.memset(
                    gvp_s[0:64, 129:130], 1.0), writes=[gvp_b], tag="a1 gv ones2")
                pc_, pcb_ = PS[3]
                p.emit_group("tensor", [
                    (lambda h, c=c, pc_=pc_, wsg_s=wsg_s: h.matmul(
                        pc_[0:64, 0:128], pooledT_s[:, ts(c, 64)],
                        wsg_s[:, 2 * D + c * 128:2 * D + (c + 1) * 128],
                        start=(c == 0), stop=(c == NCHUNK - 1)))
                    for c in range(NCHUNK)],
                    reads=[pooledT_b, wsg_b], writes=[pcb_], tag="a1 gv mm")
                p.emit("vector", (lambda h, pc_=pc_: h.tensor_copy(
                    gvp_s.rearrange("p (u e) -> p u e", u=2)[0:64, :, 0:64],
                    pc_[0:64, 0:128].rearrange("p (u e) -> p u e", u=2))),
                    reads=[pcb_], writes=[gvp_b], tag="a1 gv ev")
                for sub in range(2):
                    lh = 2 * e + sub
                    rr = sub * 64
                    for g in range(TOK // 512):
                        pe_, peb_ = PS[g % 2]
                        p.emit("tensor", (lambda h, g=g, rr=rr, pe_=pe_: h.matmul(
                            pe_[0:64, :], gkTe_s[rr:rr + 64, 0:64],
                            gqTe_s[rr:rr + 64, ts(g, 512)],
                            start=True, stop=True)),
                            reads=[gkTe_b, gqTe_b], writes=[peb_], tag="a1 lsc")
                        p.emit("scalar", (lambda h, g=g, pe_=pe_: h.activation(
                            ctx.trash_s[0:64, ts(g % 2, 512)], pe_[0:64, :],
                            AF.Exp, scale=eighth[0:64, :])),
                            reads=[peb_, ctx._ccb], writes=[ctx.trashb],
                            tag="a1 lexp")
                        p.emit("vector", (lambda h, g=g: h.tensor_tensor(
                            lpT_s[0:64, ts(g, 512)],
                            ctx.trash_s[0:64, ts(g % 2, 512)],
                            lmask_s[0:64, ts(g, 512)], op=ALU.mult)),
                            reads=[ctx.trashb, lmask_b], writes=[lpT_b],
                            tag="a1 lpmask")
                    for qt in range(NT):
                        pf_, pfb_ = PS[2 + qt % 2]
                        p.emit("tensor", (lambda h, qt=qt, sub=sub, pf_=pf_: h.matmul(
                            pf_[:, 0:65], lpT_s[0:64, ts(qt, 128)],
                            gvp_s[0:64, sub * 65:(sub + 1) * 65],
                            start=True, stop=True)),
                            reads=[lpT_b, gvp_b], writes=[pfb_], tag="a1 rv mm")
                        p.emit("vector", (lambda h, qt=qt, pf_=pf_: h.reciprocal(
                            rsum_s[:, qt % 2:qt % 2 + 1], pf_[:, 64:65])),
                            reads=[pfb_], writes=[rsum_b], tag="a1 rrecip")
                        p.emit("vector", (lambda h, lh=lh, qt=qt, pf_=pf_: h.tensor_scalar(
                            rem_s[:, qt * 512 + lh * 64:qt * 512 + (lh + 1) * 64],
                            pf_[:, 0:64], rsum_s[:, qt % 2:qt % 2 + 1], None,
                            ALU.mult)),
                            reads=[pfb_, rsum_b], writes=[rem_b],
                            tag="a1 rev")

        with ctx.scope():
            wq_s, wq_b = ctx.sbuf("a1_wqf", [128, NCHUNK * D], BF16)
            wk_s, wk_b = ctx.sbuf("a1_wkf", [128, NCHUNK * D], BF16)
            wvh_s, wvh_b = ctx.sbuf("a1_wvh", [128, NCHUNK * 512], BF16)
            v_s, v_b = ctx.sbuf("a1_vf", [128, NKT * (NH * 65)], BF16)
            qTe_s, qTe_b = ctx.sbuf("a1_qTe", [128, TOK], BF16)
            kTe_s, kTe_b = ctx.sbuf("a1_kTe", [128, KLO], BF16)
            gwe_s, gwe_b = ctx.sbuf("a1_gwe", [128, NCHUNK * 4], BF16)
            pTl_s, pTl_b = ctx.sbuf("a1_pTl", [128, NKT * 384], BF16)
            lsum_s, lsum_b = ctx.sbuf("a1_lsum", [128, 4], F32)
            esc_s, esc_b = ctx.sbuf("a1_esc", [128, 2 * 384], BF16)
            eslot = [esc_s[:, 0:384], esc_s[:, 384:768],
                     ctx.trash_s[:, 0:384], ctx.trash_s[:, 384:768]]
            eslot_full = [esc_s, esc_s, ctx.trash_s, ctx.trash_s]

            for c in range(NCHUNK):
                p.dma("sync", wq_s[:, ts(c, D)], wq[ts(c, 128), :],
                      writes=[wq_b], tag="a1 wq")
                p.dma("sync", wk_s[:, ts(c, D)], wk[ts(c, 128), :],
                      writes=[wk_b], tag="a1 wk")
                p.dma("gpsimd", gwe_s[:, ts(c, 4)], glgrw[ts(c, 128), :],
                      writes=[gwe_b], tag="a1 gwe")
            p.emit("vector", lambda h: h.memset(
                v_s.rearrange("p (k u e) -> p k u e", k=NKT, u=NH)[:, :, :, 64:65],
                1.0), writes=[v_b], tag="a1 v ones")
            for n in range(2):
                for c in range(NCHUNK):
                    p.dma("sync", wvh_s[:, ts(c, 512)],
                          wv[ts(c, 128), n * 512:(n + 1) * 512], writes=[wvh_b],
                          tag="a1 wv")
                for kt in range(NKT):
                    pa, pab = PS[kt % 2]
                    p.emit_group("tensor", [
                        (lambda h, kt=kt, c=c, pa=pa: h.matmul(
                            pa[:, :],
                            xnT_s[:, c * KLO + kt * 128:c * KLO + (kt + 1) * 128],
                            wvh_s[:, ts(c, 512)],
                            start=(c == 0), stop=(c == NCHUNK - 1)))
                        for c in range(NCHUNK)],
                        reads=[xnT_b, wvh_b], writes=[pab], tag="a1 v mm")
                    p.emit("vector", (lambda h, kt=kt, n=n, pa=pa: h.tensor_copy(
                        v_s.rearrange("p (k u e) -> p k u e", k=NKT, u=NH)[:, kt, n * 8:(n + 1) * 8, 0:64],
                        pa[:, :].rearrange("p (u e) -> p u e", u=8))),
                        reads=[pab], writes=[v_b], tag="a1 v ev")

            for e in range(NCHUNK):
                for g in range(TOK // 512):
                    pa, pab = PS[g % 2]
                    p.emit_group("tensor", [
                        (lambda h, e=e, g=g, c=c, pa=pa: h.matmul(
                            pa[:, :], wq_s[:, c * D + e * 128:c * D + (e + 1) * 128],
                            xnT_s[:, c * KLO + 256 + g * 512:c * KLO + 256 + (g + 1) * 512],
                            start=(c == 0), stop=(c == NCHUNK - 1)))
                        for c in range(NCHUNK)],
                        reads=[wq_b, xnT_b], writes=[pab], tag="a1 q mm")
                    p.emit("vector", (lambda h, g=g, pa=pa: h.tensor_copy(
                        qTe_s[:, ts(g, 512)], pa[:, :])),
                        reads=[pab], writes=[qTe_b], tag="a1 q ev")
                kgroups = [(g * 512, 512) for g in range(4)] + [(2048, 256)]
                for gi, (off, nn_) in enumerate(kgroups):
                    pa, pab = PS[2 + gi % 2]
                    p.emit_group("tensor", [
                        (lambda h, e=e, off=off, nn_=nn_, c=c, pa=pa: h.matmul(
                            pa[:, 0:nn_], wk_s[:, c * D + e * 128:c * D + (e + 1) * 128],
                            xnT_s[:, c * KLO + off:c * KLO + off + nn_],
                            start=(c == 0), stop=(c == NCHUNK - 1)))
                        for c in range(NCHUNK)],
                        reads=[wk_b, xnT_b], writes=[pab], tag="a1 k mm")
                    p.emit("vector", (lambda h, off=off, nn_=nn_, pa=pa: h.tensor_copy(
                        kTe_s[:, off:off + nn_], pa[:, 0:nn_])),
                        reads=[pab], writes=[kTe_b], tag="a1 k ev")
                for t in range(NT):
                    pd_, pdb_ = PS[4 + t % 2]
                    p.emit("tensor", (lambda h, t=t, e=e, pd_=pd_: h.matmul(
                        pd_[:, 0:4], qTe_s[:, ts(t, 128)], gwe_s[:, ts(e, 4)],
                        start=True, stop=True)),
                        reads=[qTe_b, gwe_b], writes=[pdb_], tag="a1 gate mm")
                    p.emit("scalar", (lambda h, t=t, e=e, pd_=pd_: h.activation(
                        gates_s.rearrange("p (t x g) -> p t x g", t=NT, x=2)[:, t, :, 2 * e:2 * e + 2],
                        pd_[:, 0:4].rearrange("p (x g) -> p x g", x=2),
                        AF.Sigmoid)),
                        reads=[pdb_], writes=[gates_b], tag="a1 gate sig")
                for sub in range(2):
                    hh = 2 * e + sub
                    rr = sub * 64
                    esc2_b = [Buf("esc%d" % i) for i in range(4)]
                    prev_pmask = getattr(ctx, "_a1_prev_pmask", None)
                    prev_av = getattr(ctx, "_a1_prev_av", None)
                    pTl2_b = [Buf("pTl%d" % i) for i in range(4)]
                    for kt in range(NKT):
                        lo, hi = _win(kt)
                        nn_ = hi - lo
                        q0 = kt * 128 - 256 + lo
                        sl = kt % 4
                        pe_, peb_ = PS[sl]
                        p.emit("tensor", (lambda h, kt=kt, lo=lo, nn_=nn_, q0=q0,
                                          rr=rr, pe_=pe_: h.matmul(
                            pe_[:, 0:nn_],
                            kTe_s[rr:rr + 64, kt * 128:(kt + 1) * 128],
                            qTe_s[rr:rr + 64, q0:q0 + nn_],
                            start=True, stop=True)),
                            reads=[kTe_b, qTe_b], writes=[peb_], tag="a1 sc mm")
                        p.emit("scalar", (lambda h, kt=kt, nn_=nn_, sl=sl,
                                          pe_=pe_: h.activation(
                            eslot[sl][:, 0:nn_],
                            pe_[:, 0:nn_], AF.Exp, scale=eighth)),
                            reads=[peb_, ctx._ccb], writes=[esc2_b[sl]],
                            waits=[prev_pmask] if kt == 0 else None,
                            tag="a1 exp")
                        mslot = 384 if kt < 2 else 0
                        p.emit("vector", (lambda h, kt=kt, lo=lo, nn_=nn_, sl=sl,
                                          mslot=mslot: h.tensor_tensor(
                            pTl_s[:, kt * 384 + lo:kt * 384 + lo + nn_],
                            eslot[sl][:, 0:nn_],
                            locm_s[:, mslot + lo:mslot + lo + nn_], op=ALU.mult)),
                            reads=[esc2_b[sl], locm_b], writes=[pTl2_b[kt % 4]],
                            waits=[prev_av] if kt == 0 else None,
                            tag="a1 pmask")
                    pjoin = [(b.w[0], b.w[1]) for b in pTl2_b if b.w]
                    ctx._a1_prev_pmask = max(pjoin, key=lambda t: t[1])
                    for qt in range(NT):
                        kts = [qt, qt + 1, qt + 2]
                        offs = [256, 128, 0]
                        pf_, pfb_ = PS[qt % 6]
                        builds = []
                        for i, (kt, off) in enumerate(zip(kts, offs)):
                            builds.append(lambda h, hh=hh, kt=kt, off=off, i=i,
                                          pf_=pf_: h.matmul(
                                pf_[:, 0:65],
                                pTl_s[:, kt * 384 + off:kt * 384 + off + 128],
                                v_s[:, kt * (NH * 65) + hh * 65:kt * (NH * 65) + (hh + 1) * 65],
                                start=(i == 0), stop=(i == 2)))
                        p.emit_group("tensor", builds,
                                     reads=[v_b] + pTl2_b[min(qt, 2) % 4:] if False else [v_b],
                                     waits=pjoin if qt == 0 else None,
                                     writes=[pfb_], tag="a1 av mm")
                        p.emit("vector", (lambda h, qt=qt, pf_=pf_: h.reciprocal(
                            lsum_s[:, qt % 4:qt % 4 + 1], pf_[:, 64:65])),
                            reads=[pfb_], writes=[lsum_b], tag="a1 lrecip")
                        p.emit("vector", (lambda h, hh=hh, qt=qt, pf_=pf_: h.tensor_scalar(
                            local_s[:, qt * D + hh * 64:qt * D + (hh + 1) * 64],
                            pf_[:, 0:64], lsum_s[:, qt % 4:qt % 4 + 1],
                            gates_s[:, qt * 32 + hh:qt * 32 + hh + 1],
                            ALU.mult, ALU.mult)),
                            reads=[pfb_, lsum_b, gates_b], writes=[local_b],
                            tag="a1 lev")
                        if qt == NT - 1:
                            ctx._a1_prev_av = ("PEs", p.cnt["PEs"])

    if dbg_aps is not None:
        d_local, d_rem, d_gates, d_pooled = dbg_aps[:4]
        p.dma("gpsimd", d_local, local_s[:, :], reads=[local_b], tag="dbg")
        p.dma("gpsimd", d_rem, rem_s[:, :], reads=[rem_b], tag="dbg")
        p.dma("gpsimd", d_gates, gates_s[:, :], reads=[gates_b], tag="dbg")
        p.dma("gpsimd", d_pooled, pooledT_s[:, :], reads=[pooledT_b], tag="dbg")

    with ctx.scope():
        remT_s, remT_b = ctx.sbuf("a1_remT", [128, 4 * TOK], BF16)
        rw_s, rw_b = ctx.sbuf("a1_rw", [128, 4 * D], BF16)
        wo_s, wo_b = ctx.sbuf("a1_wo", [128, NCHUNK * D], BF16)
        fused_s, fused_b = zip(*[ctx.sbuf("a1_fsd%d" % i, [128, D], BF16) for i in range(2)])
        ftmp_s, ftmp_b = ctx.sbuf("a1_ftmp", [128, 512], BF16)
        fusedT_s, fusedT_b = zip(*[ctx.sbuf("a1_fsdT%d" % i, [128, NCHUNK * 128], BF16) for i in range(2)])
        xf_s, xf_b = zip(*[ctx.sbuf("a1_xf%d" % i, [128, D], F32) for i in range(2)])
        statf_s, statf_b = zip(*[ctx.sbuf("a1_stf%d" % i, [128, 8], F32) for i in range(2)])
        lny_s, lny_b = zip(*[ctx.sbuf("a1_lny%d" % i, [128, D], BF16) for i in range(2)])
        xo_s, xo_b = zip(*[ctx.sbuf("a1_xo%d" % i, [128, D], F32) for i in range(2)])

        for c in range(4):
            p.dma("sync", rw_s[:, ts(c, D)], r_out_w[ts(c, 128), :],
                  writes=[rw_b], tag="a1 rw")
        for c in range(NCHUNK):
            p.dma("sync", wo_s[:, ts(c, D)], w_out[ts(c, 128), :],
                  writes=[wo_b], tag="a1 wo")

        for t in range(0, NT, 2):
            pt, ptb = (PT, PTb) if (t // 2) % 2 == 0 else (PT2, PT2b)
            p.emit_group("tensor", [
                (lambda h, t=t, i=i, pt=pt: h.transpose(
                    pt[:, ts(i, 128)],
                    rem_s[:, (t + i // 4) * 512 + (i % 4) * 128:(t + i // 4) * 512 + (i % 4 + 1) * 128],
                    ident[:, :]))
                for i in range(8)],
                reads=[rem_b, identb], writes=[ptb], tag="a1 remT")
            p.emit("vector", (lambda h, t=t, pt=pt: h.tensor_copy(
                remT_s.rearrange("p (c q) -> p c q", c=4)[:, :, t * 128:(t + 2) * 128].rearrange("p c (u q) -> p c u q", u=2),
                pt.rearrange("p (u c q) -> p c u q", u=2, c=4))),
                reads=[ptb], writes=[remT_b], tag="a1 remT ev")

        def a1fin_A1(t):
            for n in range(2):
                pa, pab = PS[2 * (t % 2) + n]
                p.emit_group("tensor", [
                    (lambda h, t=t, n=n, c=c, pa=pa: h.matmul(
                        pa[:, :], remT_s[:, c * TOK + t * 128:c * TOK + (t + 1) * 128],
                        rw_s[:, c * D + n * 512:c * D + (n + 1) * 512],
                        start=(c == 0), stop=(c == 3)))
                    for c in range(4)],
                    reads=[remT_b, rw_b], writes=[pab], tag="a1 ro mm")

        def a1fin_A2(t):
            par = t % 2
            for n in range(2):
                pa, pab = PS[2 * (t % 2) + n]
                p.emit("vector", (lambda h, t=t, n=n, pa=pa: h.tensor_tensor(
                    ftmp_s.rearrange("p (e q) -> p e q", e=8),
                    pa[:, :].rearrange("p (e q) -> p e q", e=8),
                    gates_s[:, t * 32 + 16 + n * 8:t * 32 + 24 + n * 8]
                        .rearrange("p (e o) -> p e o", e=8)
                        .broadcast_to((128, 8, 64)),
                    op=ALU.mult)),
                    reads=[pab, gates_b], writes=[ftmp_b], tag="a1 fuse1")
                p.emit("vector", (lambda h, t=t, n=n, par=par: h.tensor_tensor(
                    fused_s[par][:, n * 512:(n + 1) * 512],
                    ftmp_s[:, :],
                    local_s[:, t * D + n * 512:t * D + (n + 1) * 512],
                    op=ALU.add)),
                    reads=[ftmp_b, local_b], writes=[fused_b[par]],
                    tag="a1 fuse2")

        def a1fin_B1h(t):
            par = t % 2
            pt, ptb = (PT, PTb) if par == 0 else (PT2, PT2b)
            p.emit_group("tensor", [
                (lambda h, c=c, par=par, pt=pt: h.transpose(
                    pt[:, ts(c, 128)], fused_s[par][:, ts(c, 128)], ident[:, :]))
                for c in range(NCHUNK)],
                reads=[fused_b[par], identb], writes=[ptb], tag="a1 fT")
            p.emit("vector", (lambda h, par=par, pt=pt: h.tensor_copy(
                fusedT_s[par][:, :], pt[:, :])),
                reads=[ptb], writes=[fusedT_b[par]], tag="a1 fT ev")

        def a1fin_B1t(t):
            par = t % 2
            for n in range(2):
                pb_, pbb_ = PS[4 + n]
                p.emit_group("tensor", [
                    (lambda h, n=n, c=c, par=par, pb_=pb_: h.matmul(
                        pb_[:, :], fusedT_s[par][:, ts(c, 128)],
                        wo_s[:, c * D + n * 512:c * D + (n + 1) * 512],
                        start=(c == 0), stop=(c == NCHUNK - 1)))
                    for c in range(NCHUNK)],
                    reads=[fusedT_b[par], wo_b], writes=[pbb_], tag="a1 y mm")
                p.emit("vector", (lambda h, n=n, par=par, pb_=pb_: h.tensor_copy(
                    xo_s[par][:, ts(n, 512)], pb_[:, :])),
                    reads=[pbb_], writes=[xo_b[par]], tag="a1 y ev")
            ln_stats_acc(ctx, xo_s[par][:, :], xo_b[par], statf_s[par],
                         statf_b[par], pref="a1y ")

        def a1fin_B2(t):
            par = t % 2
            ln_stats_fin(ctx, statf_s[par], statf_b[par], pref="a1y ")
            p.emit("vector", (lambda h, par=par: h.tensor_scalar(
                lny_s[par][:, :], xo_s[par][:, :], statf_s[par][:, 4:5],
                statf_s[par][:, 5:6], ALU.mult, ALU.add)),
                reads=[xo_b[par], statf_b[par]], writes=[lny_b[par]],
                tag="a1 lny")
            p.dma("sync", xf_s[par][:, :], x_in[ts(t, 128), :],
                  writes=[xf_b[par]], tag="a1 xre")
            p.emit("vector", (lambda h, par=par: h.scalar_tensor_tensor(
                xo_s[par][:, :], lny_s[par][:, :], 1.0, xf_s[par][:, :],
                ALU.mult, ALU.add)),
                reads=[lny_b[par], xf_b[par]], writes=[xo_b[par]],
                tag="a1 resid")
            ctx.store_toks.append(p.dma(
                "scalar", x_out[ts(t, 128), :], xo_s[par][:, :],
                reads=[xo_b[par]], tag="a1 xstore"))
            grid_acc(ctx, xo_s[par][:, :], xo_b[par], t, "a1")
            if ctx.on_store:
                ctx.on_store(len(ctx.store_toks) - 1, ctx.store_toks[-1])

        a1fin_A1(0)
        a1fin_A2(0)
        for t in range(NT):
            if t + 1 < NT:
                a1fin_A1(t + 1)
            a1fin_B1h(t)
            if t + 1 < NT:
                a1fin_A2(t + 1)
            if t > 0:
                a1fin_B2(t - 1)
            a1fin_B1t(t)
        a1fin_B2(NT - 1)
            


def build_program_l1attn(dbg=False):
    nc = bass.Bass("TRN2", target_bir_lowering=False, debug=False)
    x_in = nc.declare_dram_parameter("x_in", [TOK, D], F32, isOutput=False)
    xhalo = nc.declare_dram_parameter("xhalo", [256, D], F32, isOutput=False)
    xother = nc.declare_dram_parameter("xother", [TOK, D], F32, isOutput=False)
    wq = nc.declare_dram_parameter("wqT", [D, D], F32, isOutput=False)
    wk = nc.declare_dram_parameter("wkT", [D, D], F32, isOutput=False)
    wv = nc.declare_dram_parameter("wvT", [D, D], F32, isOutput=False)
    wgq = nc.declare_dram_parameter("wgqT", [D, 512], F32, isOutput=False)
    wgk = nc.declare_dram_parameter("wgkT", [D, 512], F32, isOutput=False)
    wgv = nc.declare_dram_parameter("wgvT", [D, 512], F32, isOutput=False)
    r_out_w = nc.declare_dram_parameter("r_out_w", [512, D], F32, isOutput=False)
    glgrw = nc.declare_dram_parameter("glgrw", [D, 4], F32, isOutput=False)
    w_out = nc.declare_dram_parameter("w_outT", [D, D], F32, isOutput=False)
    locmask = nc.declare_dram_parameter("locmask", [128, 384], F32, isOutput=False)
    halomask = nc.declare_dram_parameter("halomask", [128, 384], F32, isOutput=False)
    lmask = nc.declare_dram_parameter("lmask", [64, TOK], F32, isOutput=False)
    ident = nc.declare_dram_parameter("ident", [128, 128], BF16, isOutput=False)
    poolw = nc.declare_dram_parameter("poolw", [128, 2], F32, isOutput=False)
    x_out = nc.declare_dram_parameter("x_out", [TOK, D], F32, isOutput=True)

    es = contextlib.ExitStack()
    with es:
        prog = Prog(nc)
        ctx = Ctx(nc, prog, es)
        ctx.consts = {"poolw": poolw.ap()}
        ctx.ident_s, ctx.ident_b = ctx.sbuf("ident_s", [128, 128], BF16)
        tr, trb = ctx.sbuf("trash", [128, D], BF16)
        ctx.trash_s = tr
        ctx.trashb = trb
        prog.dma("sync", ctx.ident_s[:, :], ident.ap(), writes=[ctx.ident_b])
        dbg_aps = None
        if dbg:
            d_local = nc.declare_dram_parameter("d_local", [128, NT * D], F32, isOutput=True)
            d_rem = nc.declare_dram_parameter("d_rem", [128, NT * 512], F32, isOutput=True)
            d_gates = nc.declare_dram_parameter("d_gates", [128, NT * 32], F32, isOutput=True)
            d_pooled = nc.declare_dram_parameter("d_pooled", [128, NCHUNK * 64], F32, isOutput=True)
            d_xnT = nc.declare_dram_parameter("d_xnT", [128, NCHUNK * KLO], F32, isOutput=True)
            d_x0 = nc.declare_dram_parameter("d_x0", [128, D], F32, isOutput=True)
            d_st0 = nc.declare_dram_parameter("d_st0", [128, 8], F32, isOutput=True)
            d_xn0 = nc.declare_dram_parameter("d_xn0", [128, D], F32, isOutput=True)
            dbg_aps = (d_local.ap(), d_rem.ap(), d_gates.ap(), d_pooled.ap(), d_xnT.ap(),
                       d_x0.ap(), d_st0.ap(), d_xn0.ap())
        build_l1attn(ctx, x_in.ap(), xhalo.ap(), xother.ap(), x_out.ap(),
                     wq.ap(), wk.ap(), wv.ap(), wgq.ap(), wgk.ap(), wgv.ap(),
                     r_out_w.ap(), glgrw.ap(), w_out.ap(), locmask.ap(),
                     halomask.ap(), lmask.ap(), dbg_aps=dbg_aps)
        prog.barrier()
        prog.realize(es)
    return nc




def build_l1ffn(ctx, x_in, xother, x_out, baseWT, base_b, muT, sigT, downWf,
                down_bf, upWf, up_bf, convTf, conv_bf, iota128, iota8,
                dbg_aps=None, cc=None, grid_ext=None):
    nc, p = ctx.nc, ctx.p
    ctx.store_toks = []
    ident = ctx.ident_s
    identb = ctx.ident_b

    xnT_s, xnT_b = ctx.sbuf("f1_xnT", [128, NCHUNK * TOK], BF16)
    part_s, part_b = ctx.sbuf("f1_part", [128, NCHUNK * 32 * 2], F32)
    part2_b = Buf("f1_part2")
    m12_s, m12_b = ctx.sbuf("f1_m12", [128, NCHUNK * 2], F32)
    musig_s, musig_b = ctx.sbuf("f1_musig", [128, NCHUNK * 8 * 4], F32)
    loss_s, loss_b = ctx.sbuf("f1_loss", [1, 32], F32)
    idx_s, idx_b = ctx.sbuf("f1_idx", [128, 4], I32)
    idxf_s, idxf_b = ctx.sbuf("f1_idxf", [1, 2], F32)
    idxbf_s, idxbf_b = ctx.sbuf("f1_idxbf", [1, 1], BF16)
    iota_s, iota_b = ctx.sbuf("f1_iota", [128, 1], I32)
    iota8_s, iota8_b = ctx.sbuf("f1_iota8", [1, 8], F32)

    bw_s, bw_b = ctx.sbuf("f1_bw", [128, NCHUNK * D], BF16)
    cw_s, cw_b = ctx.sbuf("f1_cw", [128, NCHUNK * D], BF16)
    dwf_s, dwf_b = ctx.sbuf("f1_dwf", [128, 64], F32)
    dw_s, dw_b = ctx.sbuf("f1_dw", [128, NCHUNK * 64], BF16)
    uwf_s, uwf_b = ctx.sbuf("f1_uwf", [64, D], F32)
    cwf_s, cwf_b = ctx.sbuf("f1_cwf", [128, D], F32)
    mm2_s, mm2_b = ctx.sbuf("f1_mm2", [128, 64], F32)
    uw_s, uw_b = ctx.sbuf("f1_uw", [64, D], BF16)
    brow_s, brow_b = ctx.sbuf("f1_brow", [1, 4 * D], BF16)
    browf_s, browf_b = ctx.sbuf("f1_browf", [1, D], F32)

    P6 = [ctx.psum("f1_P%d" % i, [128, 512], F32) for i in range(6)]
    PA, PAb = P6[0]
    PB, PBb = P6[1]
    PC, PCb = P6[2]
    PD, PDb = P6[3]
    PT, PTb = ctx.psum("f1_PT", [128, 1024], BF16)
    PT2, PT2b = ctx.psum("f1_PT2", [128, 1024], BF16)

    p.dma("sync", iota_s[:, :], iota128, writes=[iota_b], tag="f1 iota")
    p.dma("sync", iota8_s[:, :], iota8, writes=[iota8_b], tag="f1 iota8")
    for c in range(NCHUNK):
        p.dma("sync", bw_s[:, ts(c, D)], baseWT[ts(c, 128), :], writes=[bw_b],
              tag="f1 bw")
        p.dma("sync", musig_s[:, c * 32:c * 32 + 8], muT[ts(c, 128), :],
              writes=[musig_b], tag="f1 mu")
        p.dma("sync", musig_s[:, c * 32 + 8:c * 32 + 16], sigT[ts(c, 128), :],
              writes=[musig_b], tag="f1 sig")

    nparts = 32
    with ctx.scope():
        x1_s, x1_b = zip(*[ctx.sbuf("f1_x1%d" % i, [128, D], F32) for i in range(2)])
        st1_s, st1_b = zip(*[ctx.sbuf("f1_st1%d" % i, [128, 8], F32) for i in range(2)])
        xn1_s, xn1_b = zip(*[ctx.sbuf("f1_xn1%d" % i, [128, D], BF16) for i in range(2)])
        sq1_s, sq1_b = zip(*[ctx.sbuf("f1_sq1%d" % i, [128, D], BF16) for i in range(2)])
        pc1_s, pc1_b = zip(*[ctx.sbuf("f1_pc1%d" % i, [128, D], BF16) for i in range(2)])

        if grid_ext is None:
            fgrid_s, fgrid_b = ctx.sbuf("f1_grid", [128, 192], F32)
            if cc is not None:
                ln_prepass(ctx, [(x_in[ts(t, 128), :], None) for t in range(NT)],
                           fgrid_s, fgrid_b, [x1_s[0][:, :], x1_s[1][:, :]],
                           [x1_b[0], x1_b[1]], pref="f1 ")
        else:
            fgrid_s, fgrid_b = grid_ext
            ln_prepass_finish(ctx, NT, fgrid_s, fgrid_b, fgrid_b, pref="f1 ")

        xnTc_b = [Buf("f1xnTc%d" % c) for c in range(NCHUNK)]
        onesc_s, onesc_b = ctx.sbuf("f1_onesc", [128, 1], BF16)
        p.emit("vector", lambda h: h.memset(onesc_s[:, :], 1.0),
               writes=[onesc_b], tag="f1 onesc")

        def stat_head(par, src_slice, tag, gcol=None):
            pt, ptb = (PT, PTb) if par == 0 else (PT2, PT2b)
            p.dma("sync", x1_s[par][:, :], src_slice, writes=[x1_b[par]],
                  tag=tag + " x")
            if gcol is None:
                ln_stats(ctx, x1_s[par][:, :], x1_b[par], st1_s[par],
                         st1_b[par], pref=tag + " ")
                sc1 = st1_s[par][:, 4:5]
                sc2 = st1_s[par][:, 5:6]
                rb = [st1_b[par]]
            else:
                sc1 = fgrid_s[:, 128 + gcol:129 + gcol]
                sc2 = fgrid_s[:, 160 + gcol:161 + gcol]
                rb = [fgrid_b]
            p.emit("vector", (lambda h, par=par, sc1=sc1, sc2=sc2:
                              h.tensor_scalar(
                xn1_s[par][:, :], x1_s[par][:, :], sc1, sc2,
                ALU.mult, ALU.add)),
                reads=[x1_b[par]] + rb, writes=[xn1_b[par]],
                tag=tag + " xn")
            p.emit_group("tensor", [
                (lambda h, c=c, par=par, pt=pt: h.transpose(
                    pt[:, ts(c, 128)], xn1_s[par][:, ts(c, 128)], ident[:, :]))
                for c in range(NCHUNK)],
                reads=[xn1_b[par], identb], writes=[ptb], tag=tag + " T")

        def stat_tail(par, col, xdst, tag):
            pt, ptb = (PT, PTb) if par == 0 else (PT2, PT2b)
            if xdst is not None and cc is not None:
                t = col
                p.emit("vector", (lambda h, xdst=xdst, pt=pt: h.tensor_copy(
                    xnT_s.rearrange("p (c q) -> p c q", c=NCHUNK)[:, :, xdst:xdst + 128],
                    pt.rearrange("p (c q) -> p c q", c=NCHUNK)[:, :, :])),
                    reads=[ptb], writes=[xnTc_b[0]], tag=tag + " xnT ev")
                p.emit("vector", (lambda h, par=par: h.tensor_tensor(
                    sq1_s[par][:, :], xn1_s[par][:, :], xn1_s[par][:, :],
                    op=ALU.mult)),
                    reads=[xn1_b[par]], writes=[sq1_b[par]], tag=tag + " xsq")
                for half in range(2):
                    p.emit("tensor", (lambda h, t=t, half=half, par=par:
                                      h.matmul(
                        P6[half][0][0:1, 0:512], onesc_s[:, 0:1],
                        xn1_s[par][:, half * 512:(half + 1) * 512],
                        start=(t == 0), stop=(t == NT - 1))),
                        reads=[onesc_b, xn1_b[par]], writes=[P6[half][1]],
                        tag=tag + " m1mm")
                    p.emit("tensor", (lambda h, t=t, half=half, par=par:
                                      h.matmul(
                        P6[2 + half][0][0:1, 0:512], onesc_s[:, 0:1],
                        sq1_s[par][:, half * 512:(half + 1) * 512],
                        start=(t == 0), stop=(t == NT - 1))),
                        reads=[onesc_b, sq1_b[par]], writes=[P6[2 + half][1]],
                        tag=tag + " m2mm")
                return
            for c in range(NCHUNK):
                dst = (pc1_s[par][:, ts(c, 128)] if xdst is None else
                       xnT_s[:, c * TOK + xdst:c * TOK + xdst + 128])
                dstb = pc1_b[par] if xdst is None else xnTc_b[c]
                p.emit("vector", (lambda h, c=c, dst=dst, col=col, pt=pt:
                                  h.tensor_scalar(
                    dst, pt[:, ts(c, 128)], 1.0, 0.0, ALU.mult, ALU.add,
                    accum_out=part_s[:, c * 64 + col:c * 64 + col + 1])),
                    reads=[ptb], writes=[dstb, part_b], tag=tag + " m1")
                p.emit("scalar", (lambda h, c=c, col=col, par=par, dst=dst:
                                  h.activation(
                    sq1_s[par][:, ts(c, 128)], dst, AF.Square,
                    accum_out=part_s[:, c * 64 + 32 + col:c * 64 + 32 + col + 1])),
                    reads=[dstb], writes=[sq1_b[par], part2_b], tag=tag + " m2")

        stat_head(0, x_in[ts(0, 128), :], "f1",
                  gcol=(0 if cc is not None else None))
        for t in range(NT):
            if t + 1 < NT:
                stat_head((t + 1) % 2, x_in[ts(t + 1, 128), :], "f1",
                          gcol=(t + 1 if cc is not None else None))
            stat_tail(t % 2, t, t * 128, "f1")
        if cc is None:
            for t in range(NT):
                stat_head(t % 2, xother[ts(t, 128), :], "f1o",
                          gcol=None)
                stat_tail(t % 2, NT + t, None, "f1o")
        else:
            m12row_s, m12row_b = ctx.sbuf("f1_m12row", [1, 2 * D], F32)
            for half in range(2):
                p.emit("vector", (lambda h, half=half: h.tensor_copy(
                    m12row_s[0:1, half * 512:(half + 1) * 512],
                    P6[half][0][0:1, 0:512])),
                    reads=[P6[half][1]], writes=[m12row_b], tag="f1 m1row")
                p.emit("vector", (lambda h, half=half: h.tensor_copy(
                    m12row_s[0:1, D + half * 512:D + (half + 1) * 512],
                    P6[2 + half][0][0:1, 0:512])),
                    reads=[P6[2 + half][1]], writes=[m12row_b], tag="f1 m2row")
            builds = []
            for c in range(NCHUNK):
                for k in range(2):
                    builds.append((lambda h, c=c, k=k: h.transpose(
                        P6[4][0][:, 2 * c + k:2 * c + k + 1],
                        m12row_s[0:1, k * D + c * 128:k * D + (c + 1) * 128],
                        ctx.identf_s[0:1, 0:1])))
            p.emit_group("tensor", builds, reads=[m12row_b, ctx.identf_b],
                         writes=[P6[4][1]], tag="f1 m12T")
            p.emit("vector", lambda h: h.tensor_copy(
                m12_s[:, :], P6[4][0][:, 0:2 * NCHUNK]),
                reads=[P6[4][1]], writes=[m12_b], tag="f1 m12 ev")
        nparts = 32 if cc is None else 16

    if cc is None:
        for c in range(NCHUNK):
            p.emit("vector", (lambda h, c=c: h.reduce_sum(
                m12_s[:, 2 * c:2 * c + 1], part_s[:, c * 64:c * 64 + nparts],
                axis=AX.X)),
                reads=[part_b], writes=[m12_b], tag="f1 m1sum")
            p.emit("vector", (lambda h, c=c: h.reduce_sum(
                m12_s[:, 2 * c + 1:2 * c + 2], part_s[:, c * 64 + 32:c * 64 + 32 + nparts],
                axis=AX.X)), reads=[part2_b], writes=[m12_b], tag="f1 m2sum")
    if cc is not None:
        p.dma("gpsimd", cc["m12_in"], m12_s[:, :], reads=[m12_b],
              writes=[cc["m12_in_b"]], tag="cc m12w")
        p.collective("AllReduce", ALU.add, cc["groups"],
                     [cc["m12_in"]], [cc["m12_out"]],
                     reads=[cc["m12_in_b"]], writes=[cc["m12_out_b"]],
                     sem_name=cc["sem_m12"], tag="cc m12")
        p.dma("sync", m12_s[:, :], cc["m12_out"],
              waits=[(cc["sem_m12"], 1)], writes=[m12_b], tag="cc m12r")
    inv4096 = 1.0 / 4096.0
    p.emit("vector", lambda h: h.tensor_scalar(
        m12_s[:, :], m12_s[:, :], inv4096, None, ALU.mult),
        reads=[m12_b], writes=[m12_b], tag="f1 m12")
    for c in range(NCHUNK):
        p.emit("vector", (lambda h, c=c: h.tensor_tensor(
            musig_s[:, c * 32 + 16:c * 32 + 24], musig_s[:, c * 32 + 8:c * 32 + 16],
            musig_s[:, c * 32 + 8:c * 32 + 16], op=ALU.mult)),
            reads=[musig_b], writes=[musig_b], tag="f1 sig2")
        p.emit("vector", (lambda h, c=c: h.reciprocal(
            musig_s[:, c * 32 + 16:c * 32 + 24], musig_s[:, c * 32 + 16:c * 32 + 24])),
            reads=[musig_b], writes=[musig_b], tag="f1 iv")
        p.emit("vector", (lambda h, c=c: h.tensor_tensor(
            musig_s[:, c * 32 + 24:c * 32 + 32], musig_s[:, c * 32:c * 32 + 8],
            musig_s[:, c * 32 + 16:c * 32 + 24], op=ALU.mult)),
            reads=[musig_b], writes=[musig_b], tag="f1 muiv")
    for (name, lcol, rofs, dst) in (("A", 1, 16, 0), ("B", 0, 24, 8)):
        builds = [
            (lambda h, c=c, lcol=lcol, rofs=rofs, dst=dst: h.matmul(
                PA[0:1, dst:dst + 8], m12_s[:, 2 * c + lcol:2 * c + lcol + 1],
                musig_s[:, c * 32 + rofs:c * 32 + rofs + 8],
                start=(c == 0), stop=(c == NCHUNK - 1)))
            for c in range(NCHUNK)]
        p.emit_group("tensor", builds, reads=[m12_b, musig_b], writes=[PAb],
                     tag="f1 loss mm")
    p.emit("vector", lambda h: h.tensor_tensor(
        mm2_s[:, 0:64],
        musig_s.rearrange("p (c k) -> p c k", c=NCHUNK)[:, :, 0:8],
        musig_s.rearrange("p (c k) -> p c k", c=NCHUNK)[:, :, 24:32], op=ALU.mult),
        reads=[musig_b], writes=[mm2_b], tag="f1 mumuiv")
    builds = [
        (lambda h, c=c: h.matmul(
            PA[0:1, 16:24], ctx.onesf_s[:, 0:1],
            mm2_s[:, c * 8:(c + 1) * 8],
            start=(c == 0), stop=(c == NCHUNK - 1)))
        for c in range(NCHUNK)]
    p.emit_group("tensor", builds, reads=[mm2_b, ctx.onesf_b], writes=[PAb],
                 tag="f1 C mm")
    p.emit("vector", lambda h: h.tensor_copy(loss_s[0:1, 0:24], PA[0:1, 0:24]),
           reads=[PAb], writes=[loss_b], tag="f1 loss ev")
    p.emit("vector", lambda h: h.scalar_tensor_tensor(
        loss_s[0:1, 8:16], loss_s[0:1, 8:16], -2.0, loss_s[0:1, 0:8],
        ALU.mult, ALU.add), reads=[loss_b], writes=[loss_b], tag="f1 lossc1")
    p.emit("vector", lambda h: h.tensor_tensor(
        loss_s[0:1, 0:8], loss_s[0:1, 8:16], loss_s[0:1, 16:24], op=ALU.add),
        reads=[loss_b], writes=[loss_b], tag="f1 lossc2")
    p.emit("vector", lambda h: h.tensor_reduce(
        loss_s[0:1, 24:25], loss_s[0:1, 0:8], axis=AX.X, op=ALU.min),
        reads=[loss_b], writes=[loss_b], tag="f1 lmin")
    p.emit("vector", lambda h: h.tensor_scalar(
        loss_s[0:1, 8:16], loss_s[0:1, 0:8], loss_s[0:1, 24:25], None,
        ALU.is_equal), reads=[loss_b], writes=[loss_b], tag="f1 lmask")
    p.emit("vector", lambda h: h.tensor_tensor(
        loss_s[0:1, 8:16], loss_s[0:1, 8:16], iota8_s[0:1, :], op=ALU.mult),
        reads=[loss_b, iota8_b], writes=[loss_b], tag="f1 lidx")
    p.emit("vector", lambda h: h.reduce_sum(
        idxf_s[0:1, 0:1], loss_s[0:1, 8:16], axis=AX.X),
        reads=[loss_b], writes=[idxf_b], tag="f1 idxf")
    p.emit("vector", lambda h: h.tensor_copy(idxbf_s[0:1, :], idxf_s[0:1, 0:1]),
           reads=[idxf_b], writes=[idxbf_b], tag="f1 idxbf")
    p.emit("tensor", lambda h: h.matmul(PA[:, 0:1], ctx.ones1_s[0:1, :],
                                        idxbf_s[0:1, :], start=True, stop=True),
           reads=[ctx.ones1_b, idxbf_b], writes=[PAb], tag="f1 idx mm")
    p.emit("vector", lambda h: h.tensor_copy(idx_s[:, 0:1], PA[:, 0:1]),
           reads=[PAb], writes=[idx_b], tag="f1 idx i32")
    p.emit("vector", lambda h: h.tensor_scalar(
        idx_s[:, 1:2], idx_s[:, 0:1], 1024, None, ALU.mult),
        reads=[idx_b], writes=[idx_b], tag="f1 idxm")
    p.emit("vector", lambda h: h.tensor_tensor(
        idx_s[:, 1:2], idx_s[:, 1:2], iota_s[:, :], op=ALU.add),
        reads=[idx_b, iota_b], writes=[idx_b], tag="f1 idxa")
    p.emit("vector", lambda h: h.tensor_scalar(
        idx_s[:, 2:3], idx_s[:, 0:1], 64, None, ALU.mult),
        reads=[idx_b], writes=[idx_b], tag="f1 idxu")
    p.emit("vector", lambda h: h.tensor_tensor(
        idx_s[:, 2:3], idx_s[:, 2:3], iota_s[:, :], op=ALU.add),
        reads=[idx_b, iota_b], writes=[idx_b], tag="f1 idxua")

    def gather(dst, dstb, table, idxap, rows, cols, tag, chunk_off=0):
        p.emit("gpsimd", (lambda h: h.indirect_dma_start(
            out=dst, out_offset=None, in_=table,
            in_offset=IndirectOffsetOnAxis(ap=idxap, axis=0))),
            reads=[idx_b], writes=[dstb], sem=DMA_SEM["gpsimd"], tag=tag)

    for c in range(NCHUNK):
        if c > 0:
            p.emit("vector", (lambda h, c=c: h.tensor_scalar(
                idx_s[:, 3:4], idx_s[:, 1:2], 128 * c, None, ALU.add)),
                reads=[idx_b], writes=[idx_b], tag="f1 idxc")
        src_idx = idx_s[:, 1:2] if c == 0 else idx_s[:, 3:4]
        p.dma_custom("gpsimd", (lambda h, c=c, src_idx=src_idx: h.indirect_dma_start(
            out=dwf_s[:, :], out_offset=None, in_=downWf,
            in_offset=IndirectOffsetOnAxis(ap=src_idx, axis=0))),
            reads=[idx_b], writes=[dwf_b], tag="f1 gdw")
        p.emit("vector", (lambda h, c=c: h.tensor_copy(
            dw_s[:, ts(c, 64)], dwf_s[:, :])),
            reads=[dwf_b], writes=[dw_b], tag="f1 dw bf")
        p.dma_custom("gpsimd", (lambda h, c=c, src_idx=src_idx: h.indirect_dma_start(
            out=cwf_s[:, :], out_offset=None, in_=convTf,
            in_offset=IndirectOffsetOnAxis(ap=src_idx, axis=0))),
            reads=[idx_b], writes=[cwf_b], tag="f1 gcw")
        p.emit("vector", (lambda h, c=c: h.tensor_copy(
            cw_s[:, ts(c, D)], cwf_s[:, :])),
            reads=[cwf_b], writes=[cw_b], tag="f1 cw bf")
    p.dma_custom("gpsimd", lambda h: h.indirect_dma_start(
        out=uwf_s[0:64, :], out_offset=None, in_=upWf,
        in_offset=IndirectOffsetOnAxis(ap=idx_s[0:64, 2:3], axis=0)),
        reads=[idx_b], writes=[uwf_b], tag="f1 guw")
    p.emit("vector", lambda h: h.tensor_copy(uw_s[0:64, :], uwf_s[0:64, :]),
           reads=[uwf_b], writes=[uw_b], tag="f1 uw bf")
    p.dma("gpsimd", brow_s[0:1, 0:D], base_b, writes=[brow_b], tag="f1 bb")
    p.dma_custom("gpsimd", lambda h: h.indirect_dma_start(
        out=cwf_s[:, :], out_offset=None, in_=conv_bf,
        in_offset=IndirectOffsetOnAxis(ap=idx_s[:, 0:1], axis=0)),
        reads=[idx_b], writes=[cwf_b], tag="f1 gcb")
    p.emit("vector", lambda h: h.tensor_copy(brow_s[0:1, D:2 * D], cwf_s[0:1, :]),
           reads=[cwf_b], writes=[brow_b], tag="f1 cb bf")
    p.dma_custom("gpsimd", lambda h: h.indirect_dma_start(
        out=cwf_s[:, :], out_offset=None, in_=up_bf,
        in_offset=IndirectOffsetOnAxis(ap=idx_s[:, 0:1], axis=0)),
        reads=[idx_b, cwf_b], writes=[cwf_b], tag="f1 gub")
    p.emit("vector", lambda h: h.tensor_copy(brow_s[0:1, 2 * D:3 * D], cwf_s[0:1, :]),
           reads=[cwf_b], writes=[brow_b], tag="f1 ub bf")
    p.dma_custom("gpsimd", lambda h: h.indirect_dma_start(
        out=cwf_s[:, 0:64], out_offset=None, in_=down_bf,
        in_offset=IndirectOffsetOnAxis(ap=idx_s[:, 0:1], axis=0)),
        reads=[idx_b, cwf_b], writes=[cwf_b], tag="f1 gdb")
    p.emit("vector", lambda h: h.tensor_copy(brow_s[0:1, 3 * D:3 * D + 64],
                                             cwf_s[0:1, 0:64]),
           reads=[cwf_b], writes=[brow_b], tag="f1 db bf")

    if dbg_aps is not None:
        p.dma("gpsimd", dbg_aps[0], m12_s[:, :], reads=[m12_b], tag="dbg")
        p.dma("gpsimd", dbg_aps[1], loss_s[:, :], reads=[loss_b], tag="dbg")
        p.dma("gpsimd", dbg_aps[2], idx_s[:, :], reads=[idx_b], tag="dbg")
        p.dma("gpsimd", dbg_aps[3], brow_s[:, :], reads=[brow_b], tag="dbg")
        p.dma("gpsimd", dbg_aps[4], dw_s[:, :], reads=[dw_b], tag="dbg")

    with ctx.scope():
        h_s, h_b = zip(*[ctx.sbuf("f1_h%d" % i, [128, 64], BF16) for i in range(2)])
        hT_s, hT_b = zip(*[ctx.sbuf("f1_hT%d" % i, [64, 128], BF16) for i in range(2)])
        aT_s, aT_b = zip(*[ctx.sbuf("f1_aT%d" % i, [128, NCHUNK * 128], BF16) for i in range(2)])
        x3_s, x3_b = zip(*[ctx.sbuf("f1_x3%d" % i, [128, D], F32) for i in range(2)])
        st3_s, st3_b = zip(*[ctx.sbuf("f1_st3%d" % i, [128, 8], F32) for i in range(2)])
        lny3_s, lny3_b = zip(*[ctx.sbuf("f1_lny3%d" % i, [128, D], BF16) for i in range(2)])
        xo3_s, xo3_b = zip(*[ctx.sbuf("f1_xo3%d" % i, [128, D], F32) for i in range(2)])

        def f3_A(t):
            par = t % 2
            pC, pCb = P6[3 * par + 2]
            pt, ptb = (PT, PTb) if par == 0 else (PT2, PT2b)
            builds = [
                (lambda h, t=t, c=c, pC=pC: h.matmul(
                    pC[:, 0:64], xnT_s[:, c * TOK + t * 128:c * TOK + (t + 1) * 128],
                    dw_s[:, ts(c, 64)], start=(c == 0), stop=False))
                for c in range(NCHUNK)]
            builds.append(lambda h, pC=pC: h.matmul(
                pC[:, 0:64], ctx.ones1_s[0:1, :], brow_s[0:1, 3 * D:3 * D + 64],
                start=False, stop=True))
            p.emit_group("tensor", builds,
                         reads=[xnT_b, dw_b, ctx.ones1_b, brow_b], writes=[pCb],
                         tag="f1 dmm")
            p.emit("scalar", (lambda h, par=par, pC=pC: h.activation(
                h_s[par][:, :], pC[:, 0:64], AF.Relu)),
                reads=[pCb], writes=[h_b[par]], tag="f1 relu")
            p.emit("tensor", (lambda h, par=par, pt=pt: h.transpose(
                pt[0:64, 0:128], h_s[par][:, :], ident[:, :])),
                reads=[h_b[par], identb], writes=[ptb], tag="f1 hT")
            p.emit("vector", (lambda h, par=par, pt=pt: h.tensor_copy(
                hT_s[par][0:64, :], pt[0:64, 0:128])),
                reads=[ptb], writes=[hT_b[par]], tag="f1 hT ev")

        def f3_B1(t):
            par = t % 2
            pA, pAb = P6[3 * par]
            pB, pBb = P6[3 * par + 1]
            for c in range(NCHUNK):
                pD_, pDb_ = (pA, pAb) if c % 2 == 0 else (pB, pBb)
                p.emit_group("tensor", [
                    (lambda h, c=c, par=par, pD_=pD_: h.matmul(
                        pD_[:, 0:128], uw_s[0:64, ts(c, 128)], hT_s[par][0:64, :],
                        start=True, stop=False)),
                    (lambda h, c=c, pD_=pD_: h.matmul(
                        pD_[:, 0:128], brow_s[0:1, 2 * D + c * 128:2 * D + (c + 1) * 128],
                        ctx.ones1_s[0:1, :], start=False, stop=True))],
                    reads=[uw_b, hT_b[par], brow_b, ctx.ones1_b], writes=[pDb_],
                    tag="f1 umm")
                p.emit("vector", (lambda h, c=c, par=par, pD_=pD_: h.tensor_copy(
                    aT_s[par][:, ts(c, 128)], pD_[:, 0:128])),
                    reads=[pDb_], writes=[aT_b[par]], tag="f1 aT ev")

        def f3_B2(t):
            par = t % 2
            pA, pAb = P6[3 * par]
            pB, pBb = P6[3 * par + 1]
            for n in range(2):
                ps_, psb_ = (pA, pAb) if n == 0 else (pB, pBb)
                builds = [
                    (lambda h, t=t, n=n, c=c, ps_=ps_: h.matmul(
                        ps_[:, :], xnT_s[:, c * TOK + t * 128:c * TOK + (t + 1) * 128],
                        bw_s[:, c * D + n * 512:c * D + (n + 1) * 512],
                        start=(c == 0), stop=False))
                    for c in range(NCHUNK)]
                builds += [
                    (lambda h, n=n, c=c, par=par, ps_=ps_: h.matmul(
                        ps_[:, :], aT_s[par][:, ts(c, 128)],
                        cw_s[:, c * D + n * 512:c * D + (n + 1) * 512],
                        start=False, stop=False))
                    for c in range(NCHUNK)]
                builds.append(lambda h, n=n, ps_=ps_: h.matmul(
                    ps_[:, :], ctx.ones1_s[0:1, :], brow_s[0:1, ts(n, 512)],
                    start=False, stop=False))
                builds.append(lambda h, n=n, ps_=ps_: h.matmul(
                    ps_[:, :], ctx.ones1_s[0:1, :],
                    brow_s[0:1, D + n * 512:D + (n + 1) * 512],
                    start=False, stop=True))
                p.emit_group("tensor", builds,
                             reads=[xnT_b, bw_b, aT_b[par], cw_b, ctx.ones1_b,
                                    brow_b],
                             writes=[psb_], tag="f1 ymm")
                p.emit("vector", (lambda h, n=n, par=par, ps_=ps_: h.tensor_copy(
                    xo3_s[par][:, ts(n, 512)], ps_[:, :])),
                    reads=[psb_], writes=[xo3_b[par]], tag="f1 y ev")
            ln_stats_acc(ctx, xo3_s[par][:, :], xo3_b[par], st3_s[par],
                         st3_b[par], pref="f1y ")

        def f3_C(t):
            par = t % 2
            ln_stats_fin(ctx, st3_s[par], st3_b[par], pref="f1y ")
            p.emit("vector", (lambda h, par=par: h.tensor_scalar(
                lny3_s[par][:, :], xo3_s[par][:, :], st3_s[par][:, 4:5],
                st3_s[par][:, 5:6], ALU.mult, ALU.add)),
                reads=[xo3_b[par], st3_b[par]], writes=[lny3_b[par]],
                tag="f1 lny")
            p.dma("sync", x3_s[par][:, :], x_in[ts(t, 128), :],
                  writes=[x3_b[par]], tag="f1 xre")
            p.emit("vector", (lambda h, par=par: h.scalar_tensor_tensor(
                xo3_s[par][:, :], lny3_s[par][:, :], 1.0, x3_s[par][:, :],
                ALU.mult, ALU.add)),
                reads=[lny3_b[par], x3_b[par]], writes=[xo3_b[par]],
                tag="f1 resid")
            p.dma("scalar", x_out[ts(t, 128), :], xo3_s[par][:, :],
                  reads=[xo3_b[par]], tag="f1 xstore")

        f3_A(0)
        for t in range(NT):
            if t + 1 < NT:
                f3_A(t + 1)
            f3_B1(t)
            if t > 0:
                f3_C(t - 1)
            f3_B2(t)
        f3_C(NT - 1)


def build_program_l1ffn(dbg=False):
    nc = bass.Bass("TRN2", target_bir_lowering=False, debug=False)
    x_in = nc.declare_dram_parameter("x_in", [TOK, D], F32, isOutput=False)
    xother = nc.declare_dram_parameter("xother", [TOK, D], F32, isOutput=False)
    baseWT = nc.declare_dram_parameter("baseWT", [D, D], F32, isOutput=False)
    base_b = nc.declare_dram_parameter("base_b", [1, D], F32, isOutput=False)
    muT = nc.declare_dram_parameter("muT", [D, 8], F32, isOutput=False)
    sigT = nc.declare_dram_parameter("sigT", [D, 8], F32, isOutput=False)
    downWf = nc.declare_dram_parameter("downWf", [8 * D, 64], F32, isOutput=False)
    down_bf = nc.declare_dram_parameter("down_bf", [8, 64], F32, isOutput=False)
    upWf = nc.declare_dram_parameter("upWf", [8 * 64, D], F32, isOutput=False)
    up_bf = nc.declare_dram_parameter("up_bf", [8, D], F32, isOutput=False)
    convTf = nc.declare_dram_parameter("convTf", [8 * D, D], F32, isOutput=False)
    conv_bf = nc.declare_dram_parameter("conv_bf", [8, D], F32, isOutput=False)
    iota128 = nc.declare_dram_parameter("iota128", [128, 1], I32, isOutput=False)
    iota8 = nc.declare_dram_parameter("iota8", [1, 8], F32, isOutput=False)
    ident = nc.declare_dram_parameter("ident", [128, 128], BF16, isOutput=False)
    ones1 = nc.declare_dram_parameter("ones1", [1, 128], BF16, isOutput=False)
    onesf = nc.declare_dram_parameter("onesf", [128, 1], F32, isOutput=False)
    x_out = nc.declare_dram_parameter("x_out", [TOK, D], F32, isOutput=True)

    es = contextlib.ExitStack()
    with es:
        prog = Prog(nc)
        ctx = Ctx(nc, prog, es)
        ctx.ident_s, ctx.ident_b = ctx.sbuf("ident_s", [128, 128], BF16)
        ctx.ones1_s, ctx.ones1_b = ctx.sbuf("ones1_s", [1, 128], BF16)
        ctx.onesf_s, ctx.onesf_b = ctx.sbuf("onesf_s", [128, 1], F32)
        tr, trb = ctx.sbuf("trash", [128, D], BF16)
        ctx.trash_s = tr
        ctx.trashb = trb
        prog.dma("sync", ctx.ident_s[:, :], ident.ap(), writes=[ctx.ident_b])
        prog.dma("sync", ctx.ones1_s[:, :], ones1.ap(), writes=[ctx.ones1_b])
        prog.dma("sync", ctx.onesf_s[:, :], onesf.ap(), writes=[ctx.onesf_b])
        dbg_aps = None
        if dbg:
            d_m12 = nc.declare_dram_parameter("d_m12", [128, NCHUNK * 2], F32, isOutput=True)
            d_loss = nc.declare_dram_parameter("d_loss", [1, 32], F32, isOutput=True)
            d_idx = nc.declare_dram_parameter("d_idx", [128, 4], I32, isOutput=True)
            d_brow = nc.declare_dram_parameter("d_brow", [1, 4 * D], F32, isOutput=True)
            d_dw = nc.declare_dram_parameter("d_dw", [128, NCHUNK * 64], F32, isOutput=True)
            d_h = nc.declare_dram_parameter("d_h", [128, 64], F32, isOutput=True)
            d_aT = nc.declare_dram_parameter("d_aT", [128, NCHUNK * 128], F32, isOutput=True)
            d_y = nc.declare_dram_parameter("d_y", [128, D], F32, isOutput=True)
            d_uw = nc.declare_dram_parameter("d_uw", [64, D], F32, isOutput=True)
            dbg_aps = (d_m12.ap(), d_loss.ap(), d_idx.ap(), d_brow.ap(), d_dw.ap(),
                       d_h.ap(), d_aT.ap(), d_y.ap(), d_uw.ap())
        build_l1ffn(ctx, x_in.ap(), xother.ap(), x_out.ap(), baseWT.ap(),
                    base_b.ap(), muT.ap(), sigT.ap(), downWf.ap(), down_bf.ap(),
                    upWf.ap(), up_bf.ap(), convTf.ap(), conv_bf.ap(),
                    iota128.ap(), iota8.ap(), dbg_aps=dbg_aps)
        prog.barrier()
        prog.realize(es)
    return nc




def make_consts():
    c = {}
    c["ident_bf16"] = np.eye(128, dtype=BFNP)
    c["ident_f32"] = np.eye(128, dtype=np.float32)
    c["ones1_bf16"] = np.ones((1, 128), dtype=BFNP)
    return c


def build_program_l0attn(half=0):
    nc = bass.Bass("TRN2", target_bir_lowering=False, debug=False)
    x_in = nc.declare_dram_parameter("x_in", [TOK, D], F32, isOutput=False)
    wq = nc.declare_dram_parameter("wqT", [D, D], F32, isOutput=False)
    wk = nc.declare_dram_parameter("wkT", [D, D], F32, isOutput=False)
    wv = nc.declare_dram_parameter("wvT", [D, D], F32, isOutput=False)
    wc = nc.declare_dram_parameter("wcT", [D, D], F32, isOutput=False)
    biases = nc.declare_dram_parameter("biases", [4, D], F32, isOutput=False)
    summ = nc.declare_dram_parameter("summ", [16, D], F32, isOutput=False)
    ident = nc.declare_dram_parameter("ident", [128, 128], BF16, isOutput=False)
    identf = nc.declare_dram_parameter("identf", [128, 128], F32, isOutput=False)
    ones1 = nc.declare_dram_parameter("ones1", [1, 128], BF16, isOutput=False)
    x_out = nc.declare_dram_parameter("x_out", [TOK, D], F32, isOutput=True)

    es = contextlib.ExitStack()
    with es:
        prog = Prog(nc)
        ctx = Ctx(nc, prog, es)
        ctx.ident_s, ctx.ident_b = ctx.sbuf("ident_s", [128, 128], BF16)
        ctx.identf_s, ctx.identf_b = ctx.sbuf("identf_s", [128, 128], F32)
        ctx.ones1_s, ctx.ones1_b = ctx.sbuf("ones1_s", [1, 128], BF16)
        tr, trb = ctx.sbuf("trash", [128, D], BF16)
        ctx.trash_s = tr
        ctx.trashb = trb
        prog.dma("sync", ctx.ident_s[:, :], ident.ap(), writes=[ctx.ident_b])
        prog.dma("sync", ctx.identf_s[:, :], identf.ap(), writes=[ctx.identf_b])
        prog.dma("sync", ctx.ones1_s[:, :], ones1.ap(), writes=[ctx.ones1_b])
        build_l0attn(ctx, x_in.ap(), x_out.ap(), wq.ap(), wk.ap(), wv.ap(),
                     wc.ap(), biases.ap(), summ.ap(), {"half": half})
        prog.barrier()
        prog.realize(es)
    return nc


def build_program_fused():
    nc = bass.Bass("TRN2", target_bir_lowering=False, debug=False)
    DP = nc.declare_dram_parameter
    x0 = DP("x0", [TOK, D], F32, isOutput=False)
    summ = DP("summ", [16, D], F32, isOutput=False)
    l0_wq = DP("l0_wqT", [D, D], F32, isOutput=False)
    l0_wk = DP("l0_wkT", [D, D], F32, isOutput=False)
    l0_wv = DP("l0_wvT", [D, D], F32, isOutput=False)
    l0_wc = DP("l0_wcT", [D, D], F32, isOutput=False)
    l0_biases = DP("l0_biases", [4, D], F32, isOutput=False)
    f0_wg = DP("f0_wgT", [D, HID], I32, isOutput=False)
    f0_wu = DP("f0_wuT", [D, HID], I32, isOutput=False)
    f0_wd = DP("f0_wdT", [HID, D], I32, isOutput=False)
    a1_wq = DP("a1_wqT", [D, D], F32, isOutput=False)
    a1_wk = DP("a1_wkT", [D, D], F32, isOutput=False)
    a1_wv = DP("a1_wvT", [D, D], F32, isOutput=False)
    a1_wgq = DP("a1_wgqT", [D, 512], F32, isOutput=False)
    a1_wgk = DP("a1_wgkT", [D, 512], F32, isOutput=False)
    a1_wgv = DP("a1_wgvT", [D, 512], F32, isOutput=False)
    a1_row = DP("a1_r_out_w", [512, D], F32, isOutput=False)
    a1_glgrw = DP("a1_glgrw", [D, 4], F32, isOutput=False)
    a1_wo = DP("a1_w_outT", [D, D], F32, isOutput=False)
    a1_locmask = DP("a1_locmask", [128, 384], F32, isOutput=False)
    a1_halomask = DP("a1_halomask", [128, 384], F32, isOutput=False)
    a1_lmask = DP("a1_lmask", [64, TOK], F32, isOutput=False)
    a1_poolw = DP("a1_poolw", [128, 2], F32, isOutput=False)
    f1_baseW = DP("f1_baseWT", [D, D], F32, isOutput=False)
    f1_baseb = DP("f1_base_b", [1, D], F32, isOutput=False)
    f1_muT = DP("f1_muT", [D, 8], F32, isOutput=False)
    f1_sigT = DP("f1_sigT", [D, 8], F32, isOutput=False)
    f1_downWf = DP("f1_downWf", [8 * D, 64], F32, isOutput=False)
    f1_downbf = DP("f1_down_bf", [8, 64], F32, isOutput=False)
    f1_upWf = DP("f1_upWf", [8 * 64, D], F32, isOutput=False)
    f1_upbf = DP("f1_up_bf", [8, D], F32, isOutput=False)
    f1_convTf = DP("f1_convTf", [8 * D, D], F32, isOutput=False)
    f1_convbf = DP("f1_conv_bf", [8, D], F32, isOutput=False)
    iota128 = DP("iota128", [128, 1], I32, isOutput=False)
    iota8 = DP("iota8", [1, 8], F32, isOutput=False)
    ident = DP("ident", [128, 128], BF16, isOutput=False)
    identf = DP("identf", [128, 128], F32, isOutput=False)
    ones1 = DP("ones1", [1, 128], BF16, isOutput=False)
    onesf = DP("onesf", [128, 1], F32, isOutput=False)
    x_out = DP("x_out", [TOK, D], F32, isOutput=True)

    x1s = nc.dram_tensor("x1s", [TOK, D], F32)
    x2s = nc.dram_tensor("x2s", [TOK, D], F32)
    x3s = nc.dram_tensor("x3s", [TOK, D], F32)
    cc_tail_in = nc.dram_tensor("cc_tail_in", [256, D], F32)
    cc_tail_out = nc.dram_tensor("cc_tail_out", [512, D], F32)
    cc_pool_in = nc.dram_tensor("cc_pool_in", [128, NCHUNK * 32], BF16)
    cc_pool_out = nc.dram_tensor("cc_pool_out", [256, NCHUNK * 32], BF16)
    cc_m12_in = nc.dram_tensor("cc_m12_in", [128, NCHUNK * 2], F32)
    cc_m12_out = nc.dram_tensor("cc_m12_out", [128, NCHUNK * 2], F32)
    cc_ghalo_in = nc.dram_tensor("cc_ghalo_in", [128, 4], F32)
    cc_ghalo_out = nc.dram_tensor("cc_ghalo_out", [256, 4], F32)
    wgbf = nc.dram_tensor("wgbf", [D, HID], BF16)
    wubf = nc.dram_tensor("wubf", [D, HID], BF16)
    wdbf = nc.dram_tensor("wdbf", [HID, D], BF16)
    a1wqbf = nc.dram_tensor("a1wqbf", [D, D], BF16)
    a1wkbf = nc.dram_tensor("a1wkbf", [D, D], BF16)
    a1wvbf = nc.dram_tensor("a1wvbf", [D, D], BF16)
    a1gqbf = nc.dram_tensor("a1gqbf", [D, 512], BF16)
    a1gkbf = nc.dram_tensor("a1gkbf", [D, 512], BF16)
    a1gvbf = nc.dram_tensor("a1gvbf", [D, 512], BF16)
    a1rowbf = nc.dram_tensor("a1rowbf", [512, D], BF16)
    a1wobf = nc.dram_tensor("a1wobf", [D, D], BF16)
    f1bwbf = nc.dram_tensor("f1bwbf", [D, D], BF16)

    groups = [[0, 1], [2, 3], [4, 5], [6, 7]]

    es = contextlib.ExitStack()
    with es:
        prog = Prog(nc)
        ctx = Ctx(nc, prog, es)
        ctx.consts = {"poolw": a1_poolw.ap()}
        ctx.ident_s, ctx.ident_b = ctx.sbuf("ident_s", [128, 128], BF16)
        ctx.identf_s, ctx.identf_b = ctx.sbuf("identf_s", [128, 128], F32)
        ctx.ones1_s, ctx.ones1_b = ctx.sbuf("ones1_s", [1, 128], BF16)
        ctx.onesf_s, ctx.onesf_b = ctx.sbuf("onesf_s", [128, 1], F32)
        tr, trb = ctx.sbuf("trash", [128, D], BF16)
        ctx.trash_s = tr
        ctx.trashb = trb
        prog.dma("sync", ctx.ident_s[:, :], ident.ap(), writes=[ctx.ident_b])
        prog.dma("sync", ctx.identf_s[:, :], identf.ap(), writes=[ctx.identf_b])
        prog.dma("sync", ctx.ones1_s[:, :], ones1.ap(), writes=[ctx.ones1_b])
        prog.dma("sync", ctx.onesf_s[:, :], onesf.ap(), writes=[ctx.onesf_b])

        gridB_s, gridB_b = ctx.sbuf("gridB", [128, 192], F32)
        gridC_s, gridC_b = ctx.sbuf("gridC", [128, 192], F32)
        gridD_s, gridD_b = ctx.sbuf("gridD", [128, 192], F32)

        cc1 = {
            "groups": groups,
            "tail_in": cc_tail_in.ap(), "tail_in_b": Buf("cc_tail_in"),
            "tail_out": cc_tail_out.ap(), "tail_out_b": Buf("cc_tail_out"),
            "pool_in": cc_pool_in.ap(), "pool_in_b": Buf("cc_pool_in"),
            "pool_out": cc_pool_out.ap(), "pool_out_b": Buf("cc_pool_out"),
            "ghalo_in": cc_ghalo_in.ap(), "ghalo_in_b": Buf("cc_ghalo_in"),
            "ghalo_out": cc_ghalo_out.ap(), "ghalo_out_b": Buf("cc_ghalo_out"),
            "sem_tail": "CCT", "sem_pool": "CCP",
            "sem_ghalo": prog.new_sem("ccg"),
            "tail_tok": ("CCT", 1), "x_ready": None,
        }
        ctx.next_grid = (gridB_s, gridB_b)
        ctx.grid_scratch = (gridD_s, gridD_b)
        with ctx.scope():
            build_l0attn(ctx, x0.ap(), x1s.ap(), l0_wq.ap(), l0_wk.ap(),
                         l0_wv.ap(), l0_wc.ap(), l0_biases.ap(), summ.ap(), {})
            first_cast = [getattr(ctx, "l0_ready_tok", None)]
            for dst, src, rows in (
                    (wgbf, f0_wg, D), (wubf, f0_wu, D), (wdbf, f0_wd, HID)):
                b = Buf("wcast")
                for r0 in range(0, rows, 128):
                    prog.dma("gpsimd", dst.ap()[r0:r0 + 128, :],
                             src.ap()[r0:r0 + 128, :], writes=[b],
                             waits=first_cast[0] and [first_cast[0]],
                             tag="wcast")
                    first_cast[0] = None
        ctx.next_grid = (gridC_s, gridC_b)
        with ctx.scope():
            build_l0ffn(ctx, x1s.ap(), x2s.ap(), wgbf.ap(), wubf.ap(),
                        wdbf.ap(), grid_ext=(gridB_s, gridB_b))
            for dst, src, rows in (
                    (a1wqbf, a1_wq, D), (a1wkbf, a1_wk, D),
                    (a1wvbf, a1_wv, D), (a1gqbf, a1_wgq, D),
                    (a1gkbf, a1_wgk, D), (a1gvbf, a1_wgv, D),
                    (a1rowbf, a1_row, 512), (a1wobf, a1_wo, D),
                    (f1bwbf, f1_baseW, D)):
                b = Buf("wcast")
                for r0 in range(0, rows, 128):
                    prog.dma("gpsimd", dst.ap()[r0:r0 + 128, :],
                             src.ap()[r0:r0 + 128, :], writes=[b],
                             tag="wcast")
        ctx.next_grid = (gridD_s, gridD_b)
        with ctx.scope():
            build_l1attn(ctx, x2s.ap(), None, None, x3s.ap(), a1wqbf.ap(),
                         a1wkbf.ap(), a1wvbf.ap(), a1gqbf.ap(), a1gkbf.ap(),
                         a1gvbf.ap(), a1rowbf.ap(), a1_glgrw.ap(), a1wobf.ap(),
                         a1_locmask.ap(), a1_halomask.ap(), a1_lmask.ap(),
                         cc=cc1, grid_ext=(gridC_s, gridC_b))
        ctx.next_grid = None
        cc2 = {
            "groups": groups,
            "m12_in": cc_m12_in.ap(), "m12_in_b": Buf("cc_m12_in"),
            "m12_out": cc_m12_out.ap(), "m12_out_b": Buf("cc_m12_out"),
            "sem_m12": "CCM",
        }
        with ctx.scope():
            build_l1ffn(ctx, x3s.ap(), None, x_out.ap(), f1bwbf.ap(),
                        f1_baseb.ap(), f1_muT.ap(), f1_sigT.ap(),
                        f1_downWf.ap(), f1_downbf.ap(), f1_upWf.ap(),
                        f1_upbf.ap(), f1_convTf.ap(), f1_convbf.ap(),
                        iota128.ap(), iota8.ap(), cc=cc2,
                        grid_ext=(gridD_s, gridD_b))
        prog.barrier()
        prog.realize(es)
    return nc


_PROGS = {}


def _prog(name):
    if name not in _PROGS:
        _PROGS[name] = {"l0attn": build_program_l0attn,
                        "l0ffn": build_program_l0ffn,
                        "l1attn": build_program_l1attn,
                        "l1ffn": build_program_l1ffn,
                        "fused": build_program_fused}[name]()
    return _PROGS[name]


def _cores():
    return [(c // 2, c % 2) for c in range(8)]


def kernel(**inp):
    return _kernel_fused(**inp)


def _kernel_fused(**inp):
    consts = make_consts()
    C = np.ascontiguousarray
    x0 = inp["L0_attn_x"].astype(np.float32)
    biases = np.stack([inp["L0_attn_bq"], inp["L0_attn_bk"], inp["L0_attn_bv"],
                       inp["L0_attn_bc"]]).astype(np.float32)
    w0 = {n: C(inp[f"L0_attn_{n}"].T.astype(np.float32))
          for n in ("wq", "wk", "wv", "wc")}
    wgT = C(inp["L0_ffn_w_gate"].T)
    wuT = C(inp["L0_ffn_w_up"].T)
    wdT = C(inp["L0_ffn_w_down"].T)
    w1 = {n: C(inp[f"L1_attn_{n}"].astype(np.float32))
          for n in ("wq", "wk", "wv", "wgq", "wgk", "wgv", "w_out", "r_out_w")}
    glw = inp["L1_attn_gate_local_w"]
    grw = inp["L1_attn_gate_remote_w"]
    glgrw = np.zeros((D, 4), np.float32)
    for e in range(8):
        glgrw[e * 128:e * 128 + 64, 0] = glw[2 * e]
        glgrw[e * 128 + 64:(e + 1) * 128, 1] = glw[2 * e + 1]
        glgrw[e * 128:e * 128 + 64, 2] = grw[2 * e]
        glgrw[e * 128 + 64:(e + 1) * 128, 3] = grw[2 * e + 1]
    kkv = np.arange(128)[:, None]
    relv = np.arange(384)[None, :]
    locmask = ((relv - kkv >= 0) & (relv - kkv < 256)).astype(np.float32)
    poolw = np.zeros((128, 2), np.float32)
    poolw[:64, 0] = 1.0 / 64
    poolw[64:, 1] = 1.0 / 64
    baseWT = C(inp["L1_ffn_base_W"].T.astype(np.float32))
    muT = C(inp["L1_ffn_disc_mu"].T.astype(np.float32))
    sigT = C(inp["L1_ffn_disc_sigma"].T.astype(np.float32))
    downWf = C(inp["L1_ffn_down_W"].reshape(8 * D, 64).astype(np.float32))
    upWf = C(inp["L1_ffn_up_W"].reshape(8 * 64, D).astype(np.float32))
    convTf = C(inp["L1_ffn_conv_W"].transpose(0, 2, 1).reshape(8 * D, D)
               .astype(np.float32))
    iota128 = np.arange(128, dtype=np.int32).reshape(128, 1)
    iota8 = np.arange(8, dtype=np.float32).reshape(1, 8)
    onesf = np.ones((128, 1), np.float32)
    maps = []
    for b, half in _cores():
        o = half * TOK
        glat = np.arange(64)
        s_glob = np.arange(TOK) + o
        lmask = (glat[:, None] <= (s_glob[None, :] // 64)).astype(np.float32)
        maps.append({
            "x0": C(x0[b, o:o + TOK]),
            "summ": C(np.concatenate([x0[b, o::SEG][:NSEG],
                                      x0[b, (1 - half) * TOK::SEG][:NSEG]])),
            "l0_wqT": w0["wq"], "l0_wkT": w0["wk"], "l0_wvT": w0["wv"],
            "l0_wcT": w0["wc"], "l0_biases": biases,
            "f0_wgT": wgT, "f0_wuT": wuT, "f0_wdT": wdT,
            "a1_wqT": w1["wq"], "a1_wkT": w1["wk"], "a1_wvT": w1["wv"],
            "a1_wgqT": w1["wgq"], "a1_wgkT": w1["wgk"], "a1_wgvT": w1["wgv"],
            "a1_r_out_w": w1["r_out_w"], "a1_glgrw": glgrw,
            "a1_w_outT": w1["w_out"], "a1_locmask": locmask,
            "a1_halomask": locmask * float(half == 1), "a1_lmask": lmask,
            "a1_poolw": poolw,
            "f1_baseWT": baseWT,
            "f1_base_b": inp["L1_ffn_base_b"].reshape(1, D).astype(np.float32),
            "f1_muT": muT, "f1_sigT": sigT,
            "f1_downWf": downWf,
            "f1_down_bf": inp["L1_ffn_down_b"].astype(np.float32),
            "f1_upWf": upWf, "f1_up_bf": inp["L1_ffn_up_b"].astype(np.float32),
            "f1_convTf": convTf,
            "f1_conv_bf": inp["L1_ffn_conv_b"].astype(np.float32),
            "iota128": iota128, "iota8": iota8,
            "ident": consts["ident_bf16"], "identf": consts["ident_f32"],
            "ones1": consts["ones1_bf16"], "onesf": onesf,
        })
    r = run_bass_kernel_spmd(_prog("fused"), maps, core_ids=list(range(8)))
    x4 = np.zeros((4, 4096, D), np.float32)
    for c, (b, half) in enumerate(_cores()):
        x4[b, half * TOK:(half + 1) * TOK] = r.results[c]["x_out"]
    return x4


def _kernel_multilaunch(**inp):
    consts = make_consts()
    ident = consts["ident_bf16"]
    identf = consts["ident_f32"]
    ones1 = consts["ones1_bf16"]
    C = np.ascontiguousarray
    x0 = inp["L0_attn_x"].astype(np.float32)

    biases = np.stack([inp["L0_attn_bq"], inp["L0_attn_bk"], inp["L0_attn_bv"],
                       inp["L0_attn_bc"]]).astype(np.float32)
    w0 = {n: C(inp[f"L0_attn_{n}"].T.astype(np.float32))
          for n in ("wq", "wk", "wv", "wc")}
    maps = []
    for b, half in _cores():
        o = half * TOK
        maps.append({
            "x_in": C(x0[b, o:o + TOK]),
            "wqT": w0["wq"], "wkT": w0["wk"], "wvT": w0["wv"], "wcT": w0["wc"],
            "biases": biases,
            "summ": C(np.concatenate([x0[b, o::SEG][:NSEG],
                                      x0[b, (1 - half) * TOK::SEG][:NSEG]])),
            "ident": ident, "identf": identf, "ones1": ones1,
        })
    r = run_bass_kernel_spmd(_prog("l0attn"), maps, core_ids=list(range(8)))
    x1 = np.zeros((4, 4096, D), np.float32)
    for c, (b, half) in enumerate(_cores()):
        x1[b, half * TOK:(half + 1) * TOK] = r.results[c]["x_out"]

    wgT = C(inp["L0_ffn_w_gate"].T)
    wuT = C(inp["L0_ffn_w_up"].T)
    wdT = C(inp["L0_ffn_w_down"].T)
    maps = []
    for b, half in _cores():
        o = half * TOK
        maps.append({"x_in": C(x1[b, o:o + TOK]), "wgT": wgT, "wuT": wuT,
                     "wdT": wdT, "ident": ident})
    r = run_bass_kernel_spmd(_prog("l0ffn"), maps, core_ids=list(range(8)))
    x2 = np.zeros((4, 4096, D), np.float32)
    for c, (b, half) in enumerate(_cores()):
        x2[b, half * TOK:(half + 1) * TOK] = r.results[c]["x_out"]

    w1 = {n: C(inp[f"L1_attn_{n}"].astype(np.float32))
          for n in ("wq", "wk", "wv", "wgq", "wgk", "wgv", "w_out", "r_out_w")}
    glw = inp["L1_attn_gate_local_w"]
    grw = inp["L1_attn_gate_remote_w"]
    glgrw = np.zeros((D, 4), np.float32)
    for e in range(8):
        glgrw[e * 128:e * 128 + 64, 0] = glw[2 * e]
        glgrw[e * 128 + 64:(e + 1) * 128, 1] = glw[2 * e + 1]
        glgrw[e * 128:e * 128 + 64, 2] = grw[2 * e]
        glgrw[e * 128 + 64:(e + 1) * 128, 3] = grw[2 * e + 1]
    kkv = np.arange(128)[:, None]
    relv = np.arange(384)[None, :]
    locmask = ((relv - kkv >= 0) & (relv - kkv < 256)).astype(np.float32)
    poolw = np.zeros((128, 2), np.float32)
    poolw[:64, 0] = 1.0 / 64
    poolw[64:, 1] = 1.0 / 64
    maps = []
    for b, half in _cores():
        o = half * TOK
        oo = (1 - half) * TOK
        xhalo = (x2[b, o - 256:o] if half == 1
                 else np.zeros((256, D), np.float32))
        glat = np.array([half * 32 + j if j < 32 else (1 - half) * 32 + (j - 32)
                         for j in range(64)])
        s_glob = np.arange(TOK) + o
        lmask = (glat[:, None] <= (s_glob[None, :] // 64)).astype(np.float32)
        maps.append({
            "x_in": C(x2[b, o:o + TOK]), "xhalo": C(xhalo),
            "xother": C(x2[b, oo:oo + TOK]),
            "wqT": w1["wq"], "wkT": w1["wk"], "wvT": w1["wv"],
            "wgqT": w1["wgq"], "wgkT": w1["wgk"], "wgvT": w1["wgv"],
            "r_out_w": w1["r_out_w"], "glgrw": glgrw, "w_outT": w1["w_out"],
            "locmask": locmask, "halomask": locmask * float(half == 1),
            "lmask": lmask, "ident": ident, "poolw": poolw,
        })
    r = run_bass_kernel_spmd(_prog("l1attn"), maps, core_ids=list(range(8)))
    x3 = np.zeros((4, 4096, D), np.float32)
    for c, (b, half) in enumerate(_cores()):
        x3[b, half * TOK:(half + 1) * TOK] = r.results[c]["x_out"]

    baseWT = C(inp["L1_ffn_base_W"].T.astype(np.float32))
    muT = C(inp["L1_ffn_disc_mu"].T.astype(np.float32))
    sigT = C(inp["L1_ffn_disc_sigma"].T.astype(np.float32))
    downWf = C(inp["L1_ffn_down_W"].reshape(8 * D, 64).astype(np.float32))
    upWf = C(inp["L1_ffn_up_W"].reshape(8 * 64, D).astype(np.float32))
    convTf = C(inp["L1_ffn_conv_W"].transpose(0, 2, 1).reshape(8 * D, D)
               .astype(np.float32))
    iota128 = np.arange(128, dtype=np.int32).reshape(128, 1)
    iota8 = np.arange(8, dtype=np.float32).reshape(1, 8)
    onesf = np.ones((128, 1), np.float32)
    maps = []
    for b, half in _cores():
        o = half * TOK
        oo = (1 - half) * TOK
        maps.append({
            "x_in": C(x3[b, o:o + TOK]), "xother": C(x3[b, oo:oo + TOK]),
            "baseWT": baseWT,
            "base_b": inp["L1_ffn_base_b"].reshape(1, D).astype(np.float32),
            "muT": muT, "sigT": sigT,
            "downWf": downWf, "down_bf": inp["L1_ffn_down_b"].astype(np.float32),
            "upWf": upWf, "up_bf": inp["L1_ffn_up_b"].astype(np.float32),
            "convTf": convTf, "conv_bf": inp["L1_ffn_conv_b"].astype(np.float32),
            "iota128": iota128, "iota8": iota8,
            "ident": ident, "ones1": ones1, "onesf": onesf,
        })
    r = run_bass_kernel_spmd(_prog("l1ffn"), maps, core_ids=list(range(8)))
    x4 = np.zeros((4, 4096, D), np.float32)
    for c, (b, half) in enumerate(_cores()):
        x4[b, half * TOK:(half + 1) * TOK] = r.results[c]["x_out"]
    return x4


# version 52 (final)
# speedup vs baseline: 1.1779x; 1.0412x over previous
import contextlib
import numpy as np
import ml_dtypes

import concourse.bass as bass
import concourse.mybir as mybir
from concourse.bass_utils import run_bass_kernel_spmd
from concourse.bass_core_rust import IndirectOffsetOnAxis

F32 = mybir.dt.float32
BF16 = mybir.dt.bfloat16
I32 = mybir.dt.int32
AF = mybir.ActivationFunctionType
ALU = mybir.AluOpType
AX = mybir.AxisListType
BFNP = ml_dtypes.bfloat16

TOK = 2048
NT = TOK // 128
D = 1024
NCHUNK = D // 128
EPS = 1e-5
SEG = 256
NSEG = TOK // SEG

ENGINES = ("tensor", "vector", "scalar", "gpsimd", "sync")
ENG_SEM = {
    "tensor": ("PEs", 1),
    "vector": ("DVEs", 1),
    "scalar": ("ACTs", 1),
    "gpsimd": ("POOLs", 1),
}
DMA_SEM = {"sync": ("SQ", 16), "gpsimd": ("GQ", 16), "scalar": ("AQ", 16)}
ALL_SEMS = ["PEs", "DVEs", "ACTs", "POOLs", "SQ", "GQ", "AQ", "CC"]


class Buf:

    __slots__ = ("w", "r", "name", "qsem", "qcnt", "qkind")

    def __init__(self, name=""):
        self.w = None
        self.r = []
        self.name = name
        self.qsem = None
        self.qcnt = 0
        self.qkind = None


class Prog:

    def __init__(self, nc):
        self.nc = nc
        self.ops = {e: [] for e in ENGINES}
        self.all_sems = list(ALL_SEMS)
        self.cnt = {s: 0 for s in self.all_sems}
        self.wm = {e: {} for e in ENGINES}
        self.sem_handles = {}
        self._nqsem = 0

    def new_sem(self, prefix="q"):
        name = "%s%d" % (prefix, self._nqsem)
        self._nqsem += 1
        self.all_sems.append(name)
        self.cnt[name] = 0
        return name

    def _waits_from(self, eng, bufs_r, bufs_w, extra_waits):
        need = {}
        for b in bufs_r:
            if b.w is not None:
                s, v = b.w
                need[s] = max(need.get(s, 0), v)
        for b in bufs_w:
            toks = list(b.r)
            if b.w is not None:
                toks.append(b.w)
            for s, v in toks:
                need[s] = max(need.get(s, 0), v)
        for w in extra_waits or ():
            if w is None:
                continue
            s, v = w
            need[s] = max(need.get(s, 0), v)
        wm = self.wm[eng]
        waits = []
        for s, v in need.items():
            if wm.get(s, 0) < v:
                waits.append((s, v))
                wm[s] = v
        return waits

    def emit(self, eng, build, reads=(), writes=(), waits=None, sem=None, tag=""):
        if sem is None:
            sem, amt = ENG_SEM[eng]
        else:
            sem, amt = sem
        ws = self._waits_from(eng, reads, writes, waits)
        self.cnt[sem] += amt
        tok = (sem, self.cnt[sem])
        self.ops[eng].append((ws, build, sem, amt, tag))
        for b in reads:
            b.r.append(tok)
        for b in writes:
            b.w = tok
            b.r = []
        return tok

    def emit_group(self, eng, builds, reads=(), writes=(), waits=None, tag=""):
        sem, amt = ENG_SEM[eng]
        ws = self._waits_from(eng, reads, writes, waits)
        for b in builds[:-1]:
            self.ops[eng].append((ws, b, None, 0, tag))
            ws = []
        self.cnt[sem] += amt
        tok = (sem, self.cnt[sem])
        self.ops[eng].append((ws, builds[-1], sem, amt, tag))
        for b in reads:
            b.r.append(tok)
        for b in writes:
            b.w = tok
            b.r = []
        return tok

    def dma(self, queue, out, in_, reads=(), writes=(), waits=None, tag="", **kw):
        kind = "SW" if queue == "gpsimd" else "HW"
        assert len(writes) <= 1
        b = writes[0] if writes else reads[0]
        if b.qsem is None or b.qkind != kind:
            b.qsem = self.new_sem("q")
            b.qkind = kind
            b.qcnt = 0
        sem_name = b.qsem

        def build(h, out=out, in_=in_, kw=kw):
            return h.dma_start(out=out, in_=in_, **kw)

        need = {}
        for b2 in reads:
            if b2.w is not None:
                s, v = b2.w
                need[s] = max(need.get(s, 0), v)
        if writes:
            for s, v in writes[0].r:
                need[s] = max(need.get(s, 0), v)
        for w in waits or ():
            if w is None:
                continue
            s, v = w
            need[s] = max(need.get(s, 0), v)
        wm = self.wm[queue]
        ws = []
        for s, v in need.items():
            if wm.get(s, 0) < v:
                ws.append((s, v))
                wm[s] = v
        b.qcnt += 1
        self.cnt[sem_name] = max(self.cnt[sem_name], 16 * b.qcnt)
        tok = (sem_name, 16 * b.qcnt)
        self.ops[queue].append((ws, build, sem_name, 16, tag))
        for b2 in reads:
            b2.r.append(tok)
        if writes:
            b.w = tok
            b.r = []
        return tok

    def dma_custom(self, queue, build, reads=(), writes=(), waits=None, tag=""):
        kind = "SW" if queue == "gpsimd" else "HW"
        assert len(writes) == 1
        b = writes[0]
        if b.qsem is None or b.qkind != kind:
            b.qsem = self.new_sem("q")
            b.qkind = kind
            b.qcnt = 0
        need = {}
        for b2 in reads:
            if b2.w is not None:
                s, v = b2.w
                need[s] = max(need.get(s, 0), v)
        for s, v in b.r:
            need[s] = max(need.get(s, 0), v)
        for w in waits or ():
            if w is None:
                continue
            s, v = w
            need[s] = max(need.get(s, 0), v)
        wm = self.wm[queue]
        ws = []
        for s, v in need.items():
            if wm.get(s, 0) < v:
                ws.append((s, v))
                wm[s] = v
        b.qcnt += 1
        self.cnt[b.qsem] = 16 * b.qcnt
        tok = (b.qsem, 16 * b.qcnt)
        self.ops[queue].append((ws, build, b.qsem, 16, tag))
        for b2 in reads:
            b2.r.append(tok)
        b.w = tok
        b.r = []
        return tok

    def collective(self, kind, op, groups, ins, outs, reads=(), writes=(),
                   sem_name=None, tag="cc"):
        assert sem_name is not None
        if sem_name not in self.cnt:
            self.all_sems.append(sem_name)
            self.cnt[sem_name] = 0

        def build(h, kind=kind, op=op, groups=groups, ins=ins, outs=outs):
            return h.collective_compute(kind, op, replica_groups=groups,
                                        ins=ins, outs=outs)

        ws = self._waits_from("gpsimd", reads, writes, None)
        self.cnt[sem_name] = 1
        tok = (sem_name, 1)
        self.ops["gpsimd"].append((ws, build, sem_name, 1, tag))
        for b in reads:
            b.r.append(tok)
        for b in writes:
            b.w = tok
            b.r = []
        return tok

    def barrier(self):
        toks = [(s, self.cnt[s]) for s in self.all_sems if self.cnt[s] > 0]
        for e in ENGINES:
            wm = self.wm[e]
            ws = [(s, v) for s, v in toks if wm.get(s, 0) < v]
            for s, v in ws:
                wm[s] = v
            if ws:
                self.ops[e].append((ws, None, None, 0, "barrier"))

    def realize(self, es):
        for s in self.all_sems:
            self.sem_handles[s] = es.enter_context(self.nc.semaphore(s))
        block = es.enter_context(self.nc.Block())
        prog = self

        def runner(eng_name):
            def _run(h):
                for ws, build, sem, amt, tag in prog.ops[eng_name]:
                    for s, v in ws:
                        h.wait_ge(prog.sem_handles[s], v)
                    if build is not None:
                        inst = build(h)
                        if sem is not None and amt:
                            inst.then_inc(prog.sem_handles[sem], amt)
            return _run

        for eng_name in ENGINES:
            if prog.ops[eng_name]:
                getattr(block, eng_name)(runner(eng_name))




def ts(i, n):
    return slice(i * n, (i + 1) * n)


class Ctx:

    def __init__(self, nc, prog, es):
        self.nc = nc
        self.p = prog
        self.es = es
        self.es_root = es
        self.sb = {}
        self._cc = {}
        self._cct = es.enter_context(nc.sbuf_tensor("cconst", [128, 32], F32))
        self._ccb = Buf("cconst")
        self.sqtr_s = es.enter_context(nc.sbuf_tensor("sqtr", [128, 1024], BF16))
        self.sqtrb = Buf("sqtr")
        self.store_toks = []
        self.on_store = None

    def cconst(self, value):
        value = float(value)
        if value not in self._cc:
            col = len(self._cc)
            assert col < 32
            self._cc[value] = col
            self.p.emit("vector",
                        (lambda h, col=col, value=value: h.memset(
                            self._cct[:, col:col + 1], value)),
                        writes=[self._ccb], tag="cconst")
        col = self._cc[value]
        return self._cct[:, col:col + 1]

    def sbuf(self, name, shape, dt):
        t = self.es.enter_context(self.nc.sbuf_tensor(name, shape, dt))
        b = Buf(name)
        self.sb[name] = (t, b)
        return t, b

    def psum(self, name, shape, dt):
        t = self.es.enter_context(self.nc.psum_tensor(name, shape, dt))
        b = Buf(name)
        return t, b

    @contextlib.contextmanager
    def scope(self):
        es2 = contextlib.ExitStack()
        old = self.es
        self.es = es2
        try:
            with es2:
                yield
                self.p.barrier()
        finally:
            self.es = old


def ln_stats_acc(ctx, src_ap, src_buf, stat, statb, n_feat=D, pref="",
                 rows=128):
    p = ctx.p
    tr_ap = ctx.trash_s[0:rows, 0:n_feat]
    p.emit("scalar", lambda h: h.activation(tr_ap, src_ap, AF.Copy,
                                            accum_out=stat[:, 0:1]),
           reads=[src_buf], writes=[statb, ctx.trashb], tag=pref + "sum")
    p.emit("scalar", lambda h: h.activation(tr_ap, src_ap, AF.Square,
                                            accum_out=stat[:, 1:2]),
           reads=[src_buf], writes=[statb, ctx.trashb], tag=pref + "sumsq")


def ln_stats_acc_v(ctx, src_ap, src_buf, stat, statb, n_feat=D, pref="",
                   rows=128):
    p = ctx.p
    tr_ap = ctx.trash_s[0:rows, 0:n_feat]
    sq_ap = ctx.sqtr_s[0:rows, 0:n_feat]
    p.emit("vector", lambda h: h.tensor_scalar(
        tr_ap, src_ap, 1.0, 0.0, ALU.mult, ALU.add, accum_out=stat[:, 0:1]),
        reads=[src_buf], writes=[statb, ctx.trashb], tag=pref + "sumv")
    p.emit("vector", lambda h: h.scalar_tensor_tensor(
        sq_ap, src_ap, 1.0, src_ap, ALU.mult, ALU.mult,
        accum_out=stat[:, 1:2]),
        reads=[src_buf], writes=[statb, ctx.sqtrb], tag=pref + "sumsqv")


def grid_acc_v(ctx, src_ap, src_buf, t, tag):
    ng = getattr(ctx, "next_grid", None)
    if ng is None:
        return
    g_s, g_b = ng
    p = ctx.p
    p.emit("vector", (lambda h, src_ap=src_ap, t=t: h.tensor_scalar(
        ctx.trash_s[:, 0:D], src_ap, 1.0, 0.0, ALU.mult, ALU.add,
        accum_out=g_s[:, t:t + 1])),
        reads=[src_buf], writes=[g_b, ctx.trashb], tag=tag + " gsumv")
    p.emit("vector", (lambda h, src_ap=src_ap, t=t: h.scalar_tensor_tensor(
        ctx.sqtr_s[:, 0:D], src_ap, 1.0, src_ap, ALU.mult, ALU.mult,
        accum_out=g_s[:, 32 + t:33 + t])),
        reads=[src_buf], writes=[g_b, ctx.sqtrb], tag=tag + " gsqv")


def ln_stats_fin(ctx, stat, statb, n_feat=D, eps=EPS, eps_scale=1.0,
                 pref="", rows=128):
    p = ctx.p
    inv = 1.0 / n_feat
    p.emit("vector", lambda h: h.tensor_scalar(stat[:, 2:4], stat[:, 0:2], inv,
                                               None, ALU.mult),
           reads=[statb], writes=[statb], tag=pref + "mE2")
    p.emit("vector", lambda h: h.scalar_tensor_tensor(
        stat[:, 0:1], stat[:, 2:3], stat[:, 2:3], stat[:, 3:4],
        ALU.mult, ALU.subtract),
        reads=[statb], writes=[statb], tag=pref + "var")
    eps_ap = ctx.cconst(eps * eps_scale)[0:rows, :]
    neg1 = ctx.cconst(-1.0)[0:rows, :]
    p.emit("scalar", lambda h: h.activation(stat[:, 1:2], stat[:, 0:1], AF.Sqrt,
                                            bias=eps_ap, scale=neg1),
           reads=[statb, ctx._ccb], writes=[statb], tag=pref + "sd")
    p.emit("vector", lambda h: h.reciprocal(stat[:, 4:5], stat[:, 1:2]),
           reads=[statb], writes=[statb], tag=pref + "r")
    p.emit("vector", lambda h: h.tensor_scalar(stat[:, 5:6], stat[:, 2:3],
                                               stat[:, 4:5], -1.0, ALU.mult,
                                               ALU.mult),
           reads=[statb], writes=[statb], tag=pref + "b")


def ln_stats(ctx, src_ap, src_buf, stat, statb, n_feat=D, eps=EPS,
             eps_scale=1.0, pref="", rows=128):
    ln_stats_acc(ctx, src_ap, src_buf, stat, statb, n_feat=n_feat, pref=pref,
                 rows=rows)
    ln_stats_fin(ctx, stat, statb, n_feat=n_feat, eps=eps,
                 eps_scale=eps_scale, pref=pref, rows=rows)


def ln_prepass_tile(ctx, i, s, w, grid, gridb, grid2b, xs, xb, pref=""):
    p = ctx.p
    par = i % len(xs)
    p.dma("sync", xs[par], s, writes=[xb[par]], waits=w, tag=pref + "px")
    p.emit("vector", (lambda h, i=i, par=par: h.tensor_scalar(
        ctx.trash_s[:, 0:D], xs[par], 1.0, 0.0, ALU.mult, ALU.add,
        accum_out=grid[:, i:i + 1])),
        reads=[xb[par]], writes=[gridb, ctx.trashb], tag=pref + "psum")
    p.emit("scalar", (lambda h, i=i, par=par: h.activation(
        ctx.sqtr_s[:, 0:D], xs[par], AF.Square,
        accum_out=grid[:, 32 + i:33 + i])),
        reads=[xb[par]], writes=[grid2b, ctx.sqtrb], tag=pref + "psq")


def ln_prepass_finish(ctx, n, grid, gridb, grid2b, eps=EPS, eps_scale=1.0,
                      pref=""):
    p = ctx.p
    inv = 1.0 / D
    p.emit("vector", (lambda h, n=n: h.tensor_scalar(
        grid[:, 0:n], grid[:, 0:n], inv, None, ALU.mult)),
        reads=[gridb, grid2b], writes=[gridb], tag=pref + "pmE")
    p.emit("vector", (lambda h, n=n: h.tensor_scalar(
        grid[:, 32:32 + n], grid[:, 32:32 + n], inv, None, ALU.mult)),
        reads=[gridb], writes=[gridb], tag=pref + "pmE2")
    p.emit("vector", (lambda h, n=n: h.tensor_tensor(
        grid[:, 64:64 + n], grid[:, 0:n], grid[:, 0:n], op=ALU.mult)),
        reads=[gridb], writes=[gridb], tag=pref + "pmm")
    p.emit("vector", (lambda h, n=n: h.tensor_tensor(
        grid[:, 96:96 + n], grid[:, 64:64 + n], grid[:, 32:32 + n],
        op=ALU.subtract)),
        reads=[gridb], writes=[gridb], tag=pref + "pnv")
    eps_ap = ctx.cconst(eps * eps_scale)
    neg1 = ctx.cconst(-1.0)
    p.emit("scalar", (lambda h, n=n: h.activation(
        grid[:, 64:64 + n], grid[:, 96:96 + n], AF.Sqrt, bias=eps_ap,
        scale=neg1)),
        reads=[gridb, ctx._ccb], writes=[gridb], tag=pref + "psd")
    p.emit("vector", (lambda h, n=n: h.reciprocal(
        grid[:, 128:128 + n], grid[:, 64:64 + n])),
        reads=[gridb], writes=[gridb], tag=pref + "pr")
    p.emit("vector", (lambda h, n=n: h.scalar_tensor_tensor(
        grid[:, 160:160 + n], grid[:, 0:n], neg1, grid[:, 128:128 + n],
        ALU.mult, ALU.mult)),
        reads=[gridb, ctx._ccb], writes=[gridb], tag=pref + "pb")


def grid_acc(ctx, src_ap, src_buf, t, tag):
    ng = getattr(ctx, "next_grid", None)
    if ng is None:
        return
    g_s, g_b = ng
    p = ctx.p
    p.emit("scalar", (lambda h, src_ap=src_ap, t=t: h.activation(
        ctx.sqtr_s[:, 0:D], src_ap, AF.Copy, accum_out=g_s[:, t:t + 1])),
        reads=[src_buf], writes=[g_b, ctx.sqtrb], tag=tag + " gsum")
    p.emit("scalar", (lambda h, src_ap=src_ap, t=t: h.activation(
        ctx.sqtr_s[:, 0:D], src_ap, AF.Square,
        accum_out=g_s[:, 32 + t:33 + t])),
        reads=[src_buf], writes=[g_b, ctx.sqtrb], tag=tag + " gsq")


def ln_prepass(ctx, srcs, grid, gridb, xs, xb, eps=EPS, eps_scale=1.0, pref=""):
    grid2b = Buf(pref + "g2")
    for i, (s, w) in enumerate(srcs):
        ln_prepass_tile(ctx, i, s, w, grid, gridb, grid2b, xs, xb, pref=pref)
    ln_prepass_finish(ctx, len(srcs), grid, gridb, grid2b, eps, eps_scale, pref)




def build_l0attn(ctx, x_in, x_out, wq, wk, wv, wc, biases, summ_in, consts,
                 grid_ext=None):
    nc, p, es = ctx.nc, ctx.p, ctx.es
    ctx.store_toks = []

    wq_s, wq_b = ctx.sbuf("l0_wq", [128, NCHUNK * D], BF16)
    wk_s, wk_b = ctx.sbuf("l0_wk", [128, NCHUNK * D], BF16)
    wv_s, wv_b = ctx.sbuf("l0_wv", [128, NCHUNK * D], BF16)
    wc_s, wc_b = ctx.sbuf("l0_wc", [128, NCHUNK * D], BF16)
    bias_s, bias_b = ctx.sbuf("l0_bias", [4, D], F32)
    biasbf_s, biasbf_b = ctx.sbuf("l0_biasbf", [1, 4 * D], BF16)
    biasT_s, biasT_b = ctx.sbuf("l0_biasT", [128, 4 * NCHUNK], F32)
    summ_s, summ_b = ctx.sbuf("l0_summ", [16, D], F32)
    sstat_s, sstat_b = ctx.sbuf("l0_sstat", [16, 8], F32)
    sxn_s, sxn_b = ctx.sbuf("l0_sxn", [16, D], BF16)
    sxnT_s, sxnT_b = ctx.sbuf("l0_sxnT", [128, NCHUNK * 16], BF16)
    qgT_s, qgT_b = ctx.sbuf("l0_qgT", [128, NCHUNK * 16], BF16)
    kgT_s, kgT_b = ctx.sbuf("l0_kgT", [128, NCHUNK * 16], BF16)
    vg_s, vg_b = ctx.sbuf("l0_vg", [16, D + 1], BF16)
    pg_s, pg_b = ctx.sbuf("l0_pg", [16, 16], BF16)
    pgT_s, pgT_b = ctx.sbuf("l0_pgT", [16, 16], BF16)
    gstat_s, gstat_b = ctx.sbuf("l0_gstat", [16, 2], F32)
    goutT_s, goutT_b = ctx.sbuf("l0_goutT", [128, NCHUNK * 16], F32)

    x_s, x_b = zip(*[ctx.sbuf("l0_x%d" % i, [128, D], F32) for i in range(2)])
    stat_s, stat_b = zip(*[ctx.sbuf("l0_st%d" % i, [128, 16], F32) for i in range(2)])
    xn_s, xn_b = zip(*[ctx.sbuf("l0_xn%d" % i, [128, D], BF16) for i in range(2)])
    xnT_s, xnT_b = zip(*[ctx.sbuf("l0_xnT%d" % i, [128, NCHUNK * 2 * SEG], BF16) for i in range(2)])
    qT_s, qT_b = zip(*[ctx.sbuf("l0_qT%d" % i, [128, NCHUNK * 2 * SEG], BF16) for i in range(2)])
    kT_s, kT_b = zip(*[ctx.sbuf("l0_kT%d" % i, [128, NCHUNK * 2 * SEG], BF16) for i in range(2)])
    v_s, v_b = zip(*[ctx.sbuf("l0_v%d" % i, [128, 2 * (D + 1)], BF16) for i in range(2)])
    pT_s, pT_b = zip(*[ctx.sbuf("l0_pT%d" % i, [128, 2 * SEG], BF16) for i in range(2)])
    fused_s, fused_b = zip(*[ctx.sbuf("l0_fsd%d" % i, [128, 2 * D], BF16) for i in range(2)])
    fusedT_s, fusedT_b = zip(*[ctx.sbuf("l0_fsdT%d" % i, [128, NCHUNK * SEG], BF16) for i in range(2)])
    rsum_s, rsum_b = ctx.sbuf("l0_rsum", [128, 4], F32)
    xr_s, xr_b = zip(*[ctx.sbuf("l0_xr%d" % i, [128, D], F32) for i in range(2)])
    lny_s, lny_b = zip(*[ctx.sbuf("l0_lny%d" % i, [128, D], BF16) for i in range(2)])
    xo_s, xo_b = zip(*[ctx.sbuf("l0_xo%d" % i, [128, 2 * D], F32) for i in range(2)])

    ident = ctx.ident_s
    identb = ctx.ident_b
    ones1_s = ctx.ones1_s
    ones1_b = ctx.ones1_b

    P = [ctx.psum("l0_P%d" % i, [128, 512], F32) for i in range(6)]
    PT, PTb = ctx.psum("l0_PT", [128, 1024], BF16)
    PT2, PT2b = ctx.psum("l0_PT2", [128, 1024], BF16)

    for w_ap, (s, b) in ((wq, (wq_s, wq_b)), (wk, (wk_s, wk_b)),
                         (wv, (wv_s, wv_b)), (wc, (wc_s, wc_b))):
        for c in range(NCHUNK):
            p.dma("gpsimd", s[:, ts(c, D)], w_ap[ts(c, 128), :], writes=[b],
                  tag="l0 wload")
    p.dma("sync", bias_s[:, :], biases, writes=[bias_b], tag="l0 bias")
    p.dma("gpsimd", biasbf_s[:, :],
          biases.rearrange("r d -> (r d)").rearrange("(a n) -> a n", a=1),
          writes=[biasbf_b], tag="l0 biasbf")
    p.dma("sync", summ_s[:, :], summ_in, writes=[summ_b], tag="l0 summ")

    pS, pSb = P[4]
    for c in range(NCHUNK):
        p.emit("tensor", (lambda h, c=c: h.transpose(
            pS[:, 0:4], bias_s[0:4, ts(c, 128)], ctx.identf_s[0:4, 0:4])),
            reads=[bias_b, ctx.identf_b], writes=[pSb], tag="l0 biasT")
        p.emit("vector", (lambda h, c=c: h.tensor_copy(
            biasT_s[:, ts(c, 4)], pS[:, 0:4])),
            reads=[pSb], writes=[biasT_b], tag="l0 biasT evac")

    def biasT(row, c):
        return biasT_s[:, c * 4 + row:c * 4 + row + 1]

    ln_stats(ctx, summ_s[0:16, :], summ_b, sstat_s[0:16, :], sstat_b, pref="l0s ",
             rows=16)
    p.emit("vector", lambda h: h.tensor_scalar(
        sxn_s[0:16, :], summ_s[0:16, :], sstat_s[0:16, 4:5], sstat_s[0:16, 5:6],
        ALU.mult, ALU.add), reads=[summ_b, sstat_b], writes=[sxn_b], tag="l0s xn")
    for c in range(NCHUNK):
        p.emit("tensor", (lambda h, c=c: h.transpose(
            PT[:, 0:16], sxn_s[0:16, ts(c, 128)], ident[0:16, 0:16])),
            reads=[sxn_b, identb], writes=[PTb], tag="l0s xnT")
        p.emit("vector", (lambda h, c=c: h.tensor_copy(
            sxnT_s[:, ts(c, 16)], PT[:, 0:16])),
            reads=[PTb], writes=[sxnT_b], tag="l0s xnT evac")
    for (wsrc, wsb, dsts, dstb, brow) in ((wq_s, wq_b, qgT_s, qgT_b, 0),
                                          (wk_s, wk_b, kgT_s, kgT_b, 1)):
        for e in range(NCHUNK):
            pA, pAb = P[e % 2]
            p.emit_group("tensor", [
                (lambda h, e=e, c=c, wsrc=wsrc, pA=pA: h.matmul(
                    pA[:, 0:16], wsrc[:, c * D + e * 128:c * D + (e + 1) * 128],
                    sxnT_s[:, ts(c, 16)], start=(c == 0), stop=(c == NCHUNK - 1)))
                for c in range(NCHUNK)],
                reads=[wsb, sxnT_b], writes=[pAb], tag="l0s qg mm")
            p.emit("vector", (lambda h, e=e, dsts=dsts, brow=brow, pA=pA:
                              h.tensor_scalar(
                dsts[:, ts(e, 16)], pA[:, 0:16], biasT(brow, e), None, ALU.add)),
                reads=[pAb, biasT_b], writes=[dstb], tag="l0s qg evac")
    p.emit("vector", lambda h: h.memset(vg_s[0:16, D:D + 1], 1.0),
           writes=[vg_b], tag="l0s vg ones")
    for n in range(2):
        pA, pAb = P[n % 2]
        builds = [
            (lambda h, n=n, c=c, pA=pA: h.matmul(
                pA[0:16, 0:512], sxnT_s[:, ts(c, 16)],
                wv_s[:, c * D + n * 512:c * D + (n + 1) * 512],
                start=(c == 0), stop=False))
            for c in range(NCHUNK)]
        builds.append(lambda h, n=n, pA=pA: h.matmul(
            pA[0:16, 0:512], ones1_s[0:1, 0:16],
            biasbf_s[0:1, 2 * D + n * 512:2 * D + (n + 1) * 512],
            start=False, stop=True))
        p.emit_group("tensor", builds, reads=[sxnT_b, wv_b, ones1_b, biasbf_b],
                     writes=[pAb], tag="l0s vg mm")
        p.emit("vector", (lambda h, n=n, pA=pA: h.tensor_copy(
            vg_s[0:16, ts(n, 512)], pA[0:16, 0:512])),
            reads=[pAb], writes=[vg_b], tag="l0s vg evac")
    pC, pCb = P[2]
    p.emit_group("tensor", [
        (lambda h, c=c: h.matmul(pC[0:16, 0:16], qgT_s[:, ts(c, 16)],
                                 kgT_s[:, ts(c, 16)],
                                 start=(c == 0), stop=(c == NCHUNK - 1)))
        for c in range(NCHUNK)],
        reads=[qgT_b, kgT_b], writes=[pCb], tag="l0s sg mm")
    eighth = ctx.cconst(0.125)
    p.emit("scalar", lambda h: h.activation(
        pg_s[0:16, :], pC[0:16, 0:16], AF.Exp, scale=eighth[0:16, :],
        accum_out=gstat_s[0:16, 0:1]),
        reads=[pCb, ctx._ccb], writes=[pg_b, gstat_b], tag="l0s exp")
    p.emit("vector", lambda h: h.reciprocal(gstat_s[0:16, 1:2], gstat_s[0:16, 0:1]),
           reads=[gstat_b], writes=[gstat_b], tag="l0s recip")
    p.emit("vector", lambda h: h.tensor_scalar(
        pg_s[0:16, :], pg_s[0:16, :], gstat_s[0:16, 1:2], None, ALU.mult),
        reads=[gstat_b, pg_b], writes=[pg_b], tag="l0s pnorm")
    p.emit("tensor", lambda h: h.transpose(PT[0:16, 0:16], pg_s[0:16, 0:16],
                                           ident[0:16, 0:16]),
           reads=[pg_b, identb], writes=[PTb], tag="l0s pgT")
    p.emit("vector", lambda h: h.tensor_copy(pgT_s[0:16, :], PT[0:16, 0:16]),
           reads=[PTb], writes=[pgT_b], tag="l0s pgT evac")
    for c in range(NCHUNK):
        pA, pAb = P[c % 2]
        p.emit("tensor", (lambda h, c=c, pA=pA: h.matmul(
            pA[:, 0:16], vg_s[0:16, ts(c, 128)], pgT_s[0:16, 0:16],
            start=True, stop=True)),
            reads=[vg_b, pgT_b], writes=[pAb], tag="l0s gout mm")
        p.emit("vector", (lambda h, c=c, pA=pA: h.tensor_copy(
            goutT_s[:, ts(c, 16)], pA[:, 0:16])),
            reads=[pAb], writes=[goutT_b], tag="l0s gout evac")

    if grid_ext is None:
        if getattr(ctx, "grid_scratch", None) is not None:
            grid_s, grid_b = ctx.grid_scratch
            ctx.grid_scratch = None
        else:
            grid_s, grid_b = ctx.sbuf("l0_grid", [128, 192], F32)
        ln_prepass(ctx, [(x_in[ts(t, 128), :], None) for t in range(NT)],
                   grid_s, grid_b, [x_s[0][:, :], x_s[1][:, :]],
                   [x_b[0], x_b[1]], pref="l0 ")
        ctx.l0_ready_tok = ("DVEs", p.cnt["DVEs"])
    else:
        grid_s, grid_b = grid_ext

    pending = []

    def l0_tail(t, par, th):
        def run(t=t, par=par, th=th):
            st = stat_s[par][:, 8 * th:8 * th + 8]
            ln_stats_fin(ctx, st, stat_b[par], pref="l0y ")
            p.emit("vector", (lambda h, th=th, par=par, st=st: h.tensor_scalar(
                lny_s[par][:, :], xo_s[par][:, ts(th, D)],
                st[:, 4:5], st[:, 5:6], ALU.mult, ALU.add)),
                reads=[xo_b[par], stat_b[par]], writes=[lny_b[par]],
                tag="l0 lny")
            p.dma("sync", xr_s[par][:, :], x_in[ts(t, 128), :],
                  writes=[xr_b[par]], tag="l0 xre")
            p.emit("vector", (lambda h, th=th, par=par: h.scalar_tensor_tensor(
                xo_s[par][:, ts(th, D)], lny_s[par][:, :], 1.0,
                xr_s[par][:, :], ALU.mult, ALU.add)),
                reads=[lny_b[par], xr_b[par]], writes=[xo_b[par]],
                tag="l0 resid")
            ctx.store_toks.append(p.dma(
                "scalar", x_out[ts(t, 128), :], xo_s[par][:, ts(th, D)],
                reads=[xo_b[par]], tag="l0 xstore"))
            if ctx.on_store:
                ctx.on_store(len(ctx.store_toks) - 1, ctx.store_toks[-1])

        def grun(t=t, par=par, th=th):
            grid_acc(ctx, xo_s[par][:, ts(th, D)], xo_b[par], t, "l0")
        return run, grun

    for sp in range(NSEG // 2):
        ppar = sp % 2
        xnTp_s, xnTp_b = xnT_s[ppar], xnT_b[ppar]
        qTp_s, qTp_b = qT_s[ppar], qT_b[ppar]
        kTp_s, kTp_b = kT_s[ppar], kT_b[ppar]
        for q4 in range(4):
            t = 4 * sp + q4
            par = t % 2
            pt, ptb = PT, PTb
            p.dma("sync", x_s[par][:, :], x_in[ts(t, 128), :],
                  writes=[x_b[par]], tag="l0 xload")
            p.emit("vector", (lambda h, par=par, t=t: h.tensor_scalar(
                xn_s[par][:, :], x_s[par][:, :],
                grid_s[:, 128 + t:129 + t], grid_s[:, 160 + t:161 + t],
                ALU.mult, ALU.add)),
                reads=[x_b[par], grid_b], writes=[xn_b[par]], tag="l0 xn")
            p.emit_group("tensor", [
                (lambda h, c=c, par=par, pt=pt: h.transpose(
                    pt[:, ts(c, 128)],
                    xn_s[par][:, ts(c, 128)],
                    ident[:, :]))
                for c in range(NCHUNK)],
                reads=[xn_b[par], identb], writes=[ptb], tag="l0 xnT")
            p.emit("vector", (lambda h, q4=q4, par=par, pt=pt, xnTp_s=xnTp_s:
                              h.tensor_copy(
                xnTp_s.rearrange("p (c q) -> p c q", c=NCHUNK)[:, :, ts(q4, 128)],
                pt.rearrange("p (c q) -> p c q", c=NCHUNK)[:, :, :])),
                reads=[ptb], writes=[xnTp_b], tag="l0 xnT evac")
        for (wsrc, wsb, dsts, dstb, brow) in (
                (wq_s, wq_b, qTp_s, qTp_b, 0),
                (wk_s, wk_b, kTp_s, kTp_b, 1)):
            for e in range(NCHUNK):
                pp, ppb = P[e % 2]
                p.emit_group("tensor", [
                    (lambda h, e=e, c=c, wsrc=wsrc, pp=pp, xnTp_s=xnTp_s:
                     h.matmul(
                        pp[:, 0:512],
                        wsrc[:, c * D + e * 128:c * D + (e + 1) * 128],
                        xnTp_s[:, c * 2 * SEG:c * 2 * SEG + 512],
                        start=(c == 0), stop=(c == NCHUNK - 1)))
                    for c in range(NCHUNK)],
                    reads=[wsb, xnTp_b], writes=[ppb], tag="l0 qk mm")
                p.emit("vector", (lambda h, e=e, dsts=dsts, brow=brow, pp=pp:
                                  h.tensor_scalar(dsts[:, e * 512:(e + 1) * 512],
                                                  pp[:, 0:512], biasT(brow, e),
                                                  None, ALU.add)),
                       reads=[ppb, biasT_b], writes=[dstb], tag="l0 qk evac")
        for fn, _ in pending:
            fn()
        for _, g in pending:
            g()
        pending = []
        for si in range(2):
            s = 2 * sp + si
            gs = s
            par = s % 2
            pA, pAb = P[3 * par]
            pB, pBb = P[3 * par + 1]
            pC2, pC2b = P[3 * par + 2]
            pt, ptb = PT2, PT2b
            if sp == 0:
                for th in range(2):
                    p.emit("vector", (lambda h, th=th, par=par: h.memset(
                        v_s[par][:, th * (D + 1) + D:th * (D + 1) + D + 1], 1.0)),
                        writes=[v_b[par]], tag="l0 v ones")
            for th in range(2):
                for n in range(2):
                    pp, ppb = (pA, pAb) if n == 0 else (pB, pBb)
                    builds = [
                        (lambda h, th=th, n=n, c=c, pp=pp, si=si,
                         xnTp_s=xnTp_s: h.matmul(
                            pp[:, 0:512],
                            xnTp_s[:, c * 2 * SEG + si * SEG + th * 128:c * 2 * SEG + si * SEG + (th + 1) * 128],
                            wv_s[:, c * D + n * 512:c * D + (n + 1) * 512],
                            start=(c == 0), stop=False))
                        for c in range(NCHUNK)]
                    builds.append(lambda h, n=n, pp=pp: h.matmul(
                        pp[:, 0:512], ones1_s[0:1, :],
                        biasbf_s[0:1, 2 * D + n * 512:2 * D + (n + 1) * 512],
                        start=False, stop=True))
                    p.emit_group("tensor", builds,
                                 reads=[xnTp_b, wv_b, ones1_b, biasbf_b],
                                 writes=[ppb], tag="l0 v mm")
                    p.emit("vector", (lambda h, th=th, n=n, pp=pp, par=par:
                                      h.tensor_copy(
                        v_s[par][:, th * (D + 1) + n * 512:th * (D + 1) + (n + 1) * 512],
                        pp[:, 0:512])),
                        reads=[ppb], writes=[v_b[par]], tag="l0 v evac")
            for kt in range(2):
                pp, ppb = (pA, pAb) if kt == 0 else (pB, pBb)
                p.emit_group("tensor", [
                    (lambda h, kt=kt, c=c, pp=pp, si=si, kTp_s=kTp_s,
                     qTp_s=qTp_s: h.matmul(
                        pp[:, 0:256],
                        kTp_s[:, c * 2 * SEG + si * SEG + kt * 128:c * 2 * SEG + si * SEG + (kt + 1) * 128],
                        qTp_s[:, c * 2 * SEG + si * SEG:c * 2 * SEG + (si + 1) * SEG],
                        start=(c == 0), stop=(c == NCHUNK - 1)))
                    for c in range(NCHUNK)],
                    reads=[kTp_b, qTp_b], writes=[ppb], tag="l0 sc mm")
                p.emit("scalar", (lambda h, kt=kt, pp=pp, par=par: h.activation(
                    pT_s[par][:, ts(kt, 256)], pp[:, 0:256], AF.Exp,
                    scale=eighth)),
                    reads=[ppb, ctx._ccb], writes=[pT_b[par]], tag="l0 exp")
            for qh in range(2):
                sb_ = [
                    (lambda h, qh=qh, kt=kt, par=par: h.matmul(
                        pC2[:, 4 + qh:5 + qh],
                        pT_s[par][:, kt * 256 + qh * 128:kt * 256 + (qh + 1) * 128],
                        v_s[par][:, kt * (D + 1) + D:kt * (D + 1) + D + 1],
                        start=(kt == 0), stop=(kt == 1)))
                    for kt in range(2)]
                p.emit_group("tensor", sb_, reads=[pT_b[par], v_b[par]],
                             writes=[pC2b], tag="l0 sums mm")
                p.emit("vector", (lambda h, qh=qh, par=par: h.reciprocal(
                    rsum_s[:, 2 * par + qh:2 * par + qh + 1],
                    pC2[:, 4 + qh:5 + qh])),
                    reads=[pC2b], writes=[rsum_b], tag="l0 rsum")
                for n in range(2):
                    pp, ppb = (pA, pAb) if n == 0 else (pB, pBb)
                    builds = [
                        (lambda h, qh=qh, n=n, kt=kt, pp=pp, par=par: h.matmul(
                            pp[:, 0:512],
                            pT_s[par][:, kt * 256 + qh * 128:kt * 256 + (qh + 1) * 128],
                            v_s[par][:, kt * (D + 1) + n * 512:kt * (D + 1) + (n + 1) * 512],
                            start=(kt == 0), stop=(kt == 1)))
                        for kt in range(2)]
                    p.emit_group("tensor", builds, reads=[pT_b[par], v_b[par]],
                                 writes=[ppb], tag="l0 av mm")
                    p.emit("vector", (lambda h, qh=qh, n=n, pp=pp, par=par:
                                      h.tensor_scalar(
                        fused_s[par][:, qh * D + n * 512:qh * D + (n + 1) * 512],
                        pp[:, 0:512], rsum_s[:, 2 * par + qh:2 * par + qh + 1],
                        None, ALU.mult)),
                        reads=[ppb, rsum_b], writes=[fused_b[par]],
                        tag="l0 av evac")
            for qh in range(2):
                p.emit_group("tensor", [
                    (lambda h, qh=qh, c=c, par=par, pt=pt: h.transpose(
                        pt[:, ts(c, 128)],
                        fused_s[par][:, qh * D + c * 128:qh * D + (c + 1) * 128],
                        ident[:, :]))
                    for c in range(NCHUNK)],
                    reads=[fused_b[par], identb], writes=[ptb], tag="l0 fT")
                p.emit("vector", (lambda h, qh=qh, gs=gs, par=par,
                                  pt=pt: h.tensor_tensor(
                    fusedT_s[par].rearrange("p (c q) -> p c q", c=NCHUNK)[:, :, ts(qh, 128)],
                    pt.rearrange("p (c q) -> p c q", c=NCHUNK),
                    goutT_s.rearrange("p (c g) -> p c g", c=NCHUNK)[:, :, gs:gs + 1].broadcast_to((128, NCHUNK, 128)),
                    op=ALU.add)),
                    reads=[ptb, goutT_b], writes=[fusedT_b[par]],
                    tag="l0 fT evac")
            for th in range(2):
                t = 2 * s + th
                for n in range(2):
                    pp, ppb = (pA, pAb) if n == 0 else (pB, pBb)
                    builds = [
                        (lambda h, th=th, n=n, c=c, pp=pp, par=par: h.matmul(
                            pp[:, 0:512],
                            fusedT_s[par].rearrange("p (c q) -> p c q", c=NCHUNK)[:, c, ts(th, 128)],
                            wc_s[:, c * D + n * 512:c * D + (n + 1) * 512],
                            start=(c == 0), stop=False))
                        for c in range(NCHUNK)]
                    builds.append(lambda h, n=n, pp=pp: h.matmul(
                        pp[:, 0:512], ones1_s[0:1, :],
                        biasbf_s[0:1, 3 * D + n * 512:3 * D + (n + 1) * 512],
                        start=False, stop=True))
                    p.emit_group("tensor", builds,
                                 reads=[fusedT_b[par], wc_b, ones1_b, biasbf_b],
                                 writes=[ppb], tag="l0 wc mm")
                    p.emit("vector", (lambda h, th=th, n=n, pp=pp, par=par:
                                      h.tensor_copy(
                        xo_s[par][:, th * D + n * 512:th * D + (n + 1) * 512],
                        pp[:, 0:512])),
                        reads=[ppb], writes=[xo_b[par]], tag="l0 y evac")
                ln_stats_acc(ctx, xo_s[par][:, ts(th, D)], xo_b[par],
                             stat_s[par][:, 8 * th:8 * th + 8], stat_b[par],
                             pref="l0y ")
                pending.append(l0_tail(t, par, th))
    for fn, _ in pending:
        fn()
    for _, g in pending:
        g()
    pending = []




HID = 3072
NHC = HID // 128


def build_l0ffn(ctx, x_in, x_out, wgT, wuT, wdT, grid_ext=None):
    nc, p = ctx.nc, ctx.p
    ctx.store_toks = []

    wg_s, _wgb0 = ctx.sbuf("f0_wg", [128, NCHUNK * HID], BF16)
    wu_s, _wub0 = ctx.sbuf("f0_wu", [128, NCHUNK * HID], BF16)
    wd_s, _wdb0 = ctx.sbuf("f0_wd", [128, NHC * D], BF16)
    wg_b = [Buf("f0_wg%d" % n) for n in range(HID // 512)]
    wu_b = [Buf("f0_wu%d" % n) for n in range(HID // 512)]
    wd_b = [Buf("f0_wd%d" % n) for n in range(2)]
    x_s, x_b = zip(*[ctx.sbuf("f0_x%d" % i, [128, D], F32) for i in range(2)])
    stat_s, stat_b = zip(*[ctx.sbuf("f0_st%d" % i, [128, 8], F32) for i in range(2)])
    xn_s, xn_b = zip(*[ctx.sbuf("f0_xn%d" % i, [128, D], BF16) for i in range(2)])
    xnT_s, xnT_b = zip(*[ctx.sbuf("f0_xnT%d" % i, [128, D], BF16) for i in range(2)])
    g_s, g_b = zip(*[ctx.sbuf("f0_g%d" % i, [128, 512], BF16) for i in range(2)])
    h_s, _hb0 = zip(*[ctx.sbuf("f0_h%d" % i, [128, HID], BF16) for i in range(2)])
    hb = [[Buf("f0_h%d_%d" % (i, n)) for n in range(HID // 512)]
          for i in range(2)]
    hT_s, _hTb0 = zip(*[ctx.sbuf("f0_hT%d" % i, [128, HID], BF16)
                        for i in range(2)])
    hTg = [[Buf("f0_hTg%d_%d" % (i, g)) for g in range(3)] for i in range(2)]
    lny_s, lny_b = zip(*([ctx.sbuf("f0_lny", [128, D], BF16)] * 2))
    xr_s, xr_b = zip(*([ctx.sbuf("f0_xr", [128, D], F32)] * 2))
    xo_s, xo_b = zip(*[ctx.sbuf("f0_xo%d" % i, [128, D], F32) for i in range(2)])

    P = [ctx.psum("f0_P%d" % i, [128, 512], F32) for i in range(6)]
    PT, PTb = ctx.psum("f0_PT", [128, 1024], BF16)
    PT2, PT2b = ctx.psum("f0_PT2", [128, 1024], BF16)

    ident = ctx.ident_s
    identb = ctx.ident_b
    inv127 = ctx.cconst(1.0 / 127.0)

    for n in range(HID // 512):
        for c in range(NCHUNK):
            p.dma("gpsimd", wg_s[:, c * HID + n * 512:c * HID + (n + 1) * 512],
                  wgT[ts(c, 128), n * 512:(n + 1) * 512], writes=[wg_b[n]],
                  tag="f0 wg")
            p.dma("gpsimd", wu_s[:, c * HID + n * 512:c * HID + (n + 1) * 512],
                  wuT[ts(c, 128), n * 512:(n + 1) * 512], writes=[wu_b[n]],
                  tag="f0 wu")
    for n in range(2):
        for c in range(NHC):
            p.dma("gpsimd", wd_s[:, c * D + n * 512:c * D + (n + 1) * 512],
                  wdT[ts(c, 128), n * 512:(n + 1) * 512], writes=[wd_b[n]],
                  tag="f0 wd")

    if grid_ext is None:
        grid_s, grid_b = ctx.sbuf("f0_grid", [128, 192], F32)
        ln_prepass(ctx, [(x_in[ts(t, 128), :], None) for t in range(NT)],
                   grid_s, grid_b, [x_s[0][:, :], x_s[1][:, :]],
                   [x_b[0], x_b[1]], pref="f0 ")
    else:
        grid_s, grid_b = grid_ext
        ln_prepass_finish(ctx, NT, grid_s, grid_b, grid_b, pref="f0 ")

    def f0_fetch(t):
        par = t % 2
        p.dma("sync", x_s[par][:, :], x_in[ts(t, 128), :], writes=[x_b[par]],
              tag="f0 x")

    def f0_ln(t):
        par = t % 2
        p.emit("vector", (lambda h, par=par, t=t: h.tensor_scalar(
            xn_s[par][:, :], x_s[par][:, :], grid_s[:, 128 + t:129 + t],
            grid_s[:, 160 + t:161 + t], ALU.mult, ALU.add)),
            reads=[x_b[par], grid_b], writes=[xn_b[par]], tag="f0 xn")

    def f0_xnT(t):
        par = t % 2
        p.emit_group("tensor", [
            (lambda h, c=c, par=par: h.transpose(
                PT[:, ts(c, 128)], xn_s[par][:, ts(c, 128)], ident[:, :]))
            for c in range(NCHUNK)],
            reads=[xn_b[par], identb], writes=[PTb], tag="f0 xnT")
        p.emit("vector", (lambda h, par=par: h.tensor_copy(
            xnT_s[par][:, :], PT[:, :])),
            reads=[PTb], writes=[xnT_b[par]], tag="f0 xnT evac")

    def emit_hT(par, g):
        p.emit_group("tensor", [
            (lambda h, g=g, c=c, par=par: h.transpose(
                PT2[:, ts(c, 128)],
                h_s[par][:, g * 1024 + c * 128:g * 1024 + (c + 1) * 128],
                ident[:, :]))
            for c in range(NCHUNK)],
            reads=[hb[par][2 * g], hb[par][2 * g + 1], identb], writes=[PT2b],
            tag="f0 hT")
        p.emit("vector", (lambda h, g=g, par=par: h.tensor_copy(
            hT_s[par][:, ts(g, 1024)], PT2[:, :])),
            reads=[PT2b], writes=[hTg[par][g]], tag="f0 hT evac")

    def emit_down_sub(par, n, g, pp, ppb):
        p.emit_group("tensor", [
            (lambda h, n=n, c=c, par=par, pp=pp: h.matmul(
                pp[:, :], hT_s[par][:, ts(c, 128)],
                wd_s[:, c * D + n * 512:c * D + (n + 1) * 512],
                start=(c == 0), stop=(c == NHC - 1)))
            for c in range(8 * g, 8 * g + 8)],
            reads=[hTg[par][g], wd_b[n]], writes=[ppb], tag="f0 d mm")

    def emit_downs(t):
        par = t % 2
        pC0, pC0b = P[3 * par + 2]
        pC1, pC1b = P[3 * (1 - par) + 2]
        emit_down_sub(par, 0, 0, pC0, pC0b)
        emit_down_sub(par, 0, 1, pC0, pC0b)
        emit_down_sub(par, 1, 0, pC1, pC1b)
        emit_down_sub(par, 1, 1, pC1, pC1b)
        emit_down_sub(par, 0, 2, pC0, pC0b)
        emit_down_sub(par, 1, 2, pC1, pC1b)

    def emit_down_evacs(t):
        par = t % 2
        pC0, pC0b = P[3 * par + 2]
        pC1, pC1b = P[3 * (1 - par) + 2]
        p.emit("vector", (lambda h, par=par, pC0=pC0: h.tensor_copy(
            xo_s[par][:, 0:512], pC0[:, :])),
            reads=[pC0b], writes=[xo_b[par]], tag="f0 y evac")
        p.emit("vector", (lambda h, par=par, pC1=pC1: h.tensor_copy(
            xo_s[par][:, 512:1024], pC1[:, :])),
            reads=[pC1b], writes=[xo_b[par]], tag="f0 y evac")

    f0_tail = f0_tail_mk(ctx, x_in, x_out, xo_s, xo_b, stat_s, stat_b,
                         lny_s, lny_b, xr_s, xr_b)
    dn_pend = [None]
    f0_fetch(0)
    f0_ln(0)
    f0_xnT(0)
    f0_fetch(1)
    for t in range(NT):
        par = t % 2
        pB, pBb = P[3 * par + 1]
        for n in range(HID // 512):
            pA, pAb = P[3 * par] if n % 2 == 0 else P[3 * (1 - par)]
            p.emit_group("tensor", [
                (lambda h, n=n, c=c, par=par, pA=pA: h.matmul(
                    pA[:, :], xnT_s[par][:, ts(c, 128)],
                    wg_s[:, c * HID + n * 512:c * HID + (n + 1) * 512],
                    start=(c == 0), stop=(c == NCHUNK - 1)))
                for c in range(NCHUNK)],
                reads=[xnT_b[par], wg_b[n]], writes=[pAb], tag="f0 g mm")
            p.emit("scalar", (lambda h, n=n, par=par, pA=pA: h.activation(
                g_s[par][:, 0:512], pA[:, :], AF.Silu, scale=inv127)),
                reads=[pAb, ctx._ccb], writes=[g_b[par]], tag="f0 silu")
            p.emit_group("tensor", [
                (lambda h, n=n, c=c, par=par, pB=pB: h.matmul(
                    pB[:, :], xnT_s[par][:, ts(c, 128)],
                    wu_s[:, c * HID + n * 512:c * HID + (n + 1) * 512],
                    start=(c == 0), stop=(c == NCHUNK - 1)))
                for c in range(NCHUNK)],
                reads=[xnT_b[par], wu_b[n]], writes=[pBb], tag="f0 u mm")
            p.emit("vector", (lambda h, n=n, par=par, pB=pB: h.tensor_tensor(
                h_s[par][:, ts(n, 512)], g_s[par][:, 0:512],
                pB[:, :], op=ALU.mult)),
                reads=[g_b[par], pBb], writes=[hb[par][n]], tag="f0 h")
            if n == 2:
                emit_hT(par, 0)
            if n == 4:
                emit_hT(par, 1)
                if t + 1 < NT:
                    f0_ln(t + 1)
                if dn_pend[0] is not None:
                    dn_pend[0]()
                    dn_pend[0] = None
        if t > 0:
            emit_downs(t - 1)
        if t + 1 < NT:
            f0_xnT(t + 1)
        if t + 2 < NT:
            f0_fetch(t + 2)
        emit_hT(par, 2)

        if t > 0:
            def mk_dn(td=t - 1):
                def run():
                    pd = td % 2
                    emit_down_evacs(td)
                    ln_stats_acc_v(ctx, xo_s[pd][:, :], xo_b[pd],
                                   stat_s[pd], stat_b[pd], pref="f0y ")
                    f0_tail(td)
                return run
            dn_pend[0] = mk_dn()
    dn_pend[0]()
    emit_downs(NT - 1)
    par = (NT - 1) % 2
    emit_down_evacs(NT - 1)
    ln_stats_acc_v(ctx, xo_s[par][:, :], xo_b[par], stat_s[par], stat_b[par],
                   pref="f0y ")
    f0_tail(NT - 1)


def f0_tail_mk(ctx, x_in, x_out, xo_s, xo_b, stat_s, stat_b, lny_s, lny_b,
               xr_s, xr_b):
    p = ctx.p

    def f0_tail(t):
        par = t % 2
        ln_stats_fin(ctx, stat_s[par], stat_b[par], eps_scale=127.0 ** 4,
                     pref="f0y ")
        p.emit("vector", (lambda h, par=par: h.tensor_scalar(
            lny_s[par][:, :], xo_s[par][:, :], stat_s[par][:, 4:5],
            stat_s[par][:, 5:6], ALU.mult, ALU.add)),
            reads=[xo_b[par], stat_b[par]], writes=[lny_b[par]], tag="f0 lny")
        p.dma("sync", xr_s[par][:, :], x_in[ts(t, 128), :],
              writes=[xr_b[par]], tag="f0 xre")
        p.emit("vector", (lambda h, par=par: h.scalar_tensor_tensor(
            xo_s[par][:, :], lny_s[par][:, :], 1.0, xr_s[par][:, :],
            ALU.mult, ALU.add)),
            reads=[lny_b[par], xr_b[par]], writes=[xo_b[par]], tag="f0 resid")
        ctx.store_toks.append(p.dma(
            "scalar", x_out[ts(t, 128), :], xo_s[par][:, :], reads=[xo_b[par]],
            tag="f0 xstore"))
        grid_acc_v(ctx, xo_s[par][:, :], xo_b[par], t, "f0")
        if ctx.on_store:
            ctx.on_store(len(ctx.store_toks) - 1, ctx.store_toks[-1])

    return f0_tail


def build_program_l0ffn():
    nc = bass.Bass("TRN2", target_bir_lowering=False, debug=False)
    x_in = nc.declare_dram_parameter("x_in", [TOK, D], F32, isOutput=False)
    wgT = nc.declare_dram_parameter("wgT", [D, HID], I32, isOutput=False)
    wuT = nc.declare_dram_parameter("wuT", [D, HID], I32, isOutput=False)
    wdT = nc.declare_dram_parameter("wdT", [HID, D], I32, isOutput=False)
    ident = nc.declare_dram_parameter("ident", [128, 128], BF16, isOutput=False)
    x_out = nc.declare_dram_parameter("x_out", [TOK, D], F32, isOutput=True)

    es = contextlib.ExitStack()
    with es:
        prog = Prog(nc)
        ctx = Ctx(nc, prog, es)
        ctx.ident_s, ctx.ident_b = ctx.sbuf("ident_s", [128, 128], BF16)
        tr, trb = ctx.sbuf("trash", [128, D], BF16)
        ctx.trash_s = tr
        ctx.trashb = trb
        prog.dma("sync", ctx.ident_s[:, :], ident.ap(), writes=[ctx.ident_b])
        build_l0ffn(ctx, x_in.ap(), x_out.ap(), wgT.ap(), wuT.ap(), wdT.ap())
        prog.barrier()
        prog.realize(es)
    return nc



NKT = 18
KLO = 2304
NH = 16
NLH = 8
HD = 64


def _win(ktk):
    lo = 0
    hi = 384
    if ktk == 0:
        lo = 256
    elif ktk == 1:
        lo = 128
    if ktk == NKT - 2:
        hi = 256
    elif ktk == NKT - 1:
        hi = 128
    return lo, hi


def build_l1attn(ctx, x_in, xhalo, xother, x_out, wq, wk, wv, wgq, wgk, wgv,
                 r_out_w, glgrw, w_out, locmask, halomask, lmask, dbg_aps=None,
                 cc=None, grid_ext=None):
    nc, p = ctx.nc, ctx.p
    ctx.store_toks = []
    ident = ctx.ident_s
    identb = ctx.ident_b
    eighth = ctx.cconst(0.125)

    pooledT_s, pooledT_b = ctx.sbuf("a1_pooledT", [128, NCHUNK * 64], BF16)
    gates_s, gates_b = ctx.sbuf("a1_gates", [128, NT * 32], F32)
    local_s, local_b = ctx.sbuf("a1_local", [128, NT * D], BF16)
    rem_s, rem_b = ctx.sbuf("a1_rem", [128, NT * 512], BF16)
    lmask_s, lmask_b = ctx.sbuf("a1_lmk", [64, TOK], BF16)
    locm_s, locm_b = ctx.sbuf("a1_locm", [128, 2 * 384], BF16)

    PS = [ctx.psum("a1_P%d" % i, [128, 512], F32) for i in range(6)]
    PT, PTb = ctx.psum("a1_PT", [128, 1024], BF16)
    PT2, PT2b = ctx.psum("a1_PT2", [128, 1024], BF16)

    p.dma("gpsimd", lmask_s[:, :], lmask, writes=[lmask_b], tag="a1 lmask")
    p.dma("gpsimd", locm_s[:, 0:384], locmask, writes=[locm_b], tag="a1 locm")
    p.dma("gpsimd", locm_s[:, 384:768], halomask, writes=[locm_b], tag="a1 hmask")

    with ctx.scope():
        xnT_s, xnT_b = ctx.sbuf("a1_xnT", [128, NCHUNK * KLO], BF16)

        if cc is not None and not cc.get("tail_pre"):
            p.dma("gpsimd", cc["tail_in"], x_in[TOK - 256:TOK, :],
                  waits=[cc["x_ready"]], writes=[cc["tail_in_b"]], tag="cc tailw")
            p.collective("AllGather", ALU.bypass, cc["groups"],
                         [cc["tail_in"]], [cc["tail_out"]],
                         reads=[cc["tail_in_b"]], writes=[cc["tail_out_b"]],
                         sem_name=cc["sem_tail"], tag="cc tail")

        with ctx.scope():
            x_s, x_b = zip(*[ctx.sbuf("a1_x%d" % i, [128, D], F32) for i in range(2)])
            stat_s, stat_b = zip(*[ctx.sbuf("a1_st%d" % i, [128, 8], F32) for i in range(2)])
            xn_s, xn_b = zip(*[ctx.sbuf("a1_xn%d" % i, [128, D], BF16) for i in range(2)])
            poolw_s, poolw_b = ctx.sbuf("a1_plw", [128, 2], BF16)
            pstage_s, pstage_b = ctx.sbuf("a1_pstage", [128, NCHUNK * (32 if cc is not None else 64)], BF16)
            p.dma("gpsimd", poolw_s[:, :], ctx.consts["poolw"], writes=[poolw_b],
                  tag="a1 poolw")

            if grid_ext is not None:
                grid_s, grid_b = grid_ext
                p.dma("gpsimd", cc["ghalo_in"][:, 0:2], grid_s[:, 14:16],
                      reads=[grid_b], writes=[cc["ghalo_in_b"]], tag="cc ghw1")
                p.dma("gpsimd", cc["ghalo_in"][:, 2:4], grid_s[:, 46:48],
                      reads=[grid_b], writes=[cc["ghalo_in_b"]], tag="cc ghw2")
                p.collective("AllGather", ALU.bypass, cc["groups"],
                             [cc["ghalo_in"]], [cc["ghalo_out"]],
                             reads=[cc["ghalo_in_b"]],
                             writes=[cc["ghalo_out_b"]],
                             sem_name=cc["sem_ghalo"], tag="cc ghalo")
                p.dma("sync", grid_s[:, 16:18], cc["ghalo_out"][0:128, 0:2],
                      waits=[(cc["sem_ghalo"], 1)], writes=[grid_b],
                      tag="a1 ghr1")
                p.dma("sync", grid_s[:, 48:50], cc["ghalo_out"][0:128, 2:4],
                      waits=[(cc["sem_ghalo"], 1)], writes=[grid_b],
                      tag="a1 ghr2")
                ln_prepass_finish(ctx, NT + 2, grid_s, grid_b, grid_b,
                                  pref="a1 ")
            else:
                grid_s, grid_b = ctx.sbuf("a1_grid", [128, 192], F32)
                if cc is not None:
                    psrcs = [(x_in[ts(t, 128), :], None) for t in range(NT)]
                    psrcs += [(cc["tail_out"][th * 128:(th + 1) * 128, :],
                               [cc["tail_tok"]]) for th in range(2)]
                    ln_prepass(ctx, psrcs, grid_s, grid_b,
                               [x_s[0][:, :], x_s[1][:, :]],
                               [x_b[0], x_b[1]], pref="a1 ")

            def ln_tile(par, src_dram_slice, tag, extra_waits=None, gcol=None):
                p.dma("sync", x_s[par][:, :], src_dram_slice, writes=[x_b[par]],
                      waits=extra_waits, tag=tag + " x")
                if gcol is None:
                    ln_stats(ctx, x_s[par][:, :], x_b[par], stat_s[par],
                             stat_b[par], pref=tag + " ")
                    sc1 = stat_s[par][:, 4:5]
                    sc2 = stat_s[par][:, 5:6]
                    rb = [stat_b[par]]
                else:
                    sc1 = grid_s[:, 128 + gcol:129 + gcol]
                    sc2 = grid_s[:, 160 + gcol:161 + gcol]
                    rb = [grid_b]
                p.emit("vector", (lambda h, par=par, sc1=sc1, sc2=sc2:
                                  h.tensor_scalar(
                    xn_s[par][:, :], x_s[par][:, :], sc1, sc2,
                    ALU.mult, ALU.add)),
                    reads=[x_b[par]] + rb, writes=[xn_b[par]],
                    tag=tag + " xn")

            def xnT_tile(par, colbase, tag):
                pt, ptb = (PT, PTb) if par == 0 else (PT2, PT2b)
                p.emit_group("tensor", [
                    (lambda h, par=par, c=c, pt=pt: h.transpose(
                        pt[:, ts(c, 128)], xn_s[par][:, ts(c, 128)], ident[:, :]))
                    for c in range(NCHUNK)],
                    reads=[xn_b[par], identb], writes=[ptb], tag=tag + " xnT")
                p.emit("vector", (lambda h, colbase=colbase, pt=pt: h.tensor_copy(
                    xnT_s.rearrange("p (c q) -> p c q", c=NCHUNK)[:, :, colbase:colbase + 128],
                    pt.rearrange("p (c q) -> p c q", c=NCHUNK)[:, :, :])),
                    reads=[ptb], writes=[xnT_b], tag=tag + " xnT ev")

            def pool_tile(par, poolcol, tag):
                pp, ppb = PS[4 + par]
                for fs in range(NCHUNK):
                    p.emit("tensor", (lambda h, par=par, fs=fs, pp=pp: h.matmul(
                        pp[:, fs * 2:fs * 2 + 2], xn_s[par][:, ts(fs, 128)],
                        poolw_s[:, :], start=True, stop=True)),
                        reads=[xn_b[par], poolw_b], writes=[ppb], tag=tag + " pool")
                p.emit("vector", (lambda h, poolcol=poolcol, pp=pp: h.tensor_copy(
                    pstage_s.rearrange("p (c n) -> p c n", c=NCHUNK)[:, :, poolcol:poolcol + 2],
                    pp.rearrange("p (c n) -> p c n", c=256)[:, 0:NCHUNK, :])),
                    reads=[ppb], writes=[pstage_b], tag=tag + " poolev")

            if cc is None:
                for t in range(NT):
                    ln_tile(t % 2, x_in[ts(t, 128), :], "a1")
                    xnT_tile(t % 2, 256 + t * 128, "a1")
                    pool_tile(t % 2, 2 * t, "a1")
                for th in range(2):
                    ln_tile(th, xhalo[ts(th, 128), :], "a1h")
                    xnT_tile(th, th * 128, "a1h")
                for t in range(NT):
                    ln_tile(t % 2, xother[ts(t, 128), :], "a1o")
                    pool_tile(t % 2, 32 + 2 * t, "a1o")
                for c in range(NCHUNK):
                    p.emit("vector", (lambda h, c=c: h.tensor_copy(
                        pooledT_s.rearrange("p (c n) -> p c n", c=NCHUNK)[:, c, :],
                        pstage_s.rearrange("p (c n) -> p c n", c=NCHUNK)[:, c, :])),
                        reads=[pstage_b], writes=[pooledT_b], tag="a1 poolcp")
            else:
                for t in range(NT):
                    ln_tile(t % 2, x_in[ts(t, 128), :], "a1", gcol=t)
                    xnT_tile(t % 2, 256 + t * 128, "a1")
                    pool_tile(t % 2, 2 * t, "a1")
                for th in range(2):
                    ln_tile(th, cc["tail_out"][th * 128:(th + 1) * 128, :], "a1h",
                            extra_waits=[cc["tail_tok"]], gcol=NT + th)
                    xnT_tile(th, th * 128, "a1h")
                p.dma("gpsimd", cc["pool_in"], pstage_s[:, :],
                      reads=[pstage_b], writes=[cc["pool_in_b"]], tag="cc poolw")
                p.collective("AllGather", ALU.bypass, cc["groups"],
                             [cc["pool_in"]], [cc["pool_out"]],
                             reads=[cc["pool_in_b"]], writes=[cc["pool_out_b"]],
                             sem_name=cc["sem_pool"], tag="cc pool")
                for rnk in range(2):
                    p.dma("sync",
                          pooledT_s.rearrange("p (c n) -> p c n", c=NCHUNK)[:, :, 32 * rnk:32 * rnk + 32],
                          cc["pool_out"][rnk * 128:(rnk + 1) * 128, :].rearrange("p (c n) -> p c n", c=NCHUNK),
                          waits=[(cc["sem_pool"], 1)], writes=[pooledT_b],
                          tag="cc poolr")


        with ctx.scope():
            gqTe_s, gqTe_b = ctx.sbuf("a1_gqTe", [128, TOK], BF16)
            gkTe_s, gkTe_b = ctx.sbuf("a1_gkTe", [128, 64], BF16)
            gvp_s, gvp_b = ctx.sbuf("a1_gvp", [64, 130], BF16)
            wsg3_s, wsg3_b = zip(*[ctx.sbuf("a1_wsg%d" % i, [128, 3 * D], BF16)
                                   for i in range(2)])
            lpT_s, lpT_b = ctx.sbuf("a1_lpT", [64, TOK], BF16)
            rsum_s, rsum_b = ctx.sbuf("a1_rsum", [128, 2], F32)

            def load_wsg(e, par):
                for c in range(NCHUNK):
                    p.dma("gpsimd", wsg3_s[par][:, ts(c, 128)],
                          wgq[ts(c, 128), ts(e, 128)], writes=[wsg3_b[par]],
                          tag="a1 wsgq")
                    p.dma("gpsimd",
                          wsg3_s[par][:, D + c * 128:D + (c + 1) * 128],
                          wgk[ts(c, 128), ts(e, 128)], writes=[wsg3_b[par]],
                          tag="a1 wsgk")
                    p.dma("gpsimd",
                          wsg3_s[par][:, 2 * D + c * 128:2 * D + (c + 1) * 128],
                          wgv[ts(c, 128), ts(e, 128)], writes=[wsg3_b[par]],
                          tag="a1 wsgv")

            load_wsg(0, 0)
            for e in range(4):
                wsg_s, wsg_b = wsg3_s[e % 2], wsg3_b[e % 2]
                if e + 1 < 4:
                    load_wsg(e + 1, (e + 1) % 2)
                for g in range(TOK // 512):
                    pa, pab = PS[g % 2]
                    p.emit_group("tensor", [
                        (lambda h, g=g, c=c, pa=pa, wsg_s=wsg_s: h.matmul(
                            pa[:, :], wsg_s[:, ts(c, 128)],
                            xnT_s[:, c * KLO + 256 + g * 512:c * KLO + 256 + (g + 1) * 512],
                            start=(c == 0), stop=(c == NCHUNK - 1)))
                        for c in range(NCHUNK)],
                        reads=[wsg_b, xnT_b], writes=[pab], tag="a1 gq mm")
                    p.emit("vector", (lambda h, g=g, pa=pa: h.tensor_copy(
                        gqTe_s[:, ts(g, 512)], pa[:, :])),
                        reads=[pab], writes=[gqTe_b], tag="a1 gq ev")
                pb_, pbb_ = PS[2]
                p.emit_group("tensor", [
                    (lambda h, c=c, pb_=pb_, wsg_s=wsg_s: h.matmul(
                        pb_[:, 0:64], wsg_s[:, D + c * 128:D + (c + 1) * 128],
                        pooledT_s[:, ts(c, 64)], start=(c == 0),
                        stop=(c == NCHUNK - 1)))
                    for c in range(NCHUNK)],
                    reads=[wsg_b, pooledT_b], writes=[pbb_], tag="a1 gk mm")
                p.emit("vector", (lambda h, pb_=pb_: h.tensor_copy(
                    gkTe_s[:, 0:64], pb_[:, 0:64])),
                    reads=[pbb_], writes=[gkTe_b], tag="a1 gk ev")
                p.emit("vector", lambda h: h.memset(
                    gvp_s[0:64, 64:65], 1.0), writes=[gvp_b], tag="a1 gv ones")
                p.emit("vector", lambda h: h.memset(
                    gvp_s[0:64, 129:130], 1.0), writes=[gvp_b], tag="a1 gv ones2")
                pc_, pcb_ = PS[3]
                p.emit_group("tensor", [
                    (lambda h, c=c, pc_=pc_, wsg_s=wsg_s: h.matmul(
                        pc_[0:64, 0:128], pooledT_s[:, ts(c, 64)],
                        wsg_s[:, 2 * D + c * 128:2 * D + (c + 1) * 128],
                        start=(c == 0), stop=(c == NCHUNK - 1)))
                    for c in range(NCHUNK)],
                    reads=[pooledT_b, wsg_b], writes=[pcb_], tag="a1 gv mm")
                p.emit("vector", (lambda h, pc_=pc_: h.tensor_copy(
                    gvp_s.rearrange("p (u e) -> p u e", u=2)[0:64, :, 0:64],
                    pc_[0:64, 0:128].rearrange("p (u e) -> p u e", u=2))),
                    reads=[pcb_], writes=[gvp_b], tag="a1 gv ev")
                for sub in range(2):
                    lh = 2 * e + sub
                    rr = sub * 64
                    for g in range(TOK // 512):
                        pe_, peb_ = PS[g % 2]
                        p.emit("tensor", (lambda h, g=g, rr=rr, pe_=pe_: h.matmul(
                            pe_[0:64, :], gkTe_s[rr:rr + 64, 0:64],
                            gqTe_s[rr:rr + 64, ts(g, 512)],
                            start=True, stop=True)),
                            reads=[gkTe_b, gqTe_b], writes=[peb_], tag="a1 lsc")
                        p.emit("scalar", (lambda h, g=g, pe_=pe_: h.activation(
                            ctx.trash_s[0:64, ts(g % 2, 512)], pe_[0:64, :],
                            AF.Exp, scale=eighth[0:64, :])),
                            reads=[peb_, ctx._ccb], writes=[ctx.trashb],
                            tag="a1 lexp")
                        p.emit("vector", (lambda h, g=g: h.tensor_tensor(
                            lpT_s[0:64, ts(g, 512)],
                            ctx.trash_s[0:64, ts(g % 2, 512)],
                            lmask_s[0:64, ts(g, 512)], op=ALU.mult)),
                            reads=[ctx.trashb, lmask_b], writes=[lpT_b],
                            tag="a1 lpmask")
                    for qt in range(NT):
                        pf_, pfb_ = PS[2 + qt % 2]
                        p.emit("tensor", (lambda h, qt=qt, sub=sub, pf_=pf_: h.matmul(
                            pf_[:, 0:65], lpT_s[0:64, ts(qt, 128)],
                            gvp_s[0:64, sub * 65:(sub + 1) * 65],
                            start=True, stop=True)),
                            reads=[lpT_b, gvp_b], writes=[pfb_], tag="a1 rv mm")
                        p.emit("vector", (lambda h, qt=qt, pf_=pf_: h.reciprocal(
                            rsum_s[:, qt % 2:qt % 2 + 1], pf_[:, 64:65])),
                            reads=[pfb_], writes=[rsum_b], tag="a1 rrecip")
                        p.emit("vector", (lambda h, lh=lh, qt=qt, pf_=pf_: h.tensor_scalar(
                            rem_s[:, qt * 512 + lh * 64:qt * 512 + (lh + 1) * 64],
                            pf_[:, 0:64], rsum_s[:, qt % 2:qt % 2 + 1], None,
                            ALU.mult)),
                            reads=[pfb_, rsum_b], writes=[rem_b],
                            tag="a1 rev")

        with ctx.scope():
            wq_s, wq_b = ctx.sbuf("a1_wqf", [128, NCHUNK * D], BF16)
            wk_s, wk_b = ctx.sbuf("a1_wkf", [128, NCHUNK * D], BF16)
            wvh_s, wvh_b = ctx.sbuf("a1_wvh", [128, NCHUNK * 512], BF16)
            v_s, v_b = ctx.sbuf("a1_vf", [128, NKT * (NH * 65)], BF16)
            qTe_s, qTe_b = ctx.sbuf("a1_qTe", [128, TOK], BF16)
            kTe_s, kTe_b = ctx.sbuf("a1_kTe", [128, KLO], BF16)
            gwe_s, gwe_b = ctx.sbuf("a1_gwe", [128, NCHUNK * 4], BF16)
            pTl_s, pTl_b = ctx.sbuf("a1_pTl", [128, NKT * 384], BF16)
            lsum_s, lsum_b = ctx.sbuf("a1_lsum", [128, 4], F32)
            esc_s, esc_b = ctx.sbuf("a1_esc", [128, 2 * 384], BF16)
            eslot = [esc_s[:, 0:384], esc_s[:, 384:768],
                     ctx.trash_s[:, 0:384], ctx.trash_s[:, 384:768]]
            eslot_full = [esc_s, esc_s, ctx.trash_s, ctx.trash_s]

            for c in range(NCHUNK):
                p.dma("sync", wq_s[:, ts(c, D)], wq[ts(c, 128), :],
                      writes=[wq_b], tag="a1 wq")
                p.dma("sync", wk_s[:, ts(c, D)], wk[ts(c, 128), :],
                      writes=[wk_b], tag="a1 wk")
                p.dma("gpsimd", gwe_s[:, ts(c, 4)], glgrw[ts(c, 128), :],
                      writes=[gwe_b], tag="a1 gwe")
            p.emit("vector", lambda h: h.memset(
                v_s.rearrange("p (k u e) -> p k u e", k=NKT, u=NH)[:, :, :, 64:65],
                1.0), writes=[v_b], tag="a1 v ones")
            for n in range(2):
                for c in range(NCHUNK):
                    p.dma("sync", wvh_s[:, ts(c, 512)],
                          wv[ts(c, 128), n * 512:(n + 1) * 512], writes=[wvh_b],
                          tag="a1 wv")
                for kt in range(NKT):
                    pa, pab = PS[kt % 2]
                    p.emit_group("tensor", [
                        (lambda h, kt=kt, c=c, pa=pa: h.matmul(
                            pa[:, :],
                            xnT_s[:, c * KLO + kt * 128:c * KLO + (kt + 1) * 128],
                            wvh_s[:, ts(c, 512)],
                            start=(c == 0), stop=(c == NCHUNK - 1)))
                        for c in range(NCHUNK)],
                        reads=[xnT_b, wvh_b], writes=[pab], tag="a1 v mm")
                    p.emit("vector", (lambda h, kt=kt, n=n, pa=pa: h.tensor_copy(
                        v_s.rearrange("p (k u e) -> p k u e", k=NKT, u=NH)[:, kt, n * 8:(n + 1) * 8, 0:64],
                        pa[:, :].rearrange("p (u e) -> p u e", u=8))),
                        reads=[pab], writes=[v_b], tag="a1 v ev")

            for e in range(NCHUNK):
                for g in range(TOK // 512):
                    pa, pab = PS[g % 2]
                    p.emit_group("tensor", [
                        (lambda h, e=e, g=g, c=c, pa=pa: h.matmul(
                            pa[:, :], wq_s[:, c * D + e * 128:c * D + (e + 1) * 128],
                            xnT_s[:, c * KLO + 256 + g * 512:c * KLO + 256 + (g + 1) * 512],
                            start=(c == 0), stop=(c == NCHUNK - 1)))
                        for c in range(NCHUNK)],
                        reads=[wq_b, xnT_b], writes=[pab], tag="a1 q mm")
                    p.emit("vector", (lambda h, g=g, pa=pa: h.tensor_copy(
                        qTe_s[:, ts(g, 512)], pa[:, :])),
                        reads=[pab], writes=[qTe_b], tag="a1 q ev")
                kgroups = [(g * 512, 512) for g in range(4)] + [(2048, 256)]
                for gi, (off, nn_) in enumerate(kgroups):
                    pa, pab = PS[2 + gi % 2]
                    p.emit_group("tensor", [
                        (lambda h, e=e, off=off, nn_=nn_, c=c, pa=pa: h.matmul(
                            pa[:, 0:nn_], wk_s[:, c * D + e * 128:c * D + (e + 1) * 128],
                            xnT_s[:, c * KLO + off:c * KLO + off + nn_],
                            start=(c == 0), stop=(c == NCHUNK - 1)))
                        for c in range(NCHUNK)],
                        reads=[wk_b, xnT_b], writes=[pab], tag="a1 k mm")
                    p.emit("vector", (lambda h, off=off, nn_=nn_, pa=pa: h.tensor_copy(
                        kTe_s[:, off:off + nn_], pa[:, 0:nn_])),
                        reads=[pab], writes=[kTe_b], tag="a1 k ev")
                for t in range(NT):
                    pd_, pdb_ = PS[4 + t % 2]
                    p.emit("tensor", (lambda h, t=t, e=e, pd_=pd_: h.matmul(
                        pd_[:, 0:4], qTe_s[:, ts(t, 128)], gwe_s[:, ts(e, 4)],
                        start=True, stop=True)),
                        reads=[qTe_b, gwe_b], writes=[pdb_], tag="a1 gate mm")
                    p.emit("scalar", (lambda h, t=t, e=e, pd_=pd_: h.activation(
                        gates_s.rearrange("p (t x g) -> p t x g", t=NT, x=2)[:, t, :, 2 * e:2 * e + 2],
                        pd_[:, 0:4].rearrange("p (x g) -> p x g", x=2),
                        AF.Sigmoid)),
                        reads=[pdb_], writes=[gates_b], tag="a1 gate sig")
                for sub in range(2):
                    hh = 2 * e + sub
                    rr = sub * 64
                    esc2_b = [Buf("esc%d" % i) for i in range(4)]
                    prev_pmask = getattr(ctx, "_a1_prev_pmask", None)
                    prev_av = getattr(ctx, "_a1_prev_av", None)
                    pTl2_b = [Buf("pTl%d" % i) for i in range(4)]
                    for kt in range(NKT):
                        lo, hi = _win(kt)
                        nn_ = hi - lo
                        q0 = kt * 128 - 256 + lo
                        sl = kt % 4
                        pe_, peb_ = PS[sl]
                        p.emit("tensor", (lambda h, kt=kt, lo=lo, nn_=nn_, q0=q0,
                                          rr=rr, pe_=pe_: h.matmul(
                            pe_[:, 0:nn_],
                            kTe_s[rr:rr + 64, kt * 128:(kt + 1) * 128],
                            qTe_s[rr:rr + 64, q0:q0 + nn_],
                            start=True, stop=True)),
                            reads=[kTe_b, qTe_b], writes=[peb_], tag="a1 sc mm")
                        p.emit("scalar", (lambda h, kt=kt, nn_=nn_, sl=sl,
                                          pe_=pe_: h.activation(
                            eslot[sl][:, 0:nn_],
                            pe_[:, 0:nn_], AF.Exp, scale=eighth)),
                            reads=[peb_, ctx._ccb], writes=[esc2_b[sl]],
                            waits=[prev_pmask] if kt == 0 else None,
                            tag="a1 exp")
                        mslot = 384 if kt < 2 else 0
                        p.emit("vector", (lambda h, kt=kt, lo=lo, nn_=nn_, sl=sl,
                                          mslot=mslot: h.tensor_tensor(
                            pTl_s[:, kt * 384 + lo:kt * 384 + lo + nn_],
                            eslot[sl][:, 0:nn_],
                            locm_s[:, mslot + lo:mslot + lo + nn_], op=ALU.mult)),
                            reads=[esc2_b[sl], locm_b], writes=[pTl2_b[kt % 4]],
                            waits=[prev_av] if kt == 0 else None,
                            tag="a1 pmask")
                    pjoin = [(b.w[0], b.w[1]) for b in pTl2_b if b.w]
                    ctx._a1_prev_pmask = max(pjoin, key=lambda t: t[1])
                    for qt in range(NT):
                        kts = [qt, qt + 1, qt + 2]
                        offs = [256, 128, 0]
                        pf_, pfb_ = PS[qt % 6]
                        builds = []
                        for i, (kt, off) in enumerate(zip(kts, offs)):
                            builds.append(lambda h, hh=hh, kt=kt, off=off, i=i,
                                          pf_=pf_: h.matmul(
                                pf_[:, 0:65],
                                pTl_s[:, kt * 384 + off:kt * 384 + off + 128],
                                v_s[:, kt * (NH * 65) + hh * 65:kt * (NH * 65) + (hh + 1) * 65],
                                start=(i == 0), stop=(i == 2)))
                        p.emit_group("tensor", builds,
                                     reads=[v_b] + pTl2_b[min(qt, 2) % 4:] if False else [v_b],
                                     waits=pjoin if qt == 0 else None,
                                     writes=[pfb_], tag="a1 av mm")
                        p.emit("vector", (lambda h, qt=qt, pf_=pf_: h.reciprocal(
                            lsum_s[:, qt % 4:qt % 4 + 1], pf_[:, 64:65])),
                            reads=[pfb_], writes=[lsum_b], tag="a1 lrecip")
                        p.emit("vector", (lambda h, hh=hh, qt=qt, pf_=pf_: h.tensor_scalar(
                            local_s[:, qt * D + hh * 64:qt * D + (hh + 1) * 64],
                            pf_[:, 0:64], lsum_s[:, qt % 4:qt % 4 + 1],
                            gates_s[:, qt * 32 + hh:qt * 32 + hh + 1],
                            ALU.mult, ALU.mult)),
                            reads=[pfb_, lsum_b, gates_b], writes=[local_b],
                            tag="a1 lev")
                        if qt == NT - 1:
                            ctx._a1_prev_av = ("PEs", p.cnt["PEs"])

    if dbg_aps is not None:
        d_local, d_rem, d_gates, d_pooled = dbg_aps[:4]
        p.dma("gpsimd", d_local, local_s[:, :], reads=[local_b], tag="dbg")
        p.dma("gpsimd", d_rem, rem_s[:, :], reads=[rem_b], tag="dbg")
        p.dma("gpsimd", d_gates, gates_s[:, :], reads=[gates_b], tag="dbg")
        p.dma("gpsimd", d_pooled, pooledT_s[:, :], reads=[pooledT_b], tag="dbg")

    with ctx.scope():
        remT_s, remT_b = ctx.sbuf("a1_remT", [128, 4 * TOK], BF16)
        rw_s, rw_b = ctx.sbuf("a1_rw", [128, 4 * D], BF16)
        wo_s, wo_b = ctx.sbuf("a1_wo", [128, NCHUNK * D], BF16)
        fused_s, fused_b = zip(*[ctx.sbuf("a1_fsd%d" % i, [128, D], BF16) for i in range(2)])
        ftmp_s, ftmp_b = ctx.sbuf("a1_ftmp", [128, 512], BF16)
        fusedT_s, fusedT_b = zip(*[ctx.sbuf("a1_fsdT%d" % i, [128, NCHUNK * 128], BF16) for i in range(2)])
        xf_s, xf_b = zip(*[ctx.sbuf("a1_xf%d" % i, [128, D], F32) for i in range(2)])
        statf_s, statf_b = zip(*[ctx.sbuf("a1_stf%d" % i, [128, 8], F32) for i in range(2)])
        lny_s, lny_b = zip(*[ctx.sbuf("a1_lny%d" % i, [128, D], BF16) for i in range(2)])
        xo_s, xo_b = zip(*[ctx.sbuf("a1_xo%d" % i, [128, D], F32) for i in range(2)])

        for c in range(4):
            p.dma("sync", rw_s[:, ts(c, D)], r_out_w[ts(c, 128), :],
                  writes=[rw_b], tag="a1 rw")
        for c in range(NCHUNK):
            p.dma("sync", wo_s[:, ts(c, D)], w_out[ts(c, 128), :],
                  writes=[wo_b], tag="a1 wo")

        for t in range(0, NT, 2):
            pt, ptb = (PT, PTb) if (t // 2) % 2 == 0 else (PT2, PT2b)
            p.emit_group("tensor", [
                (lambda h, t=t, i=i, pt=pt: h.transpose(
                    pt[:, ts(i, 128)],
                    rem_s[:, (t + i // 4) * 512 + (i % 4) * 128:(t + i // 4) * 512 + (i % 4 + 1) * 128],
                    ident[:, :]))
                for i in range(8)],
                reads=[rem_b, identb], writes=[ptb], tag="a1 remT")
            p.emit("vector", (lambda h, t=t, pt=pt: h.tensor_copy(
                remT_s.rearrange("p (c q) -> p c q", c=4)[:, :, t * 128:(t + 2) * 128].rearrange("p c (u q) -> p c u q", u=2),
                pt.rearrange("p (u c q) -> p c u q", u=2, c=4))),
                reads=[ptb], writes=[remT_b], tag="a1 remT ev")

        def a1fin_A1(t):
            for n in range(2):
                pa, pab = PS[2 * (t % 2) + n]
                p.emit_group("tensor", [
                    (lambda h, t=t, n=n, c=c, pa=pa: h.matmul(
                        pa[:, :], remT_s[:, c * TOK + t * 128:c * TOK + (t + 1) * 128],
                        rw_s[:, c * D + n * 512:c * D + (n + 1) * 512],
                        start=(c == 0), stop=(c == 3)))
                    for c in range(4)],
                    reads=[remT_b, rw_b], writes=[pab], tag="a1 ro mm")

        def a1fin_A2(t):
            par = t % 2
            for n in range(2):
                pa, pab = PS[2 * (t % 2) + n]
                p.emit("vector", (lambda h, t=t, n=n, pa=pa: h.tensor_tensor(
                    ftmp_s.rearrange("p (e q) -> p e q", e=8),
                    pa[:, :].rearrange("p (e q) -> p e q", e=8),
                    gates_s[:, t * 32 + 16 + n * 8:t * 32 + 24 + n * 8]
                        .rearrange("p (e o) -> p e o", e=8)
                        .broadcast_to((128, 8, 64)),
                    op=ALU.mult)),
                    reads=[pab, gates_b], writes=[ftmp_b], tag="a1 fuse1")
                p.emit("vector", (lambda h, t=t, n=n, par=par: h.tensor_tensor(
                    fused_s[par][:, n * 512:(n + 1) * 512],
                    ftmp_s[:, :],
                    local_s[:, t * D + n * 512:t * D + (n + 1) * 512],
                    op=ALU.add)),
                    reads=[ftmp_b, local_b], writes=[fused_b[par]],
                    tag="a1 fuse2")

        def a1fin_B1h(t):
            par = t % 2
            pt, ptb = (PT, PTb) if par == 0 else (PT2, PT2b)
            p.emit_group("tensor", [
                (lambda h, c=c, par=par, pt=pt: h.transpose(
                    pt[:, ts(c, 128)], fused_s[par][:, ts(c, 128)], ident[:, :]))
                for c in range(NCHUNK)],
                reads=[fused_b[par], identb], writes=[ptb], tag="a1 fT")
            p.emit("vector", (lambda h, par=par, pt=pt: h.tensor_copy(
                fusedT_s[par][:, :], pt[:, :])),
                reads=[ptb], writes=[fusedT_b[par]], tag="a1 fT ev")

        def a1fin_B1t(t):
            par = t % 2
            for n in range(2):
                pb_, pbb_ = PS[4 + n]
                p.emit_group("tensor", [
                    (lambda h, n=n, c=c, par=par, pb_=pb_: h.matmul(
                        pb_[:, :], fusedT_s[par][:, ts(c, 128)],
                        wo_s[:, c * D + n * 512:c * D + (n + 1) * 512],
                        start=(c == 0), stop=(c == NCHUNK - 1)))
                    for c in range(NCHUNK)],
                    reads=[fusedT_b[par], wo_b], writes=[pbb_], tag="a1 y mm")
                p.emit("vector", (lambda h, n=n, par=par, pb_=pb_: h.tensor_copy(
                    xo_s[par][:, ts(n, 512)], pb_[:, :])),
                    reads=[pbb_], writes=[xo_b[par]], tag="a1 y ev")
            ln_stats_acc(ctx, xo_s[par][:, :], xo_b[par], statf_s[par],
                         statf_b[par], pref="a1y ")

        def a1fin_B2(t):
            par = t % 2
            ln_stats_fin(ctx, statf_s[par], statf_b[par], pref="a1y ")
            p.emit("vector", (lambda h, par=par: h.tensor_scalar(
                lny_s[par][:, :], xo_s[par][:, :], statf_s[par][:, 4:5],
                statf_s[par][:, 5:6], ALU.mult, ALU.add)),
                reads=[xo_b[par], statf_b[par]], writes=[lny_b[par]],
                tag="a1 lny")
            p.dma("sync", xf_s[par][:, :], x_in[ts(t, 128), :],
                  writes=[xf_b[par]], tag="a1 xre")
            p.emit("vector", (lambda h, par=par: h.scalar_tensor_tensor(
                xo_s[par][:, :], lny_s[par][:, :], 1.0, xf_s[par][:, :],
                ALU.mult, ALU.add)),
                reads=[lny_b[par], xf_b[par]], writes=[xo_b[par]],
                tag="a1 resid")
            ctx.store_toks.append(p.dma(
                "scalar", x_out[ts(t, 128), :], xo_s[par][:, :],
                reads=[xo_b[par]], tag="a1 xstore"))
            grid_acc(ctx, xo_s[par][:, :], xo_b[par], t, "a1")
            if ctx.on_store:
                ctx.on_store(len(ctx.store_toks) - 1, ctx.store_toks[-1])

        a1fin_A1(0)
        a1fin_A2(0)
        for t in range(NT):
            if t + 1 < NT:
                a1fin_A1(t + 1)
            a1fin_B1h(t)
            if t + 1 < NT:
                a1fin_A2(t + 1)
            if t > 0:
                a1fin_B2(t - 1)
            a1fin_B1t(t)
        a1fin_B2(NT - 1)
            


def build_program_l1attn(dbg=False):
    nc = bass.Bass("TRN2", target_bir_lowering=False, debug=False)
    x_in = nc.declare_dram_parameter("x_in", [TOK, D], F32, isOutput=False)
    xhalo = nc.declare_dram_parameter("xhalo", [256, D], F32, isOutput=False)
    xother = nc.declare_dram_parameter("xother", [TOK, D], F32, isOutput=False)
    wq = nc.declare_dram_parameter("wqT", [D, D], F32, isOutput=False)
    wk = nc.declare_dram_parameter("wkT", [D, D], F32, isOutput=False)
    wv = nc.declare_dram_parameter("wvT", [D, D], F32, isOutput=False)
    wgq = nc.declare_dram_parameter("wgqT", [D, 512], F32, isOutput=False)
    wgk = nc.declare_dram_parameter("wgkT", [D, 512], F32, isOutput=False)
    wgv = nc.declare_dram_parameter("wgvT", [D, 512], F32, isOutput=False)
    r_out_w = nc.declare_dram_parameter("r_out_w", [512, D], F32, isOutput=False)
    glgrw = nc.declare_dram_parameter("glgrw", [D, 4], F32, isOutput=False)
    w_out = nc.declare_dram_parameter("w_outT", [D, D], F32, isOutput=False)
    locmask = nc.declare_dram_parameter("locmask", [128, 384], F32, isOutput=False)
    halomask = nc.declare_dram_parameter("halomask", [128, 384], F32, isOutput=False)
    lmask = nc.declare_dram_parameter("lmask", [64, TOK], F32, isOutput=False)
    ident = nc.declare_dram_parameter("ident", [128, 128], BF16, isOutput=False)
    poolw = nc.declare_dram_parameter("poolw", [128, 2], F32, isOutput=False)
    x_out = nc.declare_dram_parameter("x_out", [TOK, D], F32, isOutput=True)

    es = contextlib.ExitStack()
    with es:
        prog = Prog(nc)
        ctx = Ctx(nc, prog, es)
        ctx.consts = {"poolw": poolw.ap()}
        ctx.ident_s, ctx.ident_b = ctx.sbuf("ident_s", [128, 128], BF16)
        tr, trb = ctx.sbuf("trash", [128, D], BF16)
        ctx.trash_s = tr
        ctx.trashb = trb
        prog.dma("sync", ctx.ident_s[:, :], ident.ap(), writes=[ctx.ident_b])
        dbg_aps = None
        if dbg:
            d_local = nc.declare_dram_parameter("d_local", [128, NT * D], F32, isOutput=True)
            d_rem = nc.declare_dram_parameter("d_rem", [128, NT * 512], F32, isOutput=True)
            d_gates = nc.declare_dram_parameter("d_gates", [128, NT * 32], F32, isOutput=True)
            d_pooled = nc.declare_dram_parameter("d_pooled", [128, NCHUNK * 64], F32, isOutput=True)
            d_xnT = nc.declare_dram_parameter("d_xnT", [128, NCHUNK * KLO], F32, isOutput=True)
            d_x0 = nc.declare_dram_parameter("d_x0", [128, D], F32, isOutput=True)
            d_st0 = nc.declare_dram_parameter("d_st0", [128, 8], F32, isOutput=True)
            d_xn0 = nc.declare_dram_parameter("d_xn0", [128, D], F32, isOutput=True)
            dbg_aps = (d_local.ap(), d_rem.ap(), d_gates.ap(), d_pooled.ap(), d_xnT.ap(),
                       d_x0.ap(), d_st0.ap(), d_xn0.ap())
        build_l1attn(ctx, x_in.ap(), xhalo.ap(), xother.ap(), x_out.ap(),
                     wq.ap(), wk.ap(), wv.ap(), wgq.ap(), wgk.ap(), wgv.ap(),
                     r_out_w.ap(), glgrw.ap(), w_out.ap(), locmask.ap(),
                     halomask.ap(), lmask.ap(), dbg_aps=dbg_aps)
        prog.barrier()
        prog.realize(es)
    return nc




def build_l1ffn(ctx, x_in, xother, x_out, baseWT, base_b, muT, sigT, downWf,
                down_bf, upWf, up_bf, convTf, conv_bf, iota128, iota8,
                dbg_aps=None, cc=None, grid_ext=None):
    nc, p = ctx.nc, ctx.p
    ctx.store_toks = []
    ident = ctx.ident_s
    identb = ctx.ident_b

    xnT_s, xnT_b = ctx.sbuf("f1_xnT", [128, NCHUNK * TOK], BF16)
    xob_s, _xob0 = ctx.sbuf("f1_xob", [128, NT * D], BF16)
    xobc_b = [Buf("f1xob%d" % t) for t in range(NT)]
    part_s, part_b = ctx.sbuf("f1_part", [128, NCHUNK * 32 * 2], F32)
    part2_b = Buf("f1_part2")
    m12_s, m12_b = ctx.sbuf("f1_m12", [128, NCHUNK * 2], F32)
    musig_s, musig_b = ctx.sbuf("f1_musig", [128, NCHUNK * 8 * 4], F32)
    loss_s, loss_b = ctx.sbuf("f1_loss", [1, 32], F32)
    idx_s, idx_b = ctx.sbuf("f1_idx", [128, 4], I32)
    idxf_s, idxf_b = ctx.sbuf("f1_idxf", [1, 2], F32)
    idxbf_s, idxbf_b = ctx.sbuf("f1_idxbf", [1, 1], BF16)
    iota_s, iota_b = ctx.sbuf("f1_iota", [128, 1], I32)
    iota8_s, iota8_b = ctx.sbuf("f1_iota8", [1, 8], F32)

    bw_s, bw_b = ctx.sbuf("f1_bw", [128, NCHUNK * D], BF16)
    cw_s, cw_b = ctx.sbuf("f1_cw", [128, NCHUNK * D], BF16)
    dwf_s, dwf_b = ctx.sbuf("f1_dwf", [128, 64], F32)
    dw_s, dw_b = ctx.sbuf("f1_dw", [128, NCHUNK * 64], BF16)
    uwf_s, uwf_b = ctx.sbuf("f1_uwf", [64, D], F32)
    cwf_s, cwf_b = ctx.sbuf("f1_cwf", [128, D], F32)
    mm2_s, mm2_b = ctx.sbuf("f1_mm2", [128, 64], F32)
    uw_s, uw_b = ctx.sbuf("f1_uw", [64, D], BF16)
    brow_s, brow_b = ctx.sbuf("f1_brow", [1, 4 * D], BF16)
    browf_s, browf_b = ctx.sbuf("f1_browf", [1, D], F32)

    P6 = [ctx.psum("f1_P%d" % i, [128, 512], F32) for i in range(6)]
    PA, PAb = P6[0]
    PB, PBb = P6[1]
    PC, PCb = P6[2]
    PD, PDb = P6[3]
    PT, PTb = ctx.psum("f1_PT", [128, 1024], BF16)
    PT2, PT2b = ctx.psum("f1_PT2", [128, 1024], BF16)

    p.dma("sync", iota_s[:, :], iota128, writes=[iota_b], tag="f1 iota")
    p.dma("sync", iota8_s[:, :], iota8, writes=[iota8_b], tag="f1 iota8")
    for c in range(NCHUNK):
        p.dma("sync", bw_s[:, ts(c, D)], baseWT[ts(c, 128), :], writes=[bw_b],
              tag="f1 bw")
        p.dma("sync", musig_s[:, c * 32:c * 32 + 8], muT[ts(c, 128), :],
              writes=[musig_b], tag="f1 mu")
        p.dma("sync", musig_s[:, c * 32 + 8:c * 32 + 16], sigT[ts(c, 128), :],
              writes=[musig_b], tag="f1 sig")

    nparts = 32
    with ctx.scope():
        x1_s, x1_b = zip(*[ctx.sbuf("f1_x1%d" % i, [128, D], F32) for i in range(2)])
        st1_s, st1_b = zip(*[ctx.sbuf("f1_st1%d" % i, [128, 8], F32) for i in range(2)])
        xn1_s, xn1_b = zip(*[ctx.sbuf("f1_xn1%d" % i, [128, D], BF16) for i in range(2)])
        sq1_s, sq1_b = zip(*[ctx.sbuf("f1_sq1%d" % i, [128, D], BF16) for i in range(2)])
        pc1_s, pc1_b = zip(*[ctx.sbuf("f1_pc1%d" % i, [128, D], BF16) for i in range(2)])

        if grid_ext is None:
            fgrid_s, fgrid_b = ctx.sbuf("f1_grid", [128, 192], F32)
            if cc is not None:
                ln_prepass(ctx, [(x_in[ts(t, 128), :], None) for t in range(NT)],
                           fgrid_s, fgrid_b, [x1_s[0][:, :], x1_s[1][:, :]],
                           [x1_b[0], x1_b[1]], pref="f1 ")
        else:
            fgrid_s, fgrid_b = grid_ext
            ln_prepass_finish(ctx, NT, fgrid_s, fgrid_b, fgrid_b, pref="f1 ")

        xnTc_b = [Buf("f1xnTc%d" % c) for c in range(NCHUNK)]
        onesc_s, onesc_b = ctx.sbuf("f1_onesc", [128, 1], BF16)
        p.emit("vector", lambda h: h.memset(onesc_s[:, :], 1.0),
               writes=[onesc_b], tag="f1 onesc")

        def stat_head(par, src_slice, tag, gcol=None):
            pt, ptb = (PT, PTb) if par == 0 else (PT2, PT2b)
            p.dma("sync", x1_s[par][:, :], src_slice, writes=[x1_b[par]],
                  tag=tag + " x")
            if gcol is None:
                ln_stats(ctx, x1_s[par][:, :], x1_b[par], st1_s[par],
                         st1_b[par], pref=tag + " ")
                sc1 = st1_s[par][:, 4:5]
                sc2 = st1_s[par][:, 5:6]
                rb = [st1_b[par]]
            else:
                sc1 = fgrid_s[:, 128 + gcol:129 + gcol]
                sc2 = fgrid_s[:, 160 + gcol:161 + gcol]
                rb = [fgrid_b]
            p.emit("vector", (lambda h, par=par, sc1=sc1, sc2=sc2:
                              h.tensor_scalar(
                xn1_s[par][:, :], x1_s[par][:, :], sc1, sc2,
                ALU.mult, ALU.add)),
                reads=[x1_b[par]] + rb, writes=[xn1_b[par]],
                tag=tag + " xn")
            p.emit_group("tensor", [
                (lambda h, c=c, par=par, pt=pt: h.transpose(
                    pt[:, ts(c, 128)], xn1_s[par][:, ts(c, 128)], ident[:, :]))
                for c in range(NCHUNK)],
                reads=[xn1_b[par], identb], writes=[ptb], tag=tag + " T")

        def stat_tail(par, col, xdst, tag):
            pt, ptb = (PT, PTb) if par == 0 else (PT2, PT2b)
            if xdst is not None and cc is not None:
                t = col
                p.emit("vector", (lambda h, xdst=xdst, pt=pt: h.tensor_copy(
                    xnT_s.rearrange("p (c q) -> p c q", c=NCHUNK)[:, :, xdst:xdst + 128],
                    pt.rearrange("p (c q) -> p c q", c=NCHUNK)[:, :, :])),
                    reads=[ptb], writes=[xnTc_b[0]], tag=tag + " xnT ev")
                p.emit("vector", (lambda h, par=par: h.tensor_tensor(
                    sq1_s[par][:, :], xn1_s[par][:, :], xn1_s[par][:, :],
                    op=ALU.mult)),
                    reads=[xn1_b[par]], writes=[sq1_b[par]], tag=tag + " xsq")
                for half in range(2):
                    p.emit("tensor", (lambda h, t=t, half=half, par=par:
                                      h.matmul(
                        P6[half][0][0:1, 0:512], onesc_s[:, 0:1],
                        xn1_s[par][:, half * 512:(half + 1) * 512],
                        start=(t == 0), stop=(t == NT - 1))),
                        reads=[onesc_b, xn1_b[par]], writes=[P6[half][1]],
                        tag=tag + " m1mm")
                    p.emit("tensor", (lambda h, t=t, half=half, par=par:
                                      h.matmul(
                        P6[2 + half][0][0:1, 0:512], onesc_s[:, 0:1],
                        sq1_s[par][:, half * 512:(half + 1) * 512],
                        start=(t == 0), stop=(t == NT - 1))),
                        reads=[onesc_b, sq1_b[par]], writes=[P6[2 + half][1]],
                        tag=tag + " m2mm")
                return
            for c in range(NCHUNK):
                dst = (pc1_s[par][:, ts(c, 128)] if xdst is None else
                       xnT_s[:, c * TOK + xdst:c * TOK + xdst + 128])
                dstb = pc1_b[par] if xdst is None else xnTc_b[c]
                p.emit("vector", (lambda h, c=c, dst=dst, col=col, pt=pt:
                                  h.tensor_scalar(
                    dst, pt[:, ts(c, 128)], 1.0, 0.0, ALU.mult, ALU.add,
                    accum_out=part_s[:, c * 64 + col:c * 64 + col + 1])),
                    reads=[ptb], writes=[dstb, part_b], tag=tag + " m1")
                p.emit("scalar", (lambda h, c=c, col=col, par=par, dst=dst:
                                  h.activation(
                    sq1_s[par][:, ts(c, 128)], dst, AF.Square,
                    accum_out=part_s[:, c * 64 + 32 + col:c * 64 + 32 + col + 1])),
                    reads=[dstb], writes=[sq1_b[par], part2_b], tag=tag + " m2")

        stat_head(0, x_in[ts(0, 128), :], "f1",
                  gcol=(0 if cc is not None else None))
        for t in range(NT):
            if t + 1 < NT:
                stat_head((t + 1) % 2, x_in[ts(t + 1, 128), :], "f1",
                          gcol=(t + 1 if cc is not None else None))
            stat_tail(t % 2, t, t * 128, "f1")
        if cc is None:
            for t in range(NT):
                stat_head(t % 2, xother[ts(t, 128), :], "f1o",
                          gcol=None)
                stat_tail(t % 2, NT + t, None, "f1o")
        else:
            m12row_s, m12row_b = ctx.sbuf("f1_m12row", [1, 2 * D], F32)
            for half in range(2):
                p.emit("vector", (lambda h, half=half: h.tensor_copy(
                    m12row_s[0:1, half * 512:(half + 1) * 512],
                    P6[half][0][0:1, 0:512])),
                    reads=[P6[half][1]], writes=[m12row_b], tag="f1 m1row")
                p.emit("vector", (lambda h, half=half: h.tensor_copy(
                    m12row_s[0:1, D + half * 512:D + (half + 1) * 512],
                    P6[2 + half][0][0:1, 0:512])),
                    reads=[P6[2 + half][1]], writes=[m12row_b], tag="f1 m2row")
            builds = []
            for c in range(NCHUNK):
                for k in range(2):
                    builds.append((lambda h, c=c, k=k: h.transpose(
                        P6[4][0][:, 2 * c + k:2 * c + k + 1],
                        m12row_s[0:1, k * D + c * 128:k * D + (c + 1) * 128],
                        ctx.identf_s[0:1, 0:1])))
            p.emit_group("tensor", builds, reads=[m12row_b, ctx.identf_b],
                         writes=[P6[4][1]], tag="f1 m12T")
            p.emit("vector", lambda h: h.tensor_copy(
                m12_s[:, :], P6[4][0][:, 0:2 * NCHUNK]),
                reads=[P6[4][1]], writes=[m12_b], tag="f1 m12 ev")
        nparts = 32 if cc is None else 16

    def f1_base(t):
        pb_, pbb_ = P6[4] if t % 2 == 0 else P6[5]
        for n in range(2):
            p.emit_group("tensor", [
                (lambda h, t=t, n=n, c=c, pb_=pb_: h.matmul(
                    pb_[:, :],
                    xnT_s[:, c * TOK + t * 128:c * TOK + (t + 1) * 128],
                    bw_s[:, c * D + n * 512:c * D + (n + 1) * 512],
                    start=(c == 0), stop=(c == NCHUNK - 1)))
                for c in range(NCHUNK)],
                reads=[xnT_b, bw_b], writes=[pbb_], tag="f1 bmm")
            p.emit("vector", (lambda h, t=t, n=n, pb_=pb_: h.tensor_copy(
                xob_s[:, t * D + n * 512:t * D + (n + 1) * 512], pb_[:, :])),
                reads=[pbb_], writes=[xobc_b[t]], tag="f1 b ev")

    for _bt in range(0, NT // 2):
        f1_base(_bt)

    if cc is None:
        for c in range(NCHUNK):
            p.emit("vector", (lambda h, c=c: h.reduce_sum(
                m12_s[:, 2 * c:2 * c + 1], part_s[:, c * 64:c * 64 + nparts],
                axis=AX.X)),
                reads=[part_b], writes=[m12_b], tag="f1 m1sum")
            p.emit("vector", (lambda h, c=c: h.reduce_sum(
                m12_s[:, 2 * c + 1:2 * c + 2], part_s[:, c * 64 + 32:c * 64 + 32 + nparts],
                axis=AX.X)), reads=[part2_b], writes=[m12_b], tag="f1 m2sum")
    if cc is not None:
        p.dma("gpsimd", cc["m12_in"], m12_s[:, :], reads=[m12_b],
              writes=[cc["m12_in_b"]], tag="cc m12w")
        p.collective("AllReduce", ALU.add, cc["groups"],
                     [cc["m12_in"]], [cc["m12_out"]],
                     reads=[cc["m12_in_b"]], writes=[cc["m12_out_b"]],
                     sem_name=cc["sem_m12"], tag="cc m12")
        p.dma("sync", m12_s[:, :], cc["m12_out"],
              waits=[(cc["sem_m12"], 1)], writes=[m12_b], tag="cc m12r")
    inv4096 = 1.0 / 4096.0
    p.emit("vector", lambda h: h.tensor_scalar(
        m12_s[:, :], m12_s[:, :], inv4096, None, ALU.mult),
        reads=[m12_b], writes=[m12_b], tag="f1 m12")
    for c in range(NCHUNK):
        p.emit("vector", (lambda h, c=c: h.tensor_tensor(
            musig_s[:, c * 32 + 16:c * 32 + 24], musig_s[:, c * 32 + 8:c * 32 + 16],
            musig_s[:, c * 32 + 8:c * 32 + 16], op=ALU.mult)),
            reads=[musig_b], writes=[musig_b], tag="f1 sig2")
        p.emit("vector", (lambda h, c=c: h.reciprocal(
            musig_s[:, c * 32 + 16:c * 32 + 24], musig_s[:, c * 32 + 16:c * 32 + 24])),
            reads=[musig_b], writes=[musig_b], tag="f1 iv")
        p.emit("vector", (lambda h, c=c: h.tensor_tensor(
            musig_s[:, c * 32 + 24:c * 32 + 32], musig_s[:, c * 32:c * 32 + 8],
            musig_s[:, c * 32 + 16:c * 32 + 24], op=ALU.mult)),
            reads=[musig_b], writes=[musig_b], tag="f1 muiv")
    for (name, lcol, rofs, dst) in (("A", 1, 16, 0), ("B", 0, 24, 8)):
        builds = [
            (lambda h, c=c, lcol=lcol, rofs=rofs, dst=dst: h.matmul(
                PA[0:1, dst:dst + 8], m12_s[:, 2 * c + lcol:2 * c + lcol + 1],
                musig_s[:, c * 32 + rofs:c * 32 + rofs + 8],
                start=(c == 0), stop=(c == NCHUNK - 1)))
            for c in range(NCHUNK)]
        p.emit_group("tensor", builds, reads=[m12_b, musig_b], writes=[PAb],
                     tag="f1 loss mm")
    p.emit("vector", lambda h: h.tensor_tensor(
        mm2_s[:, 0:64],
        musig_s.rearrange("p (c k) -> p c k", c=NCHUNK)[:, :, 0:8],
        musig_s.rearrange("p (c k) -> p c k", c=NCHUNK)[:, :, 24:32], op=ALU.mult),
        reads=[musig_b], writes=[mm2_b], tag="f1 mumuiv")
    builds = [
        (lambda h, c=c: h.matmul(
            PA[0:1, 16:24], ctx.onesf_s[:, 0:1],
            mm2_s[:, c * 8:(c + 1) * 8],
            start=(c == 0), stop=(c == NCHUNK - 1)))
        for c in range(NCHUNK)]
    p.emit_group("tensor", builds, reads=[mm2_b, ctx.onesf_b], writes=[PAb],
                 tag="f1 C mm")
    p.emit("vector", lambda h: h.tensor_copy(loss_s[0:1, 0:24], PA[0:1, 0:24]),
           reads=[PAb], writes=[loss_b], tag="f1 loss ev")
    p.emit("vector", lambda h: h.scalar_tensor_tensor(
        loss_s[0:1, 8:16], loss_s[0:1, 8:16], -2.0, loss_s[0:1, 0:8],
        ALU.mult, ALU.add), reads=[loss_b], writes=[loss_b], tag="f1 lossc1")
    p.emit("vector", lambda h: h.tensor_tensor(
        loss_s[0:1, 0:8], loss_s[0:1, 8:16], loss_s[0:1, 16:24], op=ALU.add),
        reads=[loss_b], writes=[loss_b], tag="f1 lossc2")
    p.emit("vector", lambda h: h.tensor_reduce(
        loss_s[0:1, 24:25], loss_s[0:1, 0:8], axis=AX.X, op=ALU.min),
        reads=[loss_b], writes=[loss_b], tag="f1 lmin")
    p.emit("vector", lambda h: h.tensor_scalar(
        loss_s[0:1, 8:16], loss_s[0:1, 0:8], loss_s[0:1, 24:25], None,
        ALU.is_equal), reads=[loss_b], writes=[loss_b], tag="f1 lmask")
    p.emit("vector", lambda h: h.tensor_tensor(
        loss_s[0:1, 8:16], loss_s[0:1, 8:16], iota8_s[0:1, :], op=ALU.mult),
        reads=[loss_b, iota8_b], writes=[loss_b], tag="f1 lidx")
    p.emit("vector", lambda h: h.reduce_sum(
        idxf_s[0:1, 0:1], loss_s[0:1, 8:16], axis=AX.X),
        reads=[loss_b], writes=[idxf_b], tag="f1 idxf")
    p.emit("vector", lambda h: h.tensor_copy(idxbf_s[0:1, :], idxf_s[0:1, 0:1]),
           reads=[idxf_b], writes=[idxbf_b], tag="f1 idxbf")
    p.emit("tensor", lambda h: h.matmul(PA[:, 0:1], ctx.ones1_s[0:1, :],
                                        idxbf_s[0:1, :], start=True, stop=True),
           reads=[ctx.ones1_b, idxbf_b], writes=[PAb], tag="f1 idx mm")
    p.emit("vector", lambda h: h.tensor_copy(idx_s[:, 0:1], PA[:, 0:1]),
           reads=[PAb], writes=[idx_b], tag="f1 idx i32")
    p.emit("vector", lambda h: h.tensor_scalar(
        idx_s[:, 1:2], idx_s[:, 0:1], 1024, None, ALU.mult),
        reads=[idx_b], writes=[idx_b], tag="f1 idxm")
    p.emit("vector", lambda h: h.tensor_tensor(
        idx_s[:, 1:2], idx_s[:, 1:2], iota_s[:, :], op=ALU.add),
        reads=[idx_b, iota_b], writes=[idx_b], tag="f1 idxa")
    p.emit("vector", lambda h: h.tensor_scalar(
        idx_s[:, 2:3], idx_s[:, 0:1], 64, None, ALU.mult),
        reads=[idx_b], writes=[idx_b], tag="f1 idxu")
    p.emit("vector", lambda h: h.tensor_tensor(
        idx_s[:, 2:3], idx_s[:, 2:3], iota_s[:, :], op=ALU.add),
        reads=[idx_b, iota_b], writes=[idx_b], tag="f1 idxua")

    def gather(dst, dstb, table, idxap, rows, cols, tag, chunk_off=0):
        p.emit("gpsimd", (lambda h: h.indirect_dma_start(
            out=dst, out_offset=None, in_=table,
            in_offset=IndirectOffsetOnAxis(ap=idxap, axis=0))),
            reads=[idx_b], writes=[dstb], sem=DMA_SEM["gpsimd"], tag=tag)

    for c in range(NCHUNK):
        if c > 0:
            p.emit("vector", (lambda h, c=c: h.tensor_scalar(
                idx_s[:, 3:4], idx_s[:, 1:2], 128 * c, None, ALU.add)),
                reads=[idx_b], writes=[idx_b], tag="f1 idxc")
        src_idx = idx_s[:, 1:2] if c == 0 else idx_s[:, 3:4]
        p.dma_custom("gpsimd", (lambda h, c=c, src_idx=src_idx: h.indirect_dma_start(
            out=dwf_s[:, :], out_offset=None, in_=downWf,
            in_offset=IndirectOffsetOnAxis(ap=src_idx, axis=0))),
            reads=[idx_b], writes=[dwf_b], tag="f1 gdw")
        p.emit("vector", (lambda h, c=c: h.tensor_copy(
            dw_s[:, ts(c, 64)], dwf_s[:, :])),
            reads=[dwf_b], writes=[dw_b], tag="f1 dw bf")
        p.dma_custom("gpsimd", (lambda h, c=c, src_idx=src_idx: h.indirect_dma_start(
            out=cwf_s[:, :], out_offset=None, in_=convTf,
            in_offset=IndirectOffsetOnAxis(ap=src_idx, axis=0))),
            reads=[idx_b], writes=[cwf_b], tag="f1 gcw")
        p.emit("vector", (lambda h, c=c: h.tensor_copy(
            cw_s[:, ts(c, D)], cwf_s[:, :])),
            reads=[cwf_b], writes=[cw_b], tag="f1 cw bf")
    p.dma_custom("gpsimd", lambda h: h.indirect_dma_start(
        out=uwf_s[0:64, :], out_offset=None, in_=upWf,
        in_offset=IndirectOffsetOnAxis(ap=idx_s[0:64, 2:3], axis=0)),
        reads=[idx_b], writes=[uwf_b], tag="f1 guw")
    p.emit("vector", lambda h: h.tensor_copy(uw_s[0:64, :], uwf_s[0:64, :]),
           reads=[uwf_b], writes=[uw_b], tag="f1 uw bf")
    p.dma("gpsimd", brow_s[0:1, 0:D], base_b, writes=[brow_b], tag="f1 bb")
    p.dma_custom("gpsimd", lambda h: h.indirect_dma_start(
        out=cwf_s[:, :], out_offset=None, in_=conv_bf,
        in_offset=IndirectOffsetOnAxis(ap=idx_s[:, 0:1], axis=0)),
        reads=[idx_b], writes=[cwf_b], tag="f1 gcb")
    p.emit("vector", lambda h: h.tensor_copy(brow_s[0:1, D:2 * D], cwf_s[0:1, :]),
           reads=[cwf_b], writes=[brow_b], tag="f1 cb bf")
    p.dma_custom("gpsimd", lambda h: h.indirect_dma_start(
        out=cwf_s[:, :], out_offset=None, in_=up_bf,
        in_offset=IndirectOffsetOnAxis(ap=idx_s[:, 0:1], axis=0)),
        reads=[idx_b, cwf_b], writes=[cwf_b], tag="f1 gub")
    p.emit("vector", lambda h: h.tensor_copy(brow_s[0:1, 2 * D:3 * D], cwf_s[0:1, :]),
           reads=[cwf_b], writes=[brow_b], tag="f1 ub bf")
    p.dma_custom("gpsimd", lambda h: h.indirect_dma_start(
        out=cwf_s[:, 0:64], out_offset=None, in_=down_bf,
        in_offset=IndirectOffsetOnAxis(ap=idx_s[:, 0:1], axis=0)),
        reads=[idx_b, cwf_b], writes=[cwf_b], tag="f1 gdb")
    p.emit("vector", lambda h: h.tensor_copy(brow_s[0:1, 3 * D:3 * D + 64],
                                             cwf_s[0:1, 0:64]),
           reads=[cwf_b], writes=[brow_b], tag="f1 db bf")

    for _bt in range(NT // 2, NT):
        f1_base(_bt)

    if dbg_aps is not None:
        p.dma("gpsimd", dbg_aps[0], m12_s[:, :], reads=[m12_b], tag="dbg")
        p.dma("gpsimd", dbg_aps[1], loss_s[:, :], reads=[loss_b], tag="dbg")
        p.dma("gpsimd", dbg_aps[2], idx_s[:, :], reads=[idx_b], tag="dbg")
        p.dma("gpsimd", dbg_aps[3], brow_s[:, :], reads=[brow_b], tag="dbg")
        p.dma("gpsimd", dbg_aps[4], dw_s[:, :], reads=[dw_b], tag="dbg")

    with ctx.scope():
        h_s, h_b = zip(*[ctx.sbuf("f1_h%d" % i, [128, 64], BF16) for i in range(2)])
        hT_s, hT_b = zip(*[ctx.sbuf("f1_hT%d" % i, [64, 128], BF16) for i in range(2)])
        aT_s, aT_b = zip(*[ctx.sbuf("f1_aT%d" % i, [128, NCHUNK * 128], BF16) for i in range(2)])
        x3_s, x3_b = zip(*[ctx.sbuf("f1_x3%d" % i, [128, D], F32) for i in range(2)])
        st3_s, st3_b = zip(*[ctx.sbuf("f1_st3%d" % i, [128, 8], F32) for i in range(2)])
        lny3_s, lny3_b = zip(*[ctx.sbuf("f1_lny3%d" % i, [128, D], BF16) for i in range(2)])
        xo3_s, xo3_b = zip(*[ctx.sbuf("f1_xo3%d" % i, [128, D], F32) for i in range(2)])

        def f3_A(t):
            par = t % 2
            pC, pCb = P6[3 * par + 2]
            pt, ptb = (PT, PTb) if par == 0 else (PT2, PT2b)
            builds = [
                (lambda h, t=t, c=c, pC=pC: h.matmul(
                    pC[:, 0:64], xnT_s[:, c * TOK + t * 128:c * TOK + (t + 1) * 128],
                    dw_s[:, ts(c, 64)], start=(c == 0), stop=False))
                for c in range(NCHUNK)]
            builds.append(lambda h, pC=pC: h.matmul(
                pC[:, 0:64], ctx.ones1_s[0:1, :], brow_s[0:1, 3 * D:3 * D + 64],
                start=False, stop=True))
            p.emit_group("tensor", builds,
                         reads=[xnT_b, dw_b, ctx.ones1_b, brow_b], writes=[pCb],
                         tag="f1 dmm")
            p.emit("scalar", (lambda h, par=par, pC=pC: h.activation(
                h_s[par][:, :], pC[:, 0:64], AF.Relu)),
                reads=[pCb], writes=[h_b[par]], tag="f1 relu")
            p.emit("tensor", (lambda h, par=par, pt=pt: h.transpose(
                pt[0:64, 0:128], h_s[par][:, :], ident[:, :])),
                reads=[h_b[par], identb], writes=[ptb], tag="f1 hT")
            p.emit("vector", (lambda h, par=par, pt=pt: h.tensor_copy(
                hT_s[par][0:64, :], pt[0:64, 0:128])),
                reads=[ptb], writes=[hT_b[par]], tag="f1 hT ev")

        def f3_B1(t):
            par = t % 2
            pA, pAb = P6[3 * par]
            pB, pBb = P6[3 * par + 1]
            for c in range(NCHUNK):
                pD_, pDb_ = (pA, pAb) if c % 2 == 0 else (pB, pBb)
                p.emit_group("tensor", [
                    (lambda h, c=c, par=par, pD_=pD_: h.matmul(
                        pD_[:, 0:128], uw_s[0:64, ts(c, 128)], hT_s[par][0:64, :],
                        start=True, stop=False)),
                    (lambda h, c=c, pD_=pD_: h.matmul(
                        pD_[:, 0:128], brow_s[0:1, 2 * D + c * 128:2 * D + (c + 1) * 128],
                        ctx.ones1_s[0:1, :], start=False, stop=True))],
                    reads=[uw_b, hT_b[par], brow_b, ctx.ones1_b], writes=[pDb_],
                    tag="f1 umm")
                p.emit("vector", (lambda h, c=c, par=par, pD_=pD_: h.tensor_copy(
                    aT_s[par][:, ts(c, 128)], pD_[:, 0:128])),
                    reads=[pDb_], writes=[aT_b[par]], tag="f1 aT ev")

        def f3_B2(t):
            par = t % 2
            pA, pAb = P6[3 * par]
            pB, pBb = P6[3 * par + 1]
            for n in range(2):
                ps_, psb_ = (pA, pAb) if n == 0 else (pB, pBb)
                builds = [
                    (lambda h, n=n, c=c, par=par, ps_=ps_: h.matmul(
                        ps_[:, :], aT_s[par][:, ts(c, 128)],
                        cw_s[:, c * D + n * 512:c * D + (n + 1) * 512],
                        start=(c == 0), stop=False))
                    for c in range(NCHUNK)]
                builds.append(lambda h, n=n, ps_=ps_: h.matmul(
                    ps_[:, :], ctx.ones1_s[0:1, :], brow_s[0:1, ts(n, 512)],
                    start=False, stop=False))
                builds.append(lambda h, n=n, ps_=ps_: h.matmul(
                    ps_[:, :], ctx.ones1_s[0:1, :],
                    brow_s[0:1, D + n * 512:D + (n + 1) * 512],
                    start=False, stop=True))
                p.emit_group("tensor", builds,
                             reads=[aT_b[par], cw_b, ctx.ones1_b, brow_b],
                             writes=[psb_], tag="f1 ymm")
                p.emit("vector", (lambda h, t=t, n=n, par=par, ps_=ps_:
                                  h.tensor_tensor(
                    xo3_s[par][:, ts(n, 512)], ps_[:, :],
                    xob_s[:, t * D + n * 512:t * D + (n + 1) * 512],
                    op=ALU.add)),
                    reads=[psb_, xobc_b[t]], writes=[xo3_b[par]],
                    tag="f1 y ev")
            ln_stats_acc(ctx, xo3_s[par][:, :], xo3_b[par], st3_s[par],
                         st3_b[par], pref="f1y ")

        def f3_C(t):
            par = t % 2
            ln_stats_fin(ctx, st3_s[par], st3_b[par], pref="f1y ")
            p.emit("vector", (lambda h, par=par: h.tensor_scalar(
                lny3_s[par][:, :], xo3_s[par][:, :], st3_s[par][:, 4:5],
                st3_s[par][:, 5:6], ALU.mult, ALU.add)),
                reads=[xo3_b[par], st3_b[par]], writes=[lny3_b[par]],
                tag="f1 lny")
            p.dma("sync", x3_s[par][:, :], x_in[ts(t, 128), :],
                  writes=[x3_b[par]], tag="f1 xre")
            p.emit("vector", (lambda h, par=par: h.scalar_tensor_tensor(
                xo3_s[par][:, :], lny3_s[par][:, :], 1.0, x3_s[par][:, :],
                ALU.mult, ALU.add)),
                reads=[lny3_b[par], x3_b[par]], writes=[xo3_b[par]],
                tag="f1 resid")
            p.dma("scalar", x_out[ts(t, 128), :], xo3_s[par][:, :],
                  reads=[xo3_b[par]], tag="f1 xstore")

        f3_A(0)
        for t in range(NT):
            if t + 1 < NT:
                f3_A(t + 1)
            f3_B1(t)
            if t > 0:
                f3_C(t - 1)
            f3_B2(t)
        f3_C(NT - 1)


def build_program_l1ffn(dbg=False):
    nc = bass.Bass("TRN2", target_bir_lowering=False, debug=False)
    x_in = nc.declare_dram_parameter("x_in", [TOK, D], F32, isOutput=False)
    xother = nc.declare_dram_parameter("xother", [TOK, D], F32, isOutput=False)
    baseWT = nc.declare_dram_parameter("baseWT", [D, D], F32, isOutput=False)
    base_b = nc.declare_dram_parameter("base_b", [1, D], F32, isOutput=False)
    muT = nc.declare_dram_parameter("muT", [D, 8], F32, isOutput=False)
    sigT = nc.declare_dram_parameter("sigT", [D, 8], F32, isOutput=False)
    downWf = nc.declare_dram_parameter("downWf", [8 * D, 64], F32, isOutput=False)
    down_bf = nc.declare_dram_parameter("down_bf", [8, 64], F32, isOutput=False)
    upWf = nc.declare_dram_parameter("upWf", [8 * 64, D], F32, isOutput=False)
    up_bf = nc.declare_dram_parameter("up_bf", [8, D], F32, isOutput=False)
    convTf = nc.declare_dram_parameter("convTf", [8 * D, D], F32, isOutput=False)
    conv_bf = nc.declare_dram_parameter("conv_bf", [8, D], F32, isOutput=False)
    iota128 = nc.declare_dram_parameter("iota128", [128, 1], I32, isOutput=False)
    iota8 = nc.declare_dram_parameter("iota8", [1, 8], F32, isOutput=False)
    ident = nc.declare_dram_parameter("ident", [128, 128], BF16, isOutput=False)
    ones1 = nc.declare_dram_parameter("ones1", [1, 128], BF16, isOutput=False)
    onesf = nc.declare_dram_parameter("onesf", [128, 1], F32, isOutput=False)
    x_out = nc.declare_dram_parameter("x_out", [TOK, D], F32, isOutput=True)

    es = contextlib.ExitStack()
    with es:
        prog = Prog(nc)
        ctx = Ctx(nc, prog, es)
        ctx.ident_s, ctx.ident_b = ctx.sbuf("ident_s", [128, 128], BF16)
        ctx.ones1_s, ctx.ones1_b = ctx.sbuf("ones1_s", [1, 128], BF16)
        ctx.onesf_s, ctx.onesf_b = ctx.sbuf("onesf_s", [128, 1], F32)
        tr, trb = ctx.sbuf("trash", [128, D], BF16)
        ctx.trash_s = tr
        ctx.trashb = trb
        prog.dma("sync", ctx.ident_s[:, :], ident.ap(), writes=[ctx.ident_b])
        prog.dma("sync", ctx.ones1_s[:, :], ones1.ap(), writes=[ctx.ones1_b])
        prog.dma("sync", ctx.onesf_s[:, :], onesf.ap(), writes=[ctx.onesf_b])
        dbg_aps = None
        if dbg:
            d_m12 = nc.declare_dram_parameter("d_m12", [128, NCHUNK * 2], F32, isOutput=True)
            d_loss = nc.declare_dram_parameter("d_loss", [1, 32], F32, isOutput=True)
            d_idx = nc.declare_dram_parameter("d_idx", [128, 4], I32, isOutput=True)
            d_brow = nc.declare_dram_parameter("d_brow", [1, 4 * D], F32, isOutput=True)
            d_dw = nc.declare_dram_parameter("d_dw", [128, NCHUNK * 64], F32, isOutput=True)
            d_h = nc.declare_dram_parameter("d_h", [128, 64], F32, isOutput=True)
            d_aT = nc.declare_dram_parameter("d_aT", [128, NCHUNK * 128], F32, isOutput=True)
            d_y = nc.declare_dram_parameter("d_y", [128, D], F32, isOutput=True)
            d_uw = nc.declare_dram_parameter("d_uw", [64, D], F32, isOutput=True)
            dbg_aps = (d_m12.ap(), d_loss.ap(), d_idx.ap(), d_brow.ap(), d_dw.ap(),
                       d_h.ap(), d_aT.ap(), d_y.ap(), d_uw.ap())
        build_l1ffn(ctx, x_in.ap(), xother.ap(), x_out.ap(), baseWT.ap(),
                    base_b.ap(), muT.ap(), sigT.ap(), downWf.ap(), down_bf.ap(),
                    upWf.ap(), up_bf.ap(), convTf.ap(), conv_bf.ap(),
                    iota128.ap(), iota8.ap(), dbg_aps=dbg_aps)
        prog.barrier()
        prog.realize(es)
    return nc




def make_consts():
    c = {}
    c["ident_bf16"] = np.eye(128, dtype=BFNP)
    c["ident_f32"] = np.eye(128, dtype=np.float32)
    c["ones1_bf16"] = np.ones((1, 128), dtype=BFNP)
    return c


def build_program_l0attn(half=0):
    nc = bass.Bass("TRN2", target_bir_lowering=False, debug=False)
    x_in = nc.declare_dram_parameter("x_in", [TOK, D], F32, isOutput=False)
    wq = nc.declare_dram_parameter("wqT", [D, D], F32, isOutput=False)
    wk = nc.declare_dram_parameter("wkT", [D, D], F32, isOutput=False)
    wv = nc.declare_dram_parameter("wvT", [D, D], F32, isOutput=False)
    wc = nc.declare_dram_parameter("wcT", [D, D], F32, isOutput=False)
    biases = nc.declare_dram_parameter("biases", [4, D], F32, isOutput=False)
    summ = nc.declare_dram_parameter("summ", [16, D], F32, isOutput=False)
    ident = nc.declare_dram_parameter("ident", [128, 128], BF16, isOutput=False)
    identf = nc.declare_dram_parameter("identf", [128, 128], F32, isOutput=False)
    ones1 = nc.declare_dram_parameter("ones1", [1, 128], BF16, isOutput=False)
    x_out = nc.declare_dram_parameter("x_out", [TOK, D], F32, isOutput=True)

    es = contextlib.ExitStack()
    with es:
        prog = Prog(nc)
        ctx = Ctx(nc, prog, es)
        ctx.ident_s, ctx.ident_b = ctx.sbuf("ident_s", [128, 128], BF16)
        ctx.identf_s, ctx.identf_b = ctx.sbuf("identf_s", [128, 128], F32)
        ctx.ones1_s, ctx.ones1_b = ctx.sbuf("ones1_s", [1, 128], BF16)
        tr, trb = ctx.sbuf("trash", [128, D], BF16)
        ctx.trash_s = tr
        ctx.trashb = trb
        prog.dma("sync", ctx.ident_s[:, :], ident.ap(), writes=[ctx.ident_b])
        prog.dma("sync", ctx.identf_s[:, :], identf.ap(), writes=[ctx.identf_b])
        prog.dma("sync", ctx.ones1_s[:, :], ones1.ap(), writes=[ctx.ones1_b])
        build_l0attn(ctx, x_in.ap(), x_out.ap(), wq.ap(), wk.ap(), wv.ap(),
                     wc.ap(), biases.ap(), summ.ap(), {"half": half})
        prog.barrier()
        prog.realize(es)
    return nc


def build_program_fused():
    nc = bass.Bass("TRN2", target_bir_lowering=False, debug=False)
    DP = nc.declare_dram_parameter
    x0 = DP("x0", [TOK, D], F32, isOutput=False)
    summ = DP("summ", [16, D], F32, isOutput=False)
    l0_wq = DP("l0_wqT", [D, D], F32, isOutput=False)
    l0_wk = DP("l0_wkT", [D, D], F32, isOutput=False)
    l0_wv = DP("l0_wvT", [D, D], F32, isOutput=False)
    l0_wc = DP("l0_wcT", [D, D], F32, isOutput=False)
    l0_biases = DP("l0_biases", [4, D], F32, isOutput=False)
    f0_wg = DP("f0_wgT", [D, HID], I32, isOutput=False)
    f0_wu = DP("f0_wuT", [D, HID], I32, isOutput=False)
    f0_wd = DP("f0_wdT", [HID, D], I32, isOutput=False)
    a1_wq = DP("a1_wqT", [D, D], F32, isOutput=False)
    a1_wk = DP("a1_wkT", [D, D], F32, isOutput=False)
    a1_wv = DP("a1_wvT", [D, D], F32, isOutput=False)
    a1_wgq = DP("a1_wgqT", [D, 512], F32, isOutput=False)
    a1_wgk = DP("a1_wgkT", [D, 512], F32, isOutput=False)
    a1_wgv = DP("a1_wgvT", [D, 512], F32, isOutput=False)
    a1_row = DP("a1_r_out_w", [512, D], F32, isOutput=False)
    a1_glgrw = DP("a1_glgrw", [D, 4], F32, isOutput=False)
    a1_wo = DP("a1_w_outT", [D, D], F32, isOutput=False)
    a1_locmask = DP("a1_locmask", [128, 384], F32, isOutput=False)
    a1_halomask = DP("a1_halomask", [128, 384], F32, isOutput=False)
    a1_lmask = DP("a1_lmask", [64, TOK], F32, isOutput=False)
    a1_poolw = DP("a1_poolw", [128, 2], F32, isOutput=False)
    f1_baseW = DP("f1_baseWT", [D, D], F32, isOutput=False)
    f1_baseb = DP("f1_base_b", [1, D], F32, isOutput=False)
    f1_muT = DP("f1_muT", [D, 8], F32, isOutput=False)
    f1_sigT = DP("f1_sigT", [D, 8], F32, isOutput=False)
    f1_downWf = DP("f1_downWf", [8 * D, 64], F32, isOutput=False)
    f1_downbf = DP("f1_down_bf", [8, 64], F32, isOutput=False)
    f1_upWf = DP("f1_upWf", [8 * 64, D], F32, isOutput=False)
    f1_upbf = DP("f1_up_bf", [8, D], F32, isOutput=False)
    f1_convTf = DP("f1_convTf", [8 * D, D], F32, isOutput=False)
    f1_convbf = DP("f1_conv_bf", [8, D], F32, isOutput=False)
    iota128 = DP("iota128", [128, 1], I32, isOutput=False)
    iota8 = DP("iota8", [1, 8], F32, isOutput=False)
    ident = DP("ident", [128, 128], BF16, isOutput=False)
    identf = DP("identf", [128, 128], F32, isOutput=False)
    ones1 = DP("ones1", [1, 128], BF16, isOutput=False)
    onesf = DP("onesf", [128, 1], F32, isOutput=False)
    x_out = DP("x_out", [TOK, D], F32, isOutput=True)

    x1s = nc.dram_tensor("x1s", [TOK, D], F32)
    x2s = nc.dram_tensor("x2s", [TOK, D], F32)
    x3s = nc.dram_tensor("x3s", [TOK, D], F32)
    cc_tail_in = nc.dram_tensor("cc_tail_in", [256, D], F32)
    cc_tail_out = nc.dram_tensor("cc_tail_out", [512, D], F32)
    cc_pool_in = nc.dram_tensor("cc_pool_in", [128, NCHUNK * 32], BF16)
    cc_pool_out = nc.dram_tensor("cc_pool_out", [256, NCHUNK * 32], BF16)
    cc_m12_in = nc.dram_tensor("cc_m12_in", [128, NCHUNK * 2], F32)
    cc_m12_out = nc.dram_tensor("cc_m12_out", [128, NCHUNK * 2], F32)
    cc_ghalo_in = nc.dram_tensor("cc_ghalo_in", [128, 4], F32)
    cc_ghalo_out = nc.dram_tensor("cc_ghalo_out", [256, 4], F32)
    wgbf = nc.dram_tensor("wgbf", [D, HID], BF16)
    wubf = nc.dram_tensor("wubf", [D, HID], BF16)
    wdbf = nc.dram_tensor("wdbf", [HID, D], BF16)
    a1wqbf = nc.dram_tensor("a1wqbf", [D, D], BF16)
    a1wkbf = nc.dram_tensor("a1wkbf", [D, D], BF16)
    a1wvbf = nc.dram_tensor("a1wvbf", [D, D], BF16)
    a1gqbf = nc.dram_tensor("a1gqbf", [D, 512], BF16)
    a1gkbf = nc.dram_tensor("a1gkbf", [D, 512], BF16)
    a1gvbf = nc.dram_tensor("a1gvbf", [D, 512], BF16)
    a1rowbf = nc.dram_tensor("a1rowbf", [512, D], BF16)
    a1wobf = nc.dram_tensor("a1wobf", [D, D], BF16)
    f1bwbf = nc.dram_tensor("f1bwbf", [D, D], BF16)

    groups = [[0, 1], [2, 3], [4, 5], [6, 7]]

    es = contextlib.ExitStack()
    with es:
        prog = Prog(nc)
        ctx = Ctx(nc, prog, es)
        ctx.consts = {"poolw": a1_poolw.ap()}
        ctx.ident_s, ctx.ident_b = ctx.sbuf("ident_s", [128, 128], BF16)
        ctx.identf_s, ctx.identf_b = ctx.sbuf("identf_s", [128, 128], F32)
        ctx.ones1_s, ctx.ones1_b = ctx.sbuf("ones1_s", [1, 128], BF16)
        ctx.onesf_s, ctx.onesf_b = ctx.sbuf("onesf_s", [128, 1], F32)
        tr, trb = ctx.sbuf("trash", [128, D], BF16)
        ctx.trash_s = tr
        ctx.trashb = trb
        prog.dma("sync", ctx.ident_s[:, :], ident.ap(), writes=[ctx.ident_b])
        prog.dma("sync", ctx.identf_s[:, :], identf.ap(), writes=[ctx.identf_b])
        prog.dma("sync", ctx.ones1_s[:, :], ones1.ap(), writes=[ctx.ones1_b])
        prog.dma("sync", ctx.onesf_s[:, :], onesf.ap(), writes=[ctx.onesf_b])

        gridB_s, gridB_b = ctx.sbuf("gridB", [128, 192], F32)
        gridC_s, gridC_b = ctx.sbuf("gridC", [128, 192], F32)
        gridD_s, gridD_b = ctx.sbuf("gridD", [128, 192], F32)

        cc1 = {
            "groups": groups,
            "tail_in": cc_tail_in.ap(), "tail_in_b": Buf("cc_tail_in"),
            "tail_out": cc_tail_out.ap(), "tail_out_b": Buf("cc_tail_out"),
            "pool_in": cc_pool_in.ap(), "pool_in_b": Buf("cc_pool_in"),
            "pool_out": cc_pool_out.ap(), "pool_out_b": Buf("cc_pool_out"),
            "ghalo_in": cc_ghalo_in.ap(), "ghalo_in_b": Buf("cc_ghalo_in"),
            "ghalo_out": cc_ghalo_out.ap(), "ghalo_out_b": Buf("cc_ghalo_out"),
            "sem_tail": "CCT", "sem_pool": "CCP",
            "sem_ghalo": prog.new_sem("ccg"),
            "tail_tok": ("CCT", 1), "x_ready": None,
        }
        ctx.next_grid = (gridB_s, gridB_b)
        ctx.grid_scratch = (gridD_s, gridD_b)
        with ctx.scope():
            build_l0attn(ctx, x0.ap(), x1s.ap(), l0_wq.ap(), l0_wk.ap(),
                         l0_wv.ap(), l0_wc.ap(), l0_biases.ap(), summ.ap(), {})
            first_cast = [getattr(ctx, "l0_ready_tok", None)]
            for dst, src, rows in (
                    (wgbf, f0_wg, D), (wubf, f0_wu, D), (wdbf, f0_wd, HID)):
                b = Buf("wcast")
                for r0 in range(0, rows, 128):
                    prog.dma("gpsimd", dst.ap()[r0:r0 + 128, :],
                             src.ap()[r0:r0 + 128, :], writes=[b],
                             waits=first_cast[0] and [first_cast[0]],
                             tag="wcast")
                    first_cast[0] = None
        ctx.next_grid = (gridC_s, gridC_b)
        with ctx.scope():
            build_l0ffn(ctx, x1s.ap(), x2s.ap(), wgbf.ap(), wubf.ap(),
                        wdbf.ap(), grid_ext=(gridB_s, gridB_b))
            for dst, src, rows in (
                    (a1wqbf, a1_wq, D), (a1wkbf, a1_wk, D),
                    (a1wvbf, a1_wv, D), (a1gqbf, a1_wgq, D),
                    (a1gkbf, a1_wgk, D), (a1gvbf, a1_wgv, D),
                    (a1rowbf, a1_row, 512), (a1wobf, a1_wo, D),
                    (f1bwbf, f1_baseW, D)):
                b = Buf("wcast")
                for r0 in range(0, rows, 128):
                    prog.dma("gpsimd", dst.ap()[r0:r0 + 128, :],
                             src.ap()[r0:r0 + 128, :], writes=[b],
                             tag="wcast")
        ctx.next_grid = (gridD_s, gridD_b)
        with ctx.scope():
            build_l1attn(ctx, x2s.ap(), None, None, x3s.ap(), a1wqbf.ap(),
                         a1wkbf.ap(), a1wvbf.ap(), a1gqbf.ap(), a1gkbf.ap(),
                         a1gvbf.ap(), a1rowbf.ap(), a1_glgrw.ap(), a1wobf.ap(),
                         a1_locmask.ap(), a1_halomask.ap(), a1_lmask.ap(),
                         cc=cc1, grid_ext=(gridC_s, gridC_b))
        ctx.next_grid = None
        cc2 = {
            "groups": groups,
            "m12_in": cc_m12_in.ap(), "m12_in_b": Buf("cc_m12_in"),
            "m12_out": cc_m12_out.ap(), "m12_out_b": Buf("cc_m12_out"),
            "sem_m12": "CCM",
        }
        with ctx.scope():
            build_l1ffn(ctx, x3s.ap(), None, x_out.ap(), f1bwbf.ap(),
                        f1_baseb.ap(), f1_muT.ap(), f1_sigT.ap(),
                        f1_downWf.ap(), f1_downbf.ap(), f1_upWf.ap(),
                        f1_upbf.ap(), f1_convTf.ap(), f1_convbf.ap(),
                        iota128.ap(), iota8.ap(), cc=cc2,
                        grid_ext=(gridD_s, gridD_b))
        prog.barrier()
        prog.realize(es)
    return nc


_PROGS = {}


def _prog(name):
    if name not in _PROGS:
        _PROGS[name] = {"l0attn": build_program_l0attn,
                        "l0ffn": build_program_l0ffn,
                        "l1attn": build_program_l1attn,
                        "l1ffn": build_program_l1ffn,
                        "fused": build_program_fused}[name]()
    return _PROGS[name]


def _cores():
    return [(c // 2, c % 2) for c in range(8)]


def kernel(**inp):
    return _kernel_fused(**inp)


def _kernel_fused(**inp):
    consts = make_consts()
    C = np.ascontiguousarray
    x0 = inp["L0_attn_x"].astype(np.float32)
    biases = np.stack([inp["L0_attn_bq"], inp["L0_attn_bk"], inp["L0_attn_bv"],
                       inp["L0_attn_bc"]]).astype(np.float32)
    w0 = {n: C(inp[f"L0_attn_{n}"].T.astype(np.float32))
          for n in ("wq", "wk", "wv", "wc")}
    wgT = C(inp["L0_ffn_w_gate"].T)
    wuT = C(inp["L0_ffn_w_up"].T)
    wdT = C(inp["L0_ffn_w_down"].T)
    w1 = {n: C(inp[f"L1_attn_{n}"].astype(np.float32))
          for n in ("wq", "wk", "wv", "wgq", "wgk", "wgv", "w_out", "r_out_w")}
    glw = inp["L1_attn_gate_local_w"]
    grw = inp["L1_attn_gate_remote_w"]
    glgrw = np.zeros((D, 4), np.float32)
    for e in range(8):
        glgrw[e * 128:e * 128 + 64, 0] = glw[2 * e]
        glgrw[e * 128 + 64:(e + 1) * 128, 1] = glw[2 * e + 1]
        glgrw[e * 128:e * 128 + 64, 2] = grw[2 * e]
        glgrw[e * 128 + 64:(e + 1) * 128, 3] = grw[2 * e + 1]
    kkv = np.arange(128)[:, None]
    relv = np.arange(384)[None, :]
    locmask = ((relv - kkv >= 0) & (relv - kkv < 256)).astype(np.float32)
    poolw = np.zeros((128, 2), np.float32)
    poolw[:64, 0] = 1.0 / 64
    poolw[64:, 1] = 1.0 / 64
    baseWT = C(inp["L1_ffn_base_W"].T.astype(np.float32))
    muT = C(inp["L1_ffn_disc_mu"].T.astype(np.float32))
    sigT = C(inp["L1_ffn_disc_sigma"].T.astype(np.float32))
    downWf = C(inp["L1_ffn_down_W"].reshape(8 * D, 64).astype(np.float32))
    upWf = C(inp["L1_ffn_up_W"].reshape(8 * 64, D).astype(np.float32))
    convTf = C(inp["L1_ffn_conv_W"].transpose(0, 2, 1).reshape(8 * D, D)
               .astype(np.float32))
    iota128 = np.arange(128, dtype=np.int32).reshape(128, 1)
    iota8 = np.arange(8, dtype=np.float32).reshape(1, 8)
    onesf = np.ones((128, 1), np.float32)
    maps = []
    for b, half in _cores():
        o = half * TOK
        glat = np.arange(64)
        s_glob = np.arange(TOK) + o
        lmask = (glat[:, None] <= (s_glob[None, :] // 64)).astype(np.float32)
        maps.append({
            "x0": C(x0[b, o:o + TOK]),
            "summ": C(np.concatenate([x0[b, o::SEG][:NSEG],
                                      x0[b, (1 - half) * TOK::SEG][:NSEG]])),
            "l0_wqT": w0["wq"], "l0_wkT": w0["wk"], "l0_wvT": w0["wv"],
            "l0_wcT": w0["wc"], "l0_biases": biases,
            "f0_wgT": wgT, "f0_wuT": wuT, "f0_wdT": wdT,
            "a1_wqT": w1["wq"], "a1_wkT": w1["wk"], "a1_wvT": w1["wv"],
            "a1_wgqT": w1["wgq"], "a1_wgkT": w1["wgk"], "a1_wgvT": w1["wgv"],
            "a1_r_out_w": w1["r_out_w"], "a1_glgrw": glgrw,
            "a1_w_outT": w1["w_out"], "a1_locmask": locmask,
            "a1_halomask": locmask * float(half == 1), "a1_lmask": lmask,
            "a1_poolw": poolw,
            "f1_baseWT": baseWT,
            "f1_base_b": inp["L1_ffn_base_b"].reshape(1, D).astype(np.float32),
            "f1_muT": muT, "f1_sigT": sigT,
            "f1_downWf": downWf,
            "f1_down_bf": inp["L1_ffn_down_b"].astype(np.float32),
            "f1_upWf": upWf, "f1_up_bf": inp["L1_ffn_up_b"].astype(np.float32),
            "f1_convTf": convTf,
            "f1_conv_bf": inp["L1_ffn_conv_b"].astype(np.float32),
            "iota128": iota128, "iota8": iota8,
            "ident": consts["ident_bf16"], "identf": consts["ident_f32"],
            "ones1": consts["ones1_bf16"], "onesf": onesf,
        })
    r = run_bass_kernel_spmd(_prog("fused"), maps, core_ids=list(range(8)))
    x4 = np.zeros((4, 4096, D), np.float32)
    for c, (b, half) in enumerate(_cores()):
        x4[b, half * TOK:(half + 1) * TOK] = r.results[c]["x_out"]
    return x4


def _kernel_multilaunch(**inp):
    consts = make_consts()
    ident = consts["ident_bf16"]
    identf = consts["ident_f32"]
    ones1 = consts["ones1_bf16"]
    C = np.ascontiguousarray
    x0 = inp["L0_attn_x"].astype(np.float32)

    biases = np.stack([inp["L0_attn_bq"], inp["L0_attn_bk"], inp["L0_attn_bv"],
                       inp["L0_attn_bc"]]).astype(np.float32)
    w0 = {n: C(inp[f"L0_attn_{n}"].T.astype(np.float32))
          for n in ("wq", "wk", "wv", "wc")}
    maps = []
    for b, half in _cores():
        o = half * TOK
        maps.append({
            "x_in": C(x0[b, o:o + TOK]),
            "wqT": w0["wq"], "wkT": w0["wk"], "wvT": w0["wv"], "wcT": w0["wc"],
            "biases": biases,
            "summ": C(np.concatenate([x0[b, o::SEG][:NSEG],
                                      x0[b, (1 - half) * TOK::SEG][:NSEG]])),
            "ident": ident, "identf": identf, "ones1": ones1,
        })
    r = run_bass_kernel_spmd(_prog("l0attn"), maps, core_ids=list(range(8)))
    x1 = np.zeros((4, 4096, D), np.float32)
    for c, (b, half) in enumerate(_cores()):
        x1[b, half * TOK:(half + 1) * TOK] = r.results[c]["x_out"]

    wgT = C(inp["L0_ffn_w_gate"].T)
    wuT = C(inp["L0_ffn_w_up"].T)
    wdT = C(inp["L0_ffn_w_down"].T)
    maps = []
    for b, half in _cores():
        o = half * TOK
        maps.append({"x_in": C(x1[b, o:o + TOK]), "wgT": wgT, "wuT": wuT,
                     "wdT": wdT, "ident": ident})
    r = run_bass_kernel_spmd(_prog("l0ffn"), maps, core_ids=list(range(8)))
    x2 = np.zeros((4, 4096, D), np.float32)
    for c, (b, half) in enumerate(_cores()):
        x2[b, half * TOK:(half + 1) * TOK] = r.results[c]["x_out"]

    w1 = {n: C(inp[f"L1_attn_{n}"].astype(np.float32))
          for n in ("wq", "wk", "wv", "wgq", "wgk", "wgv", "w_out", "r_out_w")}
    glw = inp["L1_attn_gate_local_w"]
    grw = inp["L1_attn_gate_remote_w"]
    glgrw = np.zeros((D, 4), np.float32)
    for e in range(8):
        glgrw[e * 128:e * 128 + 64, 0] = glw[2 * e]
        glgrw[e * 128 + 64:(e + 1) * 128, 1] = glw[2 * e + 1]
        glgrw[e * 128:e * 128 + 64, 2] = grw[2 * e]
        glgrw[e * 128 + 64:(e + 1) * 128, 3] = grw[2 * e + 1]
    kkv = np.arange(128)[:, None]
    relv = np.arange(384)[None, :]
    locmask = ((relv - kkv >= 0) & (relv - kkv < 256)).astype(np.float32)
    poolw = np.zeros((128, 2), np.float32)
    poolw[:64, 0] = 1.0 / 64
    poolw[64:, 1] = 1.0 / 64
    maps = []
    for b, half in _cores():
        o = half * TOK
        oo = (1 - half) * TOK
        xhalo = (x2[b, o - 256:o] if half == 1
                 else np.zeros((256, D), np.float32))
        glat = np.array([half * 32 + j if j < 32 else (1 - half) * 32 + (j - 32)
                         for j in range(64)])
        s_glob = np.arange(TOK) + o
        lmask = (glat[:, None] <= (s_glob[None, :] // 64)).astype(np.float32)
        maps.append({
            "x_in": C(x2[b, o:o + TOK]), "xhalo": C(xhalo),
            "xother": C(x2[b, oo:oo + TOK]),
            "wqT": w1["wq"], "wkT": w1["wk"], "wvT": w1["wv"],
            "wgqT": w1["wgq"], "wgkT": w1["wgk"], "wgvT": w1["wgv"],
            "r_out_w": w1["r_out_w"], "glgrw": glgrw, "w_outT": w1["w_out"],
            "locmask": locmask, "halomask": locmask * float(half == 1),
            "lmask": lmask, "ident": ident, "poolw": poolw,
        })
    r = run_bass_kernel_spmd(_prog("l1attn"), maps, core_ids=list(range(8)))
    x3 = np.zeros((4, 4096, D), np.float32)
    for c, (b, half) in enumerate(_cores()):
        x3[b, half * TOK:(half + 1) * TOK] = r.results[c]["x_out"]

    baseWT = C(inp["L1_ffn_base_W"].T.astype(np.float32))
    muT = C(inp["L1_ffn_disc_mu"].T.astype(np.float32))
    sigT = C(inp["L1_ffn_disc_sigma"].T.astype(np.float32))
    downWf = C(inp["L1_ffn_down_W"].reshape(8 * D, 64).astype(np.float32))
    upWf = C(inp["L1_ffn_up_W"].reshape(8 * 64, D).astype(np.float32))
    convTf = C(inp["L1_ffn_conv_W"].transpose(0, 2, 1).reshape(8 * D, D)
               .astype(np.float32))
    iota128 = np.arange(128, dtype=np.int32).reshape(128, 1)
    iota8 = np.arange(8, dtype=np.float32).reshape(1, 8)
    onesf = np.ones((128, 1), np.float32)
    maps = []
    for b, half in _cores():
        o = half * TOK
        oo = (1 - half) * TOK
        maps.append({
            "x_in": C(x3[b, o:o + TOK]), "xother": C(x3[b, oo:oo + TOK]),
            "baseWT": baseWT,
            "base_b": inp["L1_ffn_base_b"].reshape(1, D).astype(np.float32),
            "muT": muT, "sigT": sigT,
            "downWf": downWf, "down_bf": inp["L1_ffn_down_b"].astype(np.float32),
            "upWf": upWf, "up_bf": inp["L1_ffn_up_b"].astype(np.float32),
            "convTf": convTf, "conv_bf": inp["L1_ffn_conv_b"].astype(np.float32),
            "iota128": iota128, "iota8": iota8,
            "ident": ident, "ones1": ones1, "onesf": onesf,
        })
    r = run_bass_kernel_spmd(_prog("l1ffn"), maps, core_ids=list(range(8)))
    x4 = np.zeros((4, 4096, D), np.float32)
    for c, (b, half) in enumerate(_cores()):
        x4[b, half * TOK:(half + 1) * TOK] = r.results[c]["x_out"]
    return x4
